# version 06
# baseline (speedup 1.0000x reference)
import contextlib
import ml_dtypes
import numpy as np

import concourse.bass as bass
import concourse.mybir as mybir
from concourse.bass_utils import run_bass_kernel_spmd

F32 = mybir.dt.float32
BF16 = mybir.dt.bfloat16
I32 = mybir.dt.int32
AF = mybir.ActivationFunctionType
OP = mybir.AluOpType
AXX = mybir.AxisListType.X

DM = 2048
DI = 4096
NH = 64
HD = 64
NST = 64
CH = 64
NCH = 16
DIP = 8384
DIPP = 8448
INTER = 5504
TOK = 1024
PTOK = 1152
CTX0 = 125
REAL0 = 128
NKD = 16
NKI = 32
NMIP = 66
NMI = 43
LN_EPS = 1e-5
RMS_EPS = 1e-6
N_TILES = 16 * NMI * 2 + NMI * 16
TPC = N_TILES // 8
ACOLS = 98304

ORD = {"vector": "oV", "scalar": "oS", "gpsimd": "oG"}


class Builder:
    def __init__(self, debug=False):
        self.debug = debug
        self.nc = bass.Bass("TRN2", target_bir_lowering=False)
        self.streams = {k: [] for k in ("tensor", "vector", "scalar", "gpsimd", "sync")}
        self.sems = {}
        self.cnt = {}
        self._stack = contextlib.ExitStack()
        self.d = {}
        self.s = {}

    def sem(self, name):
        if name not in self.sems:
            self.sems[name] = self._stack.enter_context(self.nc.semaphore(name))
            self.cnt[name] = 0
        return self.sems[name]

    def val(self, name):
        return self.cnt.get(name, 0)

    def tok(self, name):
        return (name, self.val(name))

    def din(self, name, shape, dtype):
        self.d[name] = self.nc.dram_tensor(name, shape, dtype, kind="ExternalInput")
        return self.d[name]

    def dout(self, name, shape, dtype):
        self.d[name] = self.nc.dram_tensor(name, shape, dtype, kind="ExternalOutput")
        return self.d[name]

    def dtmp(self, name, shape, dtype, **kw):
        self.d[name] = self.nc.dram_tensor(name, shape, dtype, **kw)
        return self.d[name]

    def ddbg(self, name, shape, dtype):
        return self.dout(name, shape, dtype) if self.debug else self.dtmp(name, shape, dtype)

    def sb(self, name, shape, dtype):
        self.s[name] = self.nc.alloc_sbuf_tensor(name, shape, dtype)
        return self.s[name]

    def ar(self, off, cols, dtype=BF16, p0=0, p1=128):
        v = self.BIG[p0:p1, off:off + cols]
        if dtype == BF16:
            return v
        return v.bitcast(dtype)

    def bank(self, i, j=1):
        return self.PS[:, 512 * i:512 * (i + j)]

    def bankb(self, i, j=1):
        return self.PS[:, 512 * i:512 * (i + j)].bitcast(BF16)

    def op(self, engine, fn, waits=(), inc=None, dma=False):
        waits = [(s, v) for s, v in waits if v > 0]
        oname = ORD.get(engine)
        upd = None
        if oname is not None:
            self.sem(oname)
            prev = self.val(oname)
            if prev > 0:
                waits.append((oname, prev))
            if not dma:
                assert inc is None
                self.cnt[oname] = prev + 1
                upd = (oname, 1)
        if upd is None and inc is not None:
            s, n = inc
            self.sem(s)
            self.cnt[s] += n
            upd = (s, n)
        wl = [(self.sem(s), v) for s, v in waits]
        il = None if upd is None else (self.sems[upd[0]], upd[1])

        def run(eng, fn=fn, wl=wl, il=il):
            for s, v in wl:
                eng.wait_ge(s, v)
            ins = fn(eng)
            if il is not None:
                ins.then_inc(il[0], il[1])
        self.streams[engine].append(run)
        if upd is not None:
            return (upd[0], self.val(upd[0]))
        return None

    def raw(self, engine, fn):
        self.streams[engine].append(fn)

    def finish(self):
        nc = self.nc
        with nc.Block() as block:
            for name in ("sync", "tensor", "vector", "scalar", "gpsimd"):
                def section(eng, name=name):
                    for f in self.streams[name]:
                        f(eng)
                getattr(block, name)(section)
        self._stack.close()
        return nc


A_WSTAGE = 0
A_MPIECE = 32768
A_MSTAGE = 36864
A_GFLAT = 40960
A_USTAGE = 49152
A_SQDUMP = 57344
A_XN = 59392
A_HFM = 67584
A_XCONV = 0
A_W3 = 33024
A_WBF = 36608
A_ZST = 48896
A_BC = 87040
A_DT = 91136
A_CS = 93184
A_CSCAL = 95232

S2 = 128
NC2 = 8
S_STATES = 0
S_YCH = 18432
S_EXPCS = 22528
S_CST = 24576
S_EKT = 25600
S_EKTB = 26624
S_KCT = 27136
S_FLAT = 27648
S_VCT = 32896
S_VCB = 65664
S_ATTE = 65664
S_ARG = 73856
S_KSC = 77952
S_SVCT = 82048
T_DEX = 91136
T_TOT = 91200
T_ATTM = 91264
T_TOTR = 91392
T_DEXP = 92544
S_SFIN = 65664
S_SINITB = 69760
S_SCO = 71808
S_QEXP = 73856
S_YST2 = 77952
S_YSC = 32896

def phase_const(B):
    ident_b = B.sb("ident_b", [128, 128], BF16)
    ident_f = B.sb("ident_f", [128, 128], F32)
    ones1 = B.sb("ones1", [1, 128], BF16)
    ones1f = B.sb("ones1f", [1, 128], F32)
    cmask = B.sb("cmask", [64, 64], F32)
    cvals = B.sb("cvals", [128, 4], F32)

    for t in (ident_b, ident_f):
        B.op("gpsimd", lambda e, t=t: e.memset(t[:, :], 0.0))
        B.op("gpsimd", lambda e, t=t: e.affine_select(
            out=t[:, :], in_=t[:, :], compare_op=OP.not_equal,
            fill=1.0, base=0, pattern=[[-1, 128]], channel_multiplier=1))
    onescol = B.sb("onescol", [128, 1], BF16)
    B.op("gpsimd", lambda e: e.memset(onescol[:, :], 1.0))
    B.op("gpsimd", lambda e: e.memset(ones1[:, :], 1.0))
    B.op("gpsimd", lambda e: e.memset(ones1f[:, :], 1.0))
    B.op("gpsimd", lambda e: e.memset(cvals[:, 0:1], float(LN_EPS)))
    B.op("gpsimd", lambda e: e.memset(cvals[:, 1:2], 1.0))
    B.op("gpsimd", lambda e: e.memset(cvals[:, 2:3], float(RMS_EPS)))
    B.op("gpsimd", lambda e: e.memset(cmask[:, :], 0.0))
    B.c_const = B.op("gpsimd", lambda e: e.affine_select(
        out=cmask[:, :], in_=cmask[:, :], compare_op=OP.is_ge,
        fill=-1e30, base=0, pattern=[[1, 64]], channel_multiplier=-1))
    csc = B.ar(A_CSCAL, 2048, F32, 0, 64)
    B.op("gpsimd", lambda e, csc=csc: e.memset(csc[:, :], 1.0))
    B.c_cscal = B.op("gpsimd", lambda e, csc=csc: e.affine_select(
        out=csc[:, 0:1024].rearrange("p (c s) -> p c s", s=128),
        in_=csc[:, 0:1024].rearrange("p (c s) -> p c s", s=128),
        compare_op=OP.not_equal, fill=0.0, base=0,
        pattern=[[0, 8], [1, 128]], channel_multiplier=0))
    B.din("hvec", [64, 4], F32)
    hcols = B.sb("hcols", [64, 4], F32)
    t_ld = B.op("sync", lambda e: e.dma_start(out=hcols[:, :], in_=B.d["hvec"][:, :]),
                inc=("hvld", 16), dma=True)
    B.t_expA = B.op("scalar", lambda e: e.activation(hcols[:, 0:1], hcols[:, 0:1], AF.Exp),
                    waits=[t_ld])


def phase_mod(B):
    B.din("ctb", [128, 8], F32)
    B.din("wmod_sh", [256, 12288], F32)
    B.din("bselB", [128, 4], F32)
    B.dtmp("mod_part", [4, 12288], F32)
    B.dtmp("mod_full", [4, 12288], F32)

    silc = B.sb("silc", [128, 8], F32)
    bsel_sb = B.sb("bsel_sb", [128, 4], F32)
    modcols = B.sb("modcols", [128, 384], F32)
    modsel = B.sb("modsel", [128, 96], F32)
    colbuf = B.sb("colbuf", [128, 64], F32)
    tcol = B.sb("tcol", [128, 32], F32)
    grow = B.sb("grow", [32, 128], F32)
    gaB = B.sb("gaB", [128, 4096], BF16)

    wstage = B.ar(A_WSTAGE, 16384, BF16)
    mpiece = B.ar(A_MPIECE, 4096, F32)
    mstage = B.ar(A_MSTAGE, 4096, F32, 0, 4)
    gflat = B.ar(A_GFLAT, 8192, F32, 0, 1)
    mpT = B.bank(2)

    def mp4s(ns):
        return B.bank(ns % 2)[0:4, :]

    wv = B.d["wmod_sh"].ap().rearrange("(a p) n -> a p n", p=128)

    B.op("sync", lambda e: e.dma_start(out=silc[:, :], in_=B.d["ctb"][:, :]),
         inc=("p0ld", 16), dma=True)
    B.op("sync", lambda e: e.dma_start(out=bsel_sb[:, :], in_=B.d["bselB"][:, :]),
         inc=("p0ld", 16), dma=True)
    t_silu = B.op("scalar", lambda e: e.activation(silc[:, :], silc[:, :], AF.Silu),
                  waits=[("p0ld", 32)])
    if getattr(B, "modstop", 99) <= 1:
        return

    mm_toks = {}
    ev_toks = {}
    for cc in range(6):
        for ns in range(4):
            li = 4 * cc + ns
            for kt in range(2):
                g = 2 * cc + kt
                slot = g % 4
                waits = [("p0w%d" % slot, (g // 4 + 1) * 16)]
                if cc == 0 and ns == 0 and kt == 0:
                    waits.append(t_silu)
                if kt == 0 and li >= 2:
                    waits.append(ev_toks[li - 2])
                mm_toks[(g, ns)] = B.op(
                    "tensor",
                    lambda e, ns=ns, kt=kt, slot=slot: e.matmul(
                        mp4s(ns), silc[:, kt * 4:(kt + 1) * 4],
                        wstage[:, slot * 4096 + ns * 1024: slot * 4096 + (ns + 1) * 1024].bitcast(F32),
                        start=(kt == 0), stop=(kt == 1)),
                    waits=waits, inc=("p0mm", 1))
            evw = [mm_toks[(2 * cc + 1, ns)]]
            if cc >= 1:
                evw.append(("p0st", 16 * cc))
            ev_toks[li] = B.op(
                "scalar",
                lambda e, ns=ns: e.copy(mstage.bitcast(F32)[:, ns * 512:(ns + 1) * 512],
                                        mp4s(ns)),
                waits=evw)
        B.op("gpsimd",
             lambda e, cc=cc: e.dma_start(out=B.d["mod_part"][:, cc * 2048:(cc + 1) * 2048],
                                          in_=mstage.bitcast(F32)[:, :]),
             waits=[ev_toks[4 * cc + 3]], inc=("p0st", 16), dma=True)

    for cc in range(6):
        for kt in range(2):
            g = 2 * cc + kt
            slot = g % 4
            waits = []
            if g >= 4:
                waits.append(mm_toks[(g - 4, 3)])
            B.op("sync",
                 lambda e, kt=kt, cc=cc, slot=slot: e.dma_start(
                     out=wstage[:, slot * 4096:(slot + 1) * 4096].bitcast(F32),
                     in_=wv[kt, :, cc * 2048:(cc + 1) * 2048]),
                 waits=waits, inc=("p0w%d" % slot, 16), dma=True)

    if getattr(B, "modstop", 99) <= 2:
        return
    t_ar = B.op("gpsimd",
                lambda e: e.collective_compute(
                    "AllReduce", OP.add, replica_groups=[list(range(8))],
                    ins=[B.d["mod_part"].ap().opt()], outs=[B.d["mod_full"].ap().opt()]),
                waits=[("p0st", 96)], inc=("p0ar", 1), dma=True)
    if getattr(B, "modstop", 99) <= 3:
        return

    mpf = mpiece
    B.op("gpsimd", lambda e: e.memset(mpf[:, :], 0.0))
    tr_toks = {}
    cp_toks = {}
    ntr = 0
    for pc in range(6):
        waits = [t_ar]
        if pc:
            waits.append(tr_toks[ntr - 1])
        B.op("gpsimd",
             lambda e, pc=pc: e.dma_start(out=mpf[0:4, :],
                                          in_=B.d["mod_full"][:, pc * 2048:(pc + 1) * 2048]),
             waits=waits, inc=("p0mp", 16), dma=True)
        for j in range(16):
            twaits = [("p0mp", (pc + 1) * 16), B.c_const]
            if ntr >= 4:
                twaits.append(cp_toks[ntr - 4])
            tr_toks[ntr] = B.op(
                "tensor",
                lambda e, j=j, ntr=ntr: e.matmul(
                    B.bank(ntr % 4)[:, 0:128],
                    mpf[:, j * 128:(j + 1) * 128],
                    B.s["ident_f"][:, :],
                    start=True, stop=True),
                waits=twaits, inc=("p0tr", 1))
            cp_toks[ntr] = B.op(
                "scalar",
                lambda e, pc=pc, j=j, ntr=ntr: e.copy(
                    modcols[:, (pc * 16 + j) * 4:(pc * 16 + j) * 4 + 4],
                    B.bank(ntr % 4)[:, 0:4]),
                waits=[tr_toks[ntr]])
            ntr += 1

    if getattr(B, "modstop", 99) <= 4:
        return
    for b in range(4):
        src = modcols[:, :].rearrange("p (n c) -> p n c", c=4)[:, :, b]
        if b == 0:
            B.op("vector",
                 lambda e, src=src: e.tensor_scalar(modsel[:, :], src, bsel_sb[:, 0:1],
                                                    None, OP.mult),
                 waits=[cp_toks[95], ("p0ld", 32)])
        else:
            B.op("vector",
                 lambda e, src=src, b=b: e.scalar_tensor_tensor(
                     modsel[:, :], src, bsel_sb[:, b:b + 1], modsel[:, :], OP.mult, OP.add))

    B.op("vector", lambda e: e.tensor_scalar(colbuf[:, 0:16], modsel[:, 16:32], 1.0, None, OP.add))
    B.op("vector", lambda e: e.tensor_scalar(colbuf[:, 16:32], modsel[:, 64:80], 1.0, None, OP.add))
    for k, src0 in ((0, 32), (1, 80)):
        t = tcol[:, k * 16:(k + 1) * 16]
        dst0 = 32 + k * 16
        B.op("vector", lambda e, t=t, src0=src0: e.tensor_scalar(
            t, modsel[:, src0:src0 + 16], -1.0, None, OP.mult))
        B.op("vector", lambda e, t=t, src0=src0: e.tensor_tensor(
            t, modsel[:, src0:src0 + 16], t, OP.max))
        B.op("vector", lambda e, t=t: e.tensor_scalar(t, t, 0.25, None, OP.max))
        B.op("vector", lambda e, dst0=dst0, src0=src0: e.tensor_scalar(
            colbuf[:, dst0:dst0 + 16], modsel[:, src0:src0 + 16], 0.0, None, OP.is_ge))
    B.op("vector", lambda e: e.tensor_scalar(colbuf[:, 32:64], colbuf[:, 32:64],
                                             2.0, -1.0, OP.mult, OP.add))
    B.t_col = B.op("vector", lambda e: e.tensor_mul(colbuf[:, 32:64], colbuf[:, 32:64],
                                                    tcol[:, 0:32]))

    if getattr(B, "modstop", 99) <= 5:
        return
    t_gt = B.op("tensor", lambda e: e.matmul(mpT[0:32, 0:128], colbuf[:, 32:64],
                                             B.s["ident_f"][:, :],
                                             start=True, stop=True),
                waits=[B.t_col, cp_toks[95]], inc=("p0gt", 1))
    t_gr = B.op("scalar", lambda e: e.copy(grow[:, :], mpT[0:32, 0:128]),
                waits=[t_gt])
    B.op("gpsimd", lambda e: e.dma_start(out=gflat[:, 0:2048], in_=grow[0:16, 0:128]),
         waits=[t_gr], inc=("p0gf", 16), dma=True)
    B.op("gpsimd", lambda e: e.dma_start(out=gflat[:, 2048:4096], in_=grow[16:32, 0:128]),
         waits=[t_gr], inc=("p0gf", 16), dma=True)
    bc_toks = {}
    n = 0
    for h in range(2):
        for ns in range(4):
            waits = [("p0gf", 32)]
            if n >= 1:
                waits.append(bc_toks[n - 1])
            t_b = B.op("tensor",
                       lambda e, h=h, ns=ns: e.matmul(
                           mpT[:, 0:512], B.s["ones1f"][:, :],
                           gflat[:, h * 2048 + ns * 512: h * 2048 + (ns + 1) * 512],
                           start=True, stop=True),
                       waits=waits, inc=("p0bc", 1))
            bc_toks[n] = B.op("vector",
                              lambda e, h=h, ns=ns: e.tensor_copy(
                                  gaB[:, h * 2048 + ns * 512: h * 2048 + (ns + 1) * 512],
                                  mpT[:, 0:512]),
                              waits=[t_b])
            n += 1
    B.t_gaB = bc_toks[7]


def phase_ln1(B):
    ustage = B.ar(A_USTAGE, 8192, F32)
    sqdump = B.ar(A_SQDUMP, 2048, BF16)
    xn = B.ar(A_XN, 8192, BF16)
    h_fm = B.ar(A_HFM, NKD * PTOK, BF16)
    B.h_fm = h_fm
    stat = B.sb("stat", [128, 16], F32)
    psTs = (B.bankb(3, 2), B.bankb(5, 2))

    uv = B.d["u_tok"].ap().rearrange("(n p) d -> n p d", p=128)
    modsel = B.s["modsel"]
    colbuf = B.s["colbuf"]

    NT = PTOK // 128
    xn_tok = {}
    ev_tok = {}
    grp_tr_tok = {}
    tr_cnt = 0
    tgroups = [list(range(g, min(g + 4, NT))) for g in range(0, NT, 4)]
    for gi, tg in enumerate(tgroups):
        for i in tg:
            slot = i % 2
            waits = []
            if i >= 2:
                waits.append(xn_tok[i - 2])
            B.op("sync", lambda e, i=i, slot=slot: e.dma_start(
                out=ustage[:, slot * DM:(slot + 1) * DM], in_=uv[i, :, :]),
                waits=waits, inc=("uld%d" % slot, 16), dma=True)
            ldw = [("uld%d" % slot, (i // 2 + 1) * 16)]
            us = ustage[:, slot * DM:(slot + 1) * DM]
            s0 = 8 * (i % 2)
            t_sum = B.op("vector", lambda e, us=us, s0=s0: e.reduce_sum(
                stat[:, s0:s0 + 1], us, AXX), waits=ldw)
            t_qq = B.op("scalar", lambda e, us=us, s0=s0: e.activation(
                sqdump[:, 0:2048], us, AF.Square, scale=float(1.0 / np.sqrt(DM)),
                accum_out=stat[:, s0 + 1:s0 + 2]), waits=ldw)
            B.op("vector", lambda e, s0=s0: e.tensor_mul(
                stat[:, s0 + 2:s0 + 3], stat[:, s0:s0 + 1], stat[:, s0:s0 + 1]))
            t_var = B.op("vector", lambda e, s0=s0: e.scalar_tensor_tensor(
                stat[:, s0 + 3:s0 + 4], stat[:, s0 + 2:s0 + 3], float(-1.0 / (DM * DM)),
                stat[:, s0 + 1:s0 + 2], OP.mult, OP.add), waits=[t_qq])
            t_m = B.op("vector", lambda e, s0=s0: e.tensor_scalar(
                stat[:, s0 + 4:s0 + 5], stat[:, s0:s0 + 1], float(1.0 / DM), None, OP.mult))
            B.op("scalar", lambda e, s0=s0: e.activation(
                stat[:, s0 + 3:s0 + 4], stat[:, s0 + 3:s0 + 4], AF.Ln,
                bias=B.s["cvals"][:, 0:1]), waits=[t_var])
            t_rstd = B.op("scalar", lambda e, s0=s0: e.activation(
                stat[:, s0 + 5:s0 + 6], stat[:, s0 + 3:s0 + 4], AF.Exp, scale=-0.5))
            xw = [t_rstd, t_m]
            if gi >= 1:
                xw.append(grp_tr_tok[gi - 1])
            xn_tok[i] = B.op("vector", lambda e, us=us, s0=s0, i=i: e.tensor_scalar(
                xn[:, (i % 4) * DM:(i % 4 + 1) * DM], us, stat[:, s0 + 4:s0 + 5],
                stat[:, s0 + 5:s0 + 6], OP.subtract, OP.mult), waits=xw)
        for dkg in range(4):
            pst = psTs[dkg % 2]
            gw = [xn_tok[i] for i in tg]
            key = (gi, dkg - 2) if dkg >= 2 else (gi - 1, 2 + dkg)
            if key in ev_tok:
                gw.append(ev_tok[key])
            for dj in range(4):
                dk = dkg * 4 + dj
                for ti, i in enumerate(tg):
                    B.op("tensor", lambda e, dk=dk, dj=dj, ti=ti, i=i, pst=pst:
                        e.transpose(pst[:, dj * 512 + ti * 128: dj * 512 + (ti + 1) * 128],
                                    xn[:, (i % 4) * DM + dk * 128:(i % 4) * DM + (dk + 1) * 128],
                                    B.s["ident_b"][:, :]),
                        waits=gw if (dj == 0 and ti == 0) else [], inc=("tr1", 1))
                    tr_cnt += 1
            lastw = [("tr1", tr_cnt)]
            nt = len(tg)
            for dj in range(4):
                dk = dkg * 4 + dj
                tok = B.op("scalar", lambda e, dk=dk, dj=dj, gi=gi, nt=nt, pst=pst:
                    e.activation(h_fm[:, dk * PTOK + gi * 512: dk * PTOK + gi * 512 + nt * 128],
                                 pst[:, dj * 512: dj * 512 + nt * 128],
                                 AF.Identity, bias=modsel[:, dk:dk + 1],
                                 scale=colbuf[:, dk:dk + 1]),
                    waits=lastw if dj == 0 else [])
            ev_tok[(gi, dkg)] = tok
        grp_tr_tok[gi] = ("tr1", tr_cnt)
    B.t_hfm = B.tok("oS")


def phase_inproj(B):
    B.din("wip", [22, 2048, 384], F32)
    B.din("initmask", [128, 1], F32)
    zs = B.ddbg("zs", [NKI, 128, 1024], BF16)

    imask = B.sb("imask", [128, 1], F32)
    B.op("sync", lambda e: e.dma_start(out=imask[:, :], in_=B.d["initmask"][:, :]),
         inc=("imld", 16), dma=True)

    B.din("gsrc", [TPC, 128, 128], BF16)
    B.din("lutb", [128, 16], F32)
    B.dtmp("wex_sh", [TPC, 128, 128], BF16)
    B.dtmp("wex_full", [N_TILES, 128, 128], BF16, addr_space="Shared")
    lutb = B.sb("lutb_s", [128, 16], F32)
    B.op("sync", lambda e: e.dma_start(out=lutb[:, :], in_=B.d["lutb"][:, :]),
         inc=("lutld", 16), dma=True)
    xgbf = lambda sl: B.ar(50944 + sl * 1024, 1024)
    xacc = lambda sl: B.ar(55040 + sl * 1024, 1024)
    xtj = B.ar(59136, 1024)
    gv = B.d["gsrc"].ap()
    wsh = B.d["wex_sh"].ap()
    ex_ev = {}
    B.ex_next = 0
    B.n_exu = (TPC + 5) // 6
    def emit_expansion_unit():
        c = B.ex_next
        t0 = c * 6
        nt = min(6, TPC - t0)
        if nt <= 0:
            return
        B.ex_next = c + 1
        sl = c % 2
        dw = [B.t_hfm] if c < 2 else [ex_ev[c - 2]]
        B.op("gpsimd", lambda e, t0=t0, nt=nt, sl=sl: e.dma_start(
            out=xgbf(sl)[:, 0:nt * 128].rearrange("p (a b) -> p a b", a=nt),
            in_=gv[t0:t0 + nt, :, :].transpose([1, 0, 2])),
            waits=dw, inc=("exd%d" % sl, 16), dma=True)
        mw = [("exd%d" % sl, 16 * (c // 2 + 1)), ("lutld", 16)]
        if c >= 2:
            mw.append(("exs%d" % sl, 16 * (c // 2)))
        B.op("vector", lambda e, nt=nt, sl=sl: e.tensor_scalar(
            xacc(sl)[:, 0:nt * 128], xgbf(sl)[:, 0:nt * 128], 0.0, lutb[:, 0:1],
            OP.is_equal, OP.mult), waits=mw)
        for j in range(1, 16):
            B.op("vector", lambda e, nt=nt, sl=sl, j=j: e.tensor_scalar(
                xtj[:, 0:nt * 128], xgbf(sl)[:, 0:nt * 128], float(j),
                lutb[:, j:j + 1], OP.is_equal, OP.mult))
            B.op("vector", lambda e, nt=nt, sl=sl: e.tensor_add(
                xacc(sl)[:, 0:nt * 128], xacc(sl)[:, 0:nt * 128],
                xtj[:, 0:nt * 128]))
        ex_ev[c] = B.tok("oV")
        B.op("gpsimd", lambda e, t0=t0, nt=nt, sl=sl: e.dma_start(
            out=wsh[t0:t0 + nt, :, :].transpose([1, 0, 2]),
            in_=xacc(sl)[:, 0:nt * 128].rearrange("p (a b) -> p a b", a=nt)),
            waits=[ex_ev[c]], inc=("exs%d" % sl, 16), dma=True)
    B.emit_expansion_unit = emit_expansion_unit

    xconv = B.ar(A_XCONV, 32864, BF16)
    B.xconv = xconv
    wbf = B.ar(A_WBF, 12288, BF16)
    zst = B.ar(A_ZST, 2048, BF16)
    BC = B.ar(A_BC, 4096, F32)
    dt_b = B.ar(A_DT, 2048, F32, 0, 64)
    cs_b = B.ar(A_CS, 2048, F32, 0, 64)
    B.BCv = BC
    B.dtv = dt_b
    B.csv = cs_b
    h_fm = B.h_fm

    wv = B.d["wip"].ap()
    t_p0done = ("p0mm", B.val("p0mm"))

    cast_tok = {}
    ev_tok = {}
    ctxev_tok = {}
    mm_after_group = {}
    mm_cnt = B.val("i1mm")
    for g in range(22):
        slot = g % 2
        for kt in range(16):
            j = 16 * g + kt
            sub = j % 4
            waits = []
            if j >= 4:
                waits.append(cast_tok[j - 4])
            B.op("sync", lambda e, g=g, kt=kt, sub=sub: e.dma_start(
                out=B.ar(A_W3 + sub * 768, 768, F32),
                in_=wv[g, kt * 128:(kt + 1) * 128, :]),
                waits=waits, inc=("w3_%d" % sub, 16), dma=True)
            cw = [("w3_%d" % sub, (j // 4 + 1) * 16)]
            if kt == 0 and g >= 2:
                cw.append(("i1mm", mm_after_group[g - 2]))
            cast_tok[j] = B.op(
                "vector",
                lambda e, kt=kt, slot=slot, sub=sub: e.tensor_copy(
                    wbf[:, slot * 6144 + kt * 384: slot * 6144 + (kt + 1) * 384],
                    B.ar(A_W3 + sub * 768, 768, F32)),
                waits=cw)
        B.emit_expansion_unit()
        for mi in range(3):
            mc = 3 * g + mi
            for kt in range(16):
                for n in range(2):
                    bk = B.bank(mi * 2 + n)
                    waits = []
                    if kt == 0 and n == 0:
                        waits.append(cast_tok[16 * g + 15])
                        if g == 0 and mi == 0:
                            waits.append(B.t_hfm)
                            waits.append(t_p0done)
                            waits.append(B.t_gaB)
                            waits.append(("p0bc", 8))
                        if mc >= 3 and (mc - 3, 1) in ev_tok:
                            waits.append(ev_tok[(mc - 3, 0)])
                            waits.append(ev_tok[(mc - 3, 1)])
                            if (mc - 3) in ctxev_tok:
                                waits.append(ctxev_tok[mc - 3])
                    B.op("tensor", lambda e, mi=mi, kt=kt, n=n, slot=slot, bk=bk: e.matmul(
                        bk,
                        wbf[:, slot * 6144 + kt * 384 + mi * 128: slot * 6144 + kt * 384 + (mi + 1) * 128],
                        h_fm[:, kt * PTOK + REAL0 + n * 512: kt * PTOK + REAL0 + (n + 1) * 512],
                        start=(kt == 0), stop=(kt == 15)),
                        waits=waits, inc=("i1mm", 1))
                    mm_cnt += 1
                if 32 <= mc < 64:
                    cxw = []
                    if kt == 0 and (mc - 2) in ctxev_tok:
                        cxw.append(ctxev_tok[mc - 2])
                    B.op("tensor", lambda e, mi=mi, kt=kt, slot=slot, mc=mc: e.matmul(
                        B.bank(6 + mc % 2)[:, mi * 4: mi * 4 + 3],
                        wbf[:, slot * 6144 + kt * 384 + mi * 128: slot * 6144 + kt * 384 + (mi + 1) * 128],
                        h_fm[:, kt * PTOK + CTX0: kt * PTOK + CTX0 + 3],
                        start=(kt == 0), stop=(kt == 15)),
                        waits=cxw, inc=("i1mm", 1))
                    mm_cnt += 1
            mmtok = ("i1mm", mm_cnt)
            mm_after_group[g] = mm_cnt
            if mc < 32:
                for n in range(2):
                    zw = [mmtok]
                    if mc >= 2 and n == 0:
                        zw.append(("zdma%d" % (mc % 2), 16 * (mc // 2)))
                    ev_tok[(mc, n)] = B.op("scalar", lambda e, mc=mc, n=n, mi=mi: e.activation(
                        zst[:, (mc % 2) * 1024 + n * 512:(mc % 2) * 1024 + (n + 1) * 512],
                        B.bank(mi * 2 + n), AF.Silu),
                        waits=zw)
                B.op("gpsimd", lambda e, mc=mc: e.dma_start(
                    out=zs.ap()[mc, :, :], in_=zst[:, (mc % 2) * 1024:(mc % 2 + 1) * 1024]),
                    waits=[ev_tok[(mc, 1)]], inc=("zdma%d" % (mc % 2), 16), dma=True)
            elif mc < 64:
                xc = mc - 32
                for n in range(2):
                    ev_tok[(mc, n)] = B.op("scalar", lambda e, xc=xc, n=n, mi=mi: e.copy(
                        xconv[:, xc * 1027 + 3 + n * 512: xc * 1027 + 3 + (n + 1) * 512],
                        B.bank(mi * 2 + n)),
                        waits=[mmtok])
                ctxev_tok[mc] = B.op("vector", lambda e, xc=xc, mi=mi, mc=mc: e.tensor_scalar(
                    xconv[:, xc * 1027: xc * 1027 + 3],
                    B.bank(6 + mc % 2)[:, mi * 4: mi * 4 + 3], imask[:, 0:1], None, OP.mult),
                    waits=[mmtok, ("imld", 16)])
            elif mc == 64:
                for n in range(2):
                    bk = B.bank(mi * 2 + n)
                    t_min = B.op("vector", lambda e, bk=bk, n=n: e.tensor_scalar(
                        BC[:, n * 512:(n + 1) * 512], bk, 0.0, None, OP.min),
                        waits=[mmtok])
                    t_exp = B.op("scalar", lambda e, n=n: e.activation(
                        BC[:, n * 512:(n + 1) * 512], BC[:, n * 512:(n + 1) * 512], AF.Exp),
                        waits=[t_min])
                    t_rel = B.op("vector", lambda e, bk=bk, n=n: e.tensor_scalar(
                        BC[:, 1024 + n * 512:1024 + (n + 1) * 512], bk, 0.0, None, OP.max),
                        waits=[t_exp])
                    ev_tok[(mc, n)] = B.op("vector", lambda e, n=n: e.tensor_add(
                        BC[:, n * 512:(n + 1) * 512], BC[:, n * 512:(n + 1) * 512],
                        BC[:, 1024 + n * 512:1024 + (n + 1) * 512]))
                B.t_BC = B.tok("oV")
            else:
                for n in range(2):
                    bk = B.bank(mi * 2 + n)[0:64, :]
                    B.op("scalar", lambda e, bk=bk, n=n: e.activation(
                        dt_b[:, n * 512:(n + 1) * 512], bk, AF.Exp,
                        bias=B.s["hcols"][:, 1:2]),
                        waits=[mmtok, B.t_expA])
                    ev_tok[(mc, n)] = B.op("scalar", lambda e, n=n: e.activation(
                        dt_b[:, n * 512:(n + 1) * 512], dt_b[:, n * 512:(n + 1) * 512],
                        AF.Ln, bias=B.s["cvals"][0:64, 1:2]))
                t_la = B.op("vector", lambda e: e.tensor_scalar(
                    dt_b[:, 0:1024], dt_b[:, 0:1024], B.s["hcols"][:, 0:1], -1.0,
                    OP.mult, OP.mult), waits=[ev_tok[(mc, 1)]])
                B.t_cs = B.op("vector", lambda e: e.tensor_tensor_scan(
                    cs_b[:, 0:1024], B.ar(A_CSCAL, 2048, F32, 0, 64)[:, 0:1024],
                    dt_b[:, 0:1024], 0.0, OP.mult, OP.add),
                    waits=[t_la, B.c_cscal])
        B.emit_expansion_unit()
    while B.ex_next < B.n_exu:
        B.emit_expansion_unit()
    B.op("gpsimd",
         lambda e: e.collective_compute(
             "AllGather", OP.bypass, replica_groups=[list(range(8))],
             ins=[B.d["wex_sh"].ap().opt()], outs=[B.d["wex_full"].ap().opt()]),
         waits=[("exs0", 16 * ((B.n_exu + 1) // 2)), ("exs1", 16 * (B.n_exu // 2))],
         inc=("agw", 1), dma=True)
    B.t_xconv_all = B.tok("oS")
    B.t_ctx_all = B.tok("oV")



def phase_ssd(B):
    ys = B.ddbg("ys", [NC2, 128, 4096], BF16)
    B.dtmp("sfin_d", [128, 2048], F32)
    B.dtmp("sfin_ag", [256, 2048], F32)
    B.din("wcv", [128, 128], F32)
    B.din("wcb", [128, 32], F32)
    wc = B.sb("wc", [128, 160], F32)
    pairsel = B.sb("pairsel", [2, 128], F32)
    dbc = B.sb("dbc", [128, 64], F32)
    cmask01 = B.sb("cmask01", [128, 128], BF16)

    states = B.ar(S_STATES, 18432)
    states4 = states.rearrange("p (a b c) -> p a b c", a=32, c=9)
    ych = B.ar(S_YCH, 4096)
    def vct(c):
        return B.ar(S_VCT + c * 4096, 4096)
    attE = B.ar(S_ATTE, 8192)
    argb = B.ar(S_ARG, 4096, F32)
    ksc = B.ar(S_KSC, 4096)
    flat = B.ar(S_FLAT, 4096, F32, 0, 1)
    expcs = B.ar(S_EXPCS, 2048, F32, 0, 64)
    cs_t = B.ar(S_CST, 1024, F32)
    ekt = B.ar(S_EKT, 1024, F32)
    ektb = B.ar(S_EKTB, 512)
    kct = B.ar(S_KCT, 512)
    def vcb(sl):
        return B.ar(S_VCB + sl * 512, 512)
    svct = B.ar(S_SVCT, 4096)
    dex = B.ar(T_DEX, 64, F32, 0, 64)[:, 0:9]
    total = B.ar(T_TOT, 64, F32, 0, 64)[:, 0:8]
    attm = B.ar(T_ATTM, 128)
    totrow = B.ar(T_TOTR, 1152, F32, 0, 1)
    dexp = B.ar(T_DEXP, 64, F32, 0, 2)[:, 0:9]
    sfin = B.ar(S_SFIN, 4096, F32)
    sinitb = B.ar(S_SINITB, 2048)
    def sco(sl):
        return B.ar(S_SCO + sl * 576, 576, BF16)
    def qexp(sl):
        return B.ar(S_QEXP + sl * 1024, 1024, BF16)
    def yst(sl):
        return B.ar(S_YST2 + sl * 1024, 1024)
    p7flat = B.ar(S_FLAT, 4096, F32, 0, 1)
    cs_b = B.csv
    BC = B.BCv
    ysc = B.ar(S_YSC, 32768)
    B.ysc = ysc

    amm = lambda: ("amm", B.val("amm"))

    B.op("gpsimd", lambda e: e.memset(cmask01[:, :], 1.0))
    B.op("gpsimd", lambda e: e.affine_select(
        out=cmask01[:, :], in_=cmask01[:, :], compare_op=OP.is_ge,
        fill=0.0, base=0, pattern=[[1, S2]], channel_multiplier=-1))
    B.op("gpsimd", lambda e: e.memset(pairsel[:, :], 0.0))
    t_psel = B.op("gpsimd", lambda e: e.affine_select(
        out=pairsel[:, :].rearrange("p (a b) -> p a b", b=64),
        in_=pairsel[:, :].rearrange("p (a b) -> p a b", b=64),
        compare_op=OP.not_equal, fill=1.0, base=0,
        pattern=[[1, 2], [0, 64]], channel_multiplier=-1))
    B.op("sync", lambda e: e.dma_start(out=wc[:, 0:128], in_=B.d["wcv"][:, :]),
         inc=("cwld", 16), dma=True)
    B.op("sync", lambda e: e.dma_start(out=wc[:, 128:160], in_=B.d["wcb"][:, :]),
         inc=("cwld", 16), dma=True)

    tr2 = 0
    cvtr_tok = {}
    for hf in range(2):
        for xc in range(NKI):
            j = hf * NKI + xc
            sa = vcb((j % 2) * 2)
            sb_ = vcb((j % 2) * 2 + 1)
            xcv = B.xconv
            base = xc * 1027 + hf * 512
            w = [("cwld", 32), B.t_xconv_all, B.t_ctx_all]
            if j >= 2:
                w.append(cvtr_tok[j - 2])
            B.op("vector", lambda e, sa=sa, base=base, xc=xc: e.tensor_scalar(
                sa[:, :], xcv[:, base:base + 512], wc[:, 4 * xc:4 * xc + 1],
                wc[:, 128 + xc:129 + xc], OP.mult, OP.add), waits=w)
            cur, oth = sa, sb_
            for k in range(1, 4):
                B.op("vector", lambda e, cur=cur, oth=oth, base=base, xc=xc, k=k:
                     e.scalar_tensor_tensor(
                         oth[:, :], xcv[:, base + k:base + k + 512],
                         wc[:, 4 * xc + k:4 * xc + k + 1], cur[:, :], OP.mult, OP.add))
                cur, oth = oth, cur
            t_sil = B.op("scalar", lambda e, cur=cur: e.activation(cur[:, :], cur[:, :], AF.Silu),
                         waits=[B.tok("oV")])
            cb = 5 - (j % 2)
            for sub in range(4):
                tw = [t_sil]
                if j >= 2:
                    tw.append(cvtr_tok[j - 2])
                B.op("tensor", lambda e, cur=cur, sub=sub, cb=cb: e.transpose(
                    B.bankb(cb)[:, sub * 128: (sub + 1) * 128],
                    cur[:, sub * 128:(sub + 1) * 128], B.s["ident_b"][:, :]),
                    waits=tw if sub == 0 else [], inc=("tr2", 1))
                tr2 += 1
            for sub in range(4):
                cc = hf * 4 + sub
                cpw = [("tr2", tr2)] if sub == 0 else []
                if hf == 0 and xc == 0 and sub == 0:
                    cpw += [("zdma0", 16 * 16), ("zdma1", 16 * 16),
                            ("exs0", 16 * ((B.n_exu + 1) // 2)),
                            ("exs1", 16 * (B.n_exu // 2))]
                B.op("scalar", lambda e, cc=cc, xc=xc, sub=sub, cb=cb: e.copy(
                    vct(cc)[:, xc * 128:(xc + 1) * 128],
                    B.bankb(cb)[:, sub * 128: (sub + 1) * 128]),
                    waits=cpw)
            cvtr_tok[j] = B.tok("oS")
    t_conv = B.tok("oS")
    t_szero = B.op("gpsimd", lambda e: e.memset(states4[:, :, :, 0], 0.0),
                   waits=[t_conv])

    qc2 = B.BCv[0:64, 1024:2048]
    B.op("gpsimd", lambda e: e.dma_start(out=qc2, in_=B.BCv[64:128, 0:1024]),
         waits=[B.t_BC], inc=("qcd", 16), dma=True)
    t_expcs = B.op("scalar", lambda e: e.activation(expcs[:, 0:1024], cs_b[:, 0:1024], AF.Exp),
                   waits=[B.t_cs, t_conv])
    for tt in range(8):
        B.op("tensor", lambda e, tt=tt: e.transpose(
            B.bank(0)[:, tt * 64:(tt + 1) * 64],
            cs_b[:, tt * 128:(tt + 1) * 128], B.s["ident_f"][0:64, 0:64]),
            waits=[B.t_cs, ("i1mm", B.val("i1mm")), B.t_xconv_all, B.t_ctx_all]
                  if tt == 0 else [], inc=("amm", 1))
    t_cst = B.op("vector", lambda e: e.tensor_copy(cs_t[:, 0:512], B.bank(0)[:, 0:512]),
                 waits=[amm(), t_conv])
    t_tot = B.op("vector", lambda e: e.tensor_copy(
        total, cs_b[:, 0:1024].rearrange("p (c s) -> p c s", s=128)[:, :, 127]),
        waits=[B.t_cs])
    B.op("gpsimd", lambda e: e.memset(dex[:, 0:1], 0.0), waits=[B.t_cs])
    t_dex = B.op("scalar", lambda e: e.activation(dex[:, 1:9], total, AF.Exp),
                 waits=[t_tot])
    t_totr = B.op("gpsimd", lambda e: e.dma_start(out=totrow[:, 0:512],
                                                  in_=cs_t[127:128, 0:512]),
                  waits=[t_cst], inc=("totd", 16), dma=True)
    for tt in range(8):
        B.op("tensor", lambda e, tt=tt: e.matmul(
            B.bank(1)[:, tt * 64:(tt + 1) * 64], B.s["ones1f"][:, 0:128],
            totrow[:, tt * 64:(tt + 1) * 64], start=True, stop=True),
            waits=[("totd", 16), amm()] if tt == 0 else [], inc=("amm", 1))
    t_ek1 = B.op("vector", lambda e: e.tensor_tensor(
        ekt[:, 0:512], B.bank(1)[:, 0:512], cs_t[:, 0:512], OP.subtract),
        waits=[amm(), t_cst, t_conv])
    t_ek2 = B.op("scalar", lambda e: e.activation(ekt[:, 0:512], ekt[:, 0:512], AF.Exp),
                 waits=[t_ek1])
    t_ekb = B.op("vector", lambda e: e.tensor_copy(ektb[:, 0:512], ekt[:, 0:512]),
                 waits=[t_ek2])
    for tt in range(8):
        B.op("tensor", lambda e, tt=tt: e.transpose(
            B.bank(2)[:, tt * 64:(tt + 1) * 64],
            BC[0:64, tt * 128:(tt + 1) * 128], B.s["ident_f"][0:64, 0:64]),
            waits=[B.t_BC, amm(), t_ek1] if tt == 0 else [], inc=("amm", 1))
    t_kct = B.op("scalar", lambda e: e.copy(kct[:, 0:512], B.bank(2)[:, 0:512]),
                 waits=[amm(), t_conv])
    B.op("tensor", lambda e: e.transpose(B.bank(3)[0:4, 0:64],
                                         B.s["hcols"][:, 0:4], B.s["ident_f"][0:64, 0:64]),
         waits=[B.t_expA, amm()], inc=("amm", 1))
    hrows = B.sb("hrows", [4, 64], F32)
    t_dr0 = B.op("vector", lambda e: e.tensor_copy(hrows[:, :], B.bank(3)[0:4, 0:64]),
                 waits=[amm()])
    B.op("gpsimd", lambda e: e.dma_start(out=totrow[:, 512:576], in_=hrows[2:3, :]),
         waits=[t_dr0], inc=("drd", 16), dma=True)
    B.op("tensor", lambda e: e.matmul(B.bank(3)[:, 128:192], B.s["ones1f"][:, 0:128],
                                      totrow[:, 512:576], start=True, stop=True),
         waits=[("drd", 16)], inc=("amm", 1))
    t_dbc = B.op("vector", lambda e: e.tensor_copy(dbc[:, :], B.bank(3)[:, 128:192]),
                 waits=[amm()])

    ev_st = {}
    ev_y = {}
    for cc in range(NC2):
        t_svc = B.op("vector", lambda e, cc=cc: e.tensor_tensor(
            svct[:, :].rearrange("p (h q) -> p h q", q=64),
            vct(cc)[:, :].rearrange("p (h q) -> p h q", q=64),
            dbc[:, :].unsqueeze(2).broadcast_to([128, 64, 64]),
            OP.mult), waits=[t_conv, t_dbc] + ([amm()] if cc > 0 else []))
        B.op("tensor", lambda e, cc=cc: e.matmul(
            B.bank(0)[:, 0:128], BC[0:64, cc * 128:(cc + 1) * 128],
            qc2[:, cc * 128:(cc + 1) * 128], start=True, stop=True),
            waits=[B.t_BC, ("qcd", 16), amm()], inc=("amm", 1))
        t_attm = B.op("vector", lambda e: e.tensor_tensor(
            attm[:, :], B.bank(0)[:, 0:128], cmask01[:, :], OP.mult),
            waits=[amm(), t_psel])
        t_ksc = B.op("vector", lambda e, cc=cc: e.tensor_tensor(
            ksc[:, :].rearrange("p (h n) -> p h n", n=64),
            kct[:, cc * 64:(cc + 1) * 64].unsqueeze(1).broadcast_to([128, 64, 64]),
            ektb[:, cc * 64:(cc + 1) * 64].unsqueeze(2).broadcast_to([128, 64, 64]),
            OP.mult), waits=[t_ekb, t_kct] + ([amm()] if cc > 0 else []))
        arg_tok = {}
        t_eh_prev = None
        for qq in range(4):
            fw = [amm()]
            if cc == 0 and qq == 0:
                fw.append(("tr2", B.val("tr2")))
            B.op("gpsimd", lambda e, cc=cc, qq=qq: e.dma_start(
                out=flat[:, 0:2048],
                in_=cs_b[16 * qq:16 * (qq + 1), cc * 128:(cc + 1) * 128]),
                waits=fw, inc=("fld", 16), dma=True)
            nf = B.val("fld")
            for p in range(4):
                bk = B.bank(6 + (p % 2))
                bw = []
                if p == 0:
                    bw.append(("fld", nf))
                if (qq, p - 2) in arg_tok:
                    bw.append(arg_tok[(qq, p - 2)])
                elif qq > 0 and (qq - 1, p + 2) in arg_tok:
                    bw.append(arg_tok[(qq - 1, p + 2)])
                B.op("tensor", lambda e, p=p, bk=bk: e.matmul(
                    bk[:, 0:512], B.s["ones1f"][:, 0:128],
                    flat[:, p * 512:(p + 1) * 512], start=True, stop=True),
                    waits=bw, inc=("amm", 1))
                aw = [amm(), t_cst]
                if p == 0 and t_eh_prev is not None:
                    aw.append(t_eh_prev)
                arg_tok[(qq, p)] = B.op("vector", lambda e, cc=cc, qq=qq, p=p: e.tensor_tensor(
                    argb[:, p * 512:(p + 1) * 512].rearrange("p (h s) -> p h s", s=128),
                    B.bank(6 + (p % 2))[:, 0:512].rearrange("p (h s) -> p h s", s=128),
                    cs_t[:, cc * 64 + qq * 16 + p * 4: cc * 64 + qq * 16 + (p + 1) * 4]
                        .unsqueeze(2).broadcast_to([128, 4, 128]),
                    OP.subtract),
                    waits=aw)
            t_cl = B.op("vector", lambda e: e.tensor_scalar(
                argb[:, 0:2048], argb[:, 0:2048], 0.0, None, OP.min),
                waits=[arg_tok[(qq, 3)]])
            ew = [t_cl]
            if cc == 0 and qq == 0:
                ew.append(("tr2", B.val("tr2")))
            if cc > 0 or qq > 0:
                ew.append(amm())
            t_eh_prev = B.op("scalar", lambda e, qq=qq: e.activation(
                attE[:, qq * 2048:(qq + 1) * 2048], argb[:, 0:2048], AF.Exp),
                waits=ew)
            B.op("vector", lambda e, qq=qq: e.tensor_tensor(
                attE[:, qq * 2048:(qq + 1) * 2048].rearrange("p (h s) -> p h s", s=128),
                attE[:, qq * 2048:(qq + 1) * 2048].rearrange("p (h s) -> p h s", s=128),
                attm[:, :].unsqueeze(1).broadcast_to([128, 16, 128]),
                OP.mult), waits=[t_eh_prev, t_attm])
        t_attE = B.tok("oV")
        for pg in range(4):
            ybk = 1 + 2 * (pg % 2)
            sw = [t_ksc, t_attE, t_svc]
            key = (cc, pg - 2) if pg >= 2 else (cc - 1, pg + 2)
            if key in ev_y:
                sw.append(ev_y[key])
            stkey = (cc, pg - 1) if pg >= 1 else (cc - 1, 3)
            if stkey in ev_st:
                sw.append(ev_st[stkey])
            for pi in range(8):
                pair = pg * 8 + pi
                for hh in range(2):
                    h = pair * 2 + hh
                    tp = None if hh == 0 else (0, 64)
                    B.op("tensor", lambda e, h=h, pi=pi, hh=hh, cc=cc, tp=tp: e.matmul(
                        B.bank(5)[(hh * 64):(hh * 64 + 64), pi * 64:(pi + 1) * 64],
                        ksc[:, h * 64:(h + 1) * 64], vct(cc)[:, h * 64:(h + 1) * 64],
                        start=True, stop=True, tile_position=tp),
                        waits=sw if (pi == 0 and hh == 0) else [], inc=("amm", 1))
                    B.op("tensor", lambda e, h=h, pi=pi, hh=hh, cc=cc, tp=tp, ybk=ybk: e.matmul(
                        B.bank(ybk + pi // 4)[(hh * 64):(hh * 64 + 64),
                                              (pi % 4) * 128:(pi % 4 + 1) * 128],
                        vct(cc)[:, h * 64:(h + 1) * 64],
                        attE[:, h * 128:(h + 1) * 128],
                        start=True, stop=False, tile_position=tp),
                        inc=("amm", 1))
                    B.op("tensor", lambda e, h=h, pi=pi, hh=hh, tp=tp, ybk=ybk: e.matmul(
                        B.bank(ybk + pi // 4)[(hh * 64):(hh * 64 + 64),
                                              (pi % 4) * 128:(pi % 4 + 1) * 128],
                        svct[:, h * 64:(h + 1) * 64],
                        B.s["ident_b"][:, :],
                        start=False, stop=True, tile_position=tp),
                        inc=("amm", 1))
            ev_st[(cc, pg)] = B.op("scalar", lambda e, cc=cc, pg=pg: e.copy(
                states4[:, pg * 8:(pg + 1) * 8, :, cc + 1],
                B.bank(5)[:, 0:512].rearrange("p (a b) -> p a b", a=8)),
                waits=[amm(), t_szero])
            yw = [amm()]
            if cc > 0:
                yw.append(("ysd", 16 * cc))
            ev_y[(cc, pg)] = B.op("scalar", lambda e, pg=pg, ybk=ybk: e.copy(
                ych[:, pg * 1024:(pg + 1) * 1024], B.bank(ybk, 2)[:, 0:1024]),
                waits=yw)
        B.op("gpsimd", lambda e, cc=cc: e.dma_start(
            out=ys.ap()[cc, :, :], in_=ych[:, :]),
            waits=[ev_y[(cc, 3)]], inc=("ysd", 16), dma=True)
    t_states_done = B.tok("oS")

    scan_prev = [None]
    def emit_scan(pair, sl, extra_waits):
        B.op("gpsimd", lambda e, pair=pair: e.dma_start(
            out=dexp, in_=dex.tensor[0:0, 0:0] if False else
            B.ar(T_DEX, 64, F32, 2 * pair, 2 * pair + 2)[:, 0:9]),
            waits=[("amm", B.val("amm"))], inc=("dxd", 16), dma=True)
        nd = B.val("dxd")
        for p in range(2):
            B.op("tensor", lambda e, p=p: e.matmul(
                B.bank(6 + p)[:, 0:288], pairsel[:, :],
                dexp.unsqueeze(1).broadcast_to([2, 32, 9]),
                start=True, stop=True),
                waits=[("dxd", nd), t_psel, t_dex] +
                      ((extra_waits + ([scan_prev[0]] if scan_prev[0] else []))
                       if p == 0 else []),
                inc=("amm", 1))
        toks = []
        for p in range(2):
            toks.append(B.op("vector", lambda e, pair=pair, sl=sl, p=p: e.tensor_tensor_scan(
                sco(sl)[:, p * 288:(p + 1) * 288],
                B.bank(6 + p)[:, 0:288],
                states[:, pair * 576 + p * 288: pair * 576 + (p + 1) * 288],
                0.0, OP.mult, OP.add),
                waits=[("amm", B.val("amm"))]))
        scan_prev[0] = toks[-1]
        return toks[-1]

    for pair in range(32):
        t_sc = emit_scan(pair, pair % 2, [t_states_done] if pair == 0 else [])
        B.op("vector", lambda e, pair=pair: e.tensor_copy(
            sfin[:, pair * 64:(pair + 1) * 64],
            sco(pair % 2).rearrange("p (a b) -> p a b", b=9)[:, :, 8]),
            waits=[t_sc])
    t_sfin = B.tok("oV")
    B.op("gpsimd", lambda e: e.dma_start(out=B.d["sfin_d"][:, :], in_=sfin[:, 0:2048]),
         waits=[t_sfin], inc=("sfd", 16), dma=True)
    t_ag = B.op("gpsimd", lambda e: e.collective_compute(
        "AllGather", OP.bypass,
        replica_groups=[[0, 1], [2, 3], [4, 5], [6, 7]],
        ins=[B.d["sfin_d"].ap().opt()], outs=[B.d["sfin_ag"].ap().opt()]),
        waits=[("sfd", 16)], inc=("agp", 1), dma=True)
    B.op("gpsimd", lambda e: e.dma_start(out=sfin[:, 0:2048], in_=B.d["sfin_ag"][0:128, :]),
         waits=[t_ag], inc=("sgl", 16), dma=True)
    t_sin = B.op("vector", lambda e: e.tensor_scalar(
        sinitb[:, 0:2048], sfin[:, 0:2048], B.s["imask"][:, 0:1], None, OP.mult),
        waits=[("sgl", 16), ("imld", 16)])
    t_sl0 = B.op("vector", lambda e: e.tensor_copy(
        states4[:, :, :, 0],
        sinitb[:, 0:2048].rearrange("p (a b) -> p a b", b=64)),
        waits=[t_sin])

    asm_tok = {}
    prev_qtt = None
    for pair in range(32):
        t_scB = emit_scan(pair, pair % 2, [t_sl0] if pair == 0 else [])
        flat2 = B.ar(S_FLAT, 2048, F32, 0, 2)
        B.op("gpsimd", lambda e, pair=pair, flat2=flat2: e.dma_start(
            out=flat2, in_=expcs[2 * pair:2 * pair + 2, 0:1024]),
            waits=[("amm", B.val("amm")), t_expcs], inc=("p7f", 16), dma=True)
        qx = qexp(pair % 2)
        nf = B.val("p7f")
        t_qtt = None
        for p in range(2):
            bw = [("p7f", nf)] if p == 0 else []
            if t_qtt is not None:
                bw.append(t_qtt)
            elif pair > 0:
                bw.append(prev_qtt)
            B.op("tensor", lambda e, p=p, flat2=flat2: e.matmul(
                B.bank(0 if p == 0 else 5)[:, 0:512], pairsel[:, :],
                flat2[:, p * 512:(p + 1) * 512], start=True, stop=True),
                waits=bw, inc=("amm", 1))
            for hh in range(2):
                t_qtt = B.op("vector", lambda e, p=p, hh=hh, qx=qx: e.tensor_tensor(
                    qx[hh * 64:hh * 64 + 64, p * 512:(p + 1) * 512],
                    (qc2 if hh == 0 else BC[64:128, 0:1024])[:, p * 512:(p + 1) * 512]
                    if hh == 0 else BC[64:128, p * 512:(p + 1) * 512],
                    B.bank(0 if p == 0 else 5)[hh * 64:hh * 64 + 64, 0:512], OP.mult),
                    waits=[amm()] if hh == 0 else [])
        prev_qtt = t_qtt
        t_qx = B.tok("oV")
        scosl = sco(pair % 2).rearrange("p (a b) -> p a b", b=9)
        for hh in range(2):
            for c in range(NC2):
                tp = None if hh == 0 else (64, 64)
                ybk = 1 + 2 * (pair % 2)
                B.op("tensor", lambda e, hh=hh, c=c, tp=tp, ybk=ybk, scosl=scosl, qx=qx,
                     pair=pair: e.matmul(
                    B.bank(ybk + c // 4)[(hh * 64):(hh * 64 + 64),
                                         (c % 4) * 128:(c % 4 + 1) * 128],
                    scosl[hh * 64:hh * 64 + 64, :, c],
                    qx[hh * 64:hh * 64 + 64, c * 128:(c + 1) * 128],
                    start=True, stop=True, tile_position=tp),
                    waits=([t_scB, t_qx] + ([asm_tok[pair - 2]] if pair >= 2 else []))
                          if (hh == 0 and c == 0) else [],
                    inc=("amm", 1))
        gw = [("ysd", 16 * NC2)]
        if pair >= 2:
            gw.append(asm_tok[pair - 2])
        B.op("gpsimd", lambda e, pair=pair: e.dma_start(
            out=yst(pair % 2)[:, :],
            in_=B.d["ys"].ap().transpose([1, 0, 2])[:, :, pair * 128:(pair + 1) * 128]),
            waits=gw, inc=("ygd%d" % (pair % 2), 16), dma=True)
        asm_tok[pair] = B.op("vector", lambda e, pair=pair: e.tensor_add(
            ysc[:, pair * 1024:(pair + 1) * 1024],
            yst(pair % 2)[:, :],
            B.bank(1 + 2 * (pair % 2), 2)[:, 0:1024]),
            waits=[("ygd%d" % (pair % 2), 16 * (pair // 2 + 1)), ("amm", B.val("amm"))])
    B.t_ysc = B.tok("oV")


E_GST = 0
E_GBF = 8192
E_TJ = 10240
E_ACC = 12288
E_WEX = 14336
Y_ZSB = 22528
Y_WOP = 24576
Y_WOPB = 26624
Y_SQ = 27648
Y_OFM = 65664
Y_X2 = 16384
Y_U2 = 87040
L_XN2 = 87040
L_SQD = 93440
F_H2 = 0
F_HFF = 32896
F_WGU = 76928
F_WD = 85120
F_FMS = 91136
F_TMP = 95232
F_OUTS = 86144


def phase_tail(B):
    B.din("nsc", [128, 32], F32)
    B.din("wop", [32, 128, 2048], F32)
    B.dout("out_tok", [TOK, DM], F32)

    nsc = B.sb("nsc_s", [128, 32], F32)
    rowstat = B.ar(T_TOTR, 2048, F32, 0, 1)
    B.op("sync", lambda e: e.dma_start(out=nsc[:, :], in_=B.d["nsc"][:, :]),
         inc=("tld", 16), dma=True)

    ysc = B.ysc
    amm = lambda: ("amm", B.val("amm"))
    uv = B.d["u_tok"].ap().rearrange("(n p) d -> n p d", p=128)
    modsel = B.s["modsel"]
    colbuf = B.s["colbuf"]
    gaB = B.s["gaB"]

    ysq = lambda sl: B.ar(Y_SQ + sl * 1024, 1024)
    sq_tok = {}
    for pair in range(32):
        sw = [B.t_ysc]
        if pair >= 2:
            sw.append(("sqm", pair - 1))
            sw.append(("sqm0", pair - 1))
        sq_tok[pair] = B.op("scalar", lambda e, pair=pair: e.activation(
            ysq(pair % 2)[:, :], ysc[:, pair * 1024:(pair + 1) * 1024], AF.Square,
            scale=float(1.0)), waits=sw)
        for n in range(2):
            B.op("tensor", lambda e, pair=pair, n=n: e.matmul(
                B.bank(4 + n)[0:1, 0:512], B.s["onescol"][:, 0:1],
                ysq(pair % 2)[:, n * 512:(n + 1) * 512],
                start=(pair == 0), stop=(pair == 31)),
                waits=[sq_tok[pair]] if n == 0 else [],
                inc=("sqm", 1) if n == 1 else ("sqm0", 1))
    t_rms1 = B.op("scalar", lambda e: e.activation(
        rowstat[0:1, 0:512], B.bank(4)[0:1, 0:512], AF.Ln,
        scale=float(1.0 / DI), bias=B.s["cvals"][0:1, 2:3]),
        waits=[("sqm", 32), ("sqm0", 32)])
    B.op("scalar", lambda e: e.activation(
        rowstat[0:1, 512:1024], B.bank(5)[0:1, 0:512], AF.Ln,
        scale=float(1.0 / DI), bias=B.s["cvals"][0:1, 2:3]))
    t_rms2 = B.op("scalar", lambda e: e.activation(
        rowstat[0:1, 0:1024], rowstat[0:1, 0:1024], AF.Exp, scale=-0.5))
    if B.debug:
        B.dout("dbg_rstd", [1, 1024], F32)
        B.op("gpsimd", lambda e: e.dma_start(out=B.d["dbg_rstd"][:, :],
                                             in_=rowstat[0:1, 0:1024]),
             waits=[t_rms2], inc=("dbg", 16), dma=True)
    for n in range(2):
        B.op("tensor", lambda e, n=n: e.matmul(
            B.bank(6 + n)[:, 0:512], B.s["ones1f"][:, 0:128],
            rowstat[0:1, n * 512:(n + 1) * 512], start=True, stop=True),
            waits=[t_rms2, ("sqm", 32)], inc=("rstb", 1))
    zsb = lambda sl: B.ar(Y_ZSB + sl * 1024, 1024)
    ysc_tok = {}
    for pair in range(32):
        zw = []
        if pair == 0:
            zw += [B.t_ysc, ("zdma0", 256), ("zdma1", 256)]
        if pair >= 2:
            zw.append(ysc_tok[pair - 2])
        B.op("gpsimd", lambda e, pair=pair: e.dma_start(
            out=zsb(pair % 2)[:, :], in_=B.d["zs"].ap()[pair, :, :]),
            waits=zw, inc=("zrl%d" % (pair % 2), 16), dma=True)
        t_prez = B.op("vector", lambda e, pair=pair: e.scalar_tensor_tensor(
            ysc[:, pair * 1024:(pair + 1) * 1024],
            ysc[:, pair * 1024:(pair + 1) * 1024],
            nsc[:, pair:pair + 1],
            B.bank(6, 2)[:, 0:1024],
            OP.mult, OP.mult),
            waits=[("rstb", 2), ("tld", 16)])
        prez_w = []
        if B.debug and pair == 0:
            B.dout("dbg_zsb", [128, 1024], BF16)
            B.op("gpsimd", lambda e: e.dma_start(out=B.d["dbg_zsb"][:, :],
                                                 in_=zsb(0)[:, :]),
                 waits=[("zrl0", 16)], inc=("dbg", 16), dma=True)
        if B.debug and pair == 0:
            B.dout("dbg_prez", [128, 1024], BF16)
            B.op("gpsimd", lambda e: e.dma_start(out=B.d["dbg_prez"][:, :],
                                                 in_=ysc[:, 0:1024]),
                 waits=[t_prez], inc=("dbg", 16), dma=True)
            prez_w = [("dbg", B.val("dbg"))]
        ysc_tok[pair] = B.op("vector", lambda e, pair=pair: e.tensor_mul(
            ysc[:, pair * 1024:(pair + 1) * 1024],
            ysc[:, pair * 1024:(pair + 1) * 1024],
            zsb(pair % 2)[:, :]),
            waits=[("zrl%d" % (pair % 2), 16 * (pair // 2 + 1))] + prez_w)
    t_yscl = B.tok("oV")

    wopst = lambda sl: B.ar(Y_WOP + sl * 1024, 1024, F32)
    wopb = lambda sl: B.ar(Y_WOPB + sl * 512, 512)
    ofm = B.ar(Y_OFM, 16384)
    wv2 = B.d["wop"].ap()
    op_ev = {}
    cast_tok = {}
    amm_after_op = {}
    for mg in range(4):
        for kc in range(32):
            j = mg * 32 + kc
            sl = j % 2
            dw = []
            if j < 2:
                dw.append(B.t_ysc)
            if j >= 2:
                dw.append(cast_tok[j - 2])
            B.op("sync", lambda e, mg=mg, kc=kc, sl=sl: e.dma_start(
                out=wopst(sl)[:, 0:512],
                in_=wv2[kc, :, mg * 512:(mg + 1) * 512]),
                waits=dw, inc=("opw%d" % sl, 16), dma=True)
            cw2 = [("opw%d" % sl, 16 * (j // 2 + 1))]
            if j == 0:
                cw2.append(t_yscl)
            if j >= 2:
                cw2.append(("amm", amm_after_op.get(j - 2, 0)))
            cast_tok[j] = B.op("vector", lambda e, sl=sl: e.tensor_copy(
                wopb(sl)[:, 0:512], wopst(sl)[:, 0:512]),
                waits=cw2)
            for mi in range(4):
                for n in range(2):
                    mw = []
                    if mi == 0 and n == 0:
                        mw = [cast_tok[j], ("rstb", 2)]
                        if kc == 0:
                            mw.append(t_yscl)
                            if mg >= 1:
                                mw += [op_ev[(mg - 1, mi2, n2)] for mi2 in range(4)
                                       for n2 in range(2)]
                    B.op("tensor", lambda e, mi=mi, n=n, sl=sl, kc=kc: e.matmul(
                        B.bank(mi * 2 + n)[:, 0:512],
                        wopb(sl)[:, mi * 128:(mi + 1) * 128],
                        ysc[:, kc * 1024 + n * 512: kc * 1024 + (n + 1) * 512],
                        start=(kc == 0), stop=(kc == 31)),
                        waits=mw, inc=("amm", 1))
            amm_after_op[j] = B.val("amm")
        for mi in range(4):
            for n in range(2):
                ew2 = []
                if mi == 0 and n == 0:
                    ew2 = [amm()]
                op_ev[(mg, mi, n)] = B.op("scalar", lambda e, mg=mg, mi=mi, n=n: e.copy(
                    ofm[:, (mg * 4 + mi) * 1024 + n * 512:(mg * 4 + mi) * 1024 + (n + 1) * 512],
                    B.bank(mi * 2 + n)[:, 0:512]),
                    waits=ew2)
    if B.debug:
        B.dout("dbg_ofm", [128, 16384], BF16)
        B.op("gpsimd", lambda e: e.dma_start(out=B.d["dbg_ofm"][:, :], in_=ofm),
             waits=[B.tok("oS")], inc=("dbg", 16), dma=True)
        B.cnt["rtrHOLD"] = 0
    t_ofm = B.tok("oS")
    if B.debug:
        t_ofm_dbg = ("dbg", B.val("dbg"))
    else:
        t_ofm_dbg = None

    x2 = B.ar(Y_X2, 16384)
    u2 = lambda sl: B.ar(Y_U2 + sl * 4096, 4096, F32)
    restmp = lambda sl: B.ar(sl * 2048, 2048)
    rscr = B.ar(4096, 512)
    res_tok = {}
    rcp_tok = {}
    ntr = 0
    for tt in range(8):
        sl = tt % 2
        dw = [t_ofm] + ([t_ofm_dbg] if t_ofm_dbg else [])
        if tt >= 2:
            dw.append(res_tok[tt - 2])
        B.op("sync", lambda e, tt=tt, sl=sl: e.dma_start(
            out=u2(sl)[:, 0:2048], in_=uv[tt + 1, :, :]),
            waits=dw, inc=("url%d" % sl, 16), dma=True)
        for dkg in range(4):
            for q in range(4):
                dk = dkg * 4 + q
                tw = [t_ofm]
                if ntr >= 32:
                    tw.append(rcp_tok[tt - 2])
                B.op("tensor", lambda e, tt=tt, dk=dk, q=q, dkg=dkg: e.transpose(
                    B.bankb(dkg + 4 * (tt % 2))[:, q * 128:(q + 1) * 128],
                    ofm[:, dk * 1024 + tt * 128: dk * 1024 + (tt + 1) * 128],
                    B.s["ident_b"][:, :]),
                    waits=tw if (dkg == 0 and q == 0) else [], inc=("rtr", 1))
                ntr += 1
        for dkg in range(4):
            cw = []
            if dkg == 0:
                cw.append(("rtr", ntr))
                if tt >= 2:
                    cw.append(res_tok[tt - 2])
            B.op("scalar", lambda e, dkg=dkg, sl=sl: e.copy(
                restmp(sl)[:, dkg * 512:(dkg + 1) * 512],
                B.bankb(dkg + 4 * sl)[:, 0:512]), waits=cw)
        rcp_tok[tt] = B.tok("oS")
        for dkg in range(4):
            B.op("vector", lambda e, dkg=dkg, sl=sl: e.tensor_tensor(
                rscr[:, 0:512],
                restmp(sl)[:, dkg * 512:(dkg + 1) * 512],
                gaB[:, dkg * 512:(dkg + 1) * 512], OP.mult),
                waits=[rcp_tok[tt], ("url%d" % sl, 16 * (tt // 2 + 1))]
                      if dkg == 0 else [])
            rtk = B.op("vector", lambda e, tt=tt, dkg=dkg, sl=sl: e.scalar_tensor_tensor(
                x2[:, tt * 2048 + dkg * 512: tt * 2048 + (dkg + 1) * 512],
                u2(sl)[:, dkg * 512:(dkg + 1) * 512], 1.0,
                rscr[:, 0:512], OP.mult, OP.add))
        res_tok[tt] = rtk
    t_x2 = B.tok("oV")

    h2 = B.ar(F_H2, 16384)
    xn2 = lambda sl: B.ar(L_XN2 + sl * 2048, 2048)
    sqd = B.ar(L_SQD, 2048)
    stat = B.s["stat"]
    ln_tok = {}
    ev2_tok = {}
    ntr2 = 0
    for tt in range(8):
        s0 = 8 * (tt % 2)
        lw = [t_x2]
        t_sum = B.op("vector", lambda e, tt=tt, s0=s0: e.reduce_sum(
            stat[:, s0:s0 + 1], x2[:, tt * 2048:(tt + 1) * 2048], AXX), waits=lw)
        t_qq = B.op("scalar", lambda e, tt=tt, s0=s0: e.activation(
            sqd[:, :], x2[:, tt * 2048:(tt + 1) * 2048], AF.Square,
            scale=float(1.0 / np.sqrt(DM)), accum_out=stat[:, s0 + 1:s0 + 2]),
            waits=lw + ([("rtr", ntr)] if tt == 0 else []))
        B.op("vector", lambda e, s0=s0: e.tensor_mul(
            stat[:, s0 + 2:s0 + 3], stat[:, s0:s0 + 1], stat[:, s0:s0 + 1]))
        t_var = B.op("vector", lambda e, s0=s0: e.scalar_tensor_tensor(
            stat[:, s0 + 3:s0 + 4], stat[:, s0 + 2:s0 + 3], float(-1.0 / (DM * DM)),
            stat[:, s0 + 1:s0 + 2], OP.mult, OP.add), waits=[t_qq])
        t_m = B.op("vector", lambda e, s0=s0: e.tensor_scalar(
            stat[:, s0 + 4:s0 + 5], stat[:, s0:s0 + 1], float(1.0 / DM), None, OP.mult))
        B.op("scalar", lambda e, s0=s0: e.activation(
            stat[:, s0 + 3:s0 + 4], stat[:, s0 + 3:s0 + 4], AF.Ln,
            bias=B.s["cvals"][:, 0:1]), waits=[t_var])
        t_rstd = B.op("scalar", lambda e, s0=s0: e.activation(
            stat[:, s0 + 5:s0 + 6], stat[:, s0 + 3:s0 + 4], AF.Exp, scale=-0.5))
        xw = [t_rstd, t_m]
        if tt >= 2:
            xw.append(("ltr", ntr2))
        ln_tok[tt] = B.op("vector", lambda e, tt=tt, s0=s0: e.tensor_scalar(
            xn2((tt % 2))[:, :], x2[:, tt * 2048:(tt + 1) * 2048],
            stat[:, s0 + 4:s0 + 5], stat[:, s0 + 5:s0 + 6],
            OP.subtract, OP.mult), waits=xw)
        for dkg in range(4):
            for q in range(4):
                dk = dkg * 4 + q
                tw2 = [ln_tok[tt]]
                if (tt, dkg) not in ((0, 0), (0, 1)):
                    key = (tt, dkg - 2) if dkg >= 2 else (tt - 1, dkg + 2)
                    tw2.append(ev2_tok[key])
                B.op("tensor", lambda e, tt=tt, dk=dk, q=q, dkg=dkg: e.transpose(
                    B.bankb(4 + dkg % 2)[:, q * 128:(q + 1) * 128],
                    xn2((tt % 2))[:, dk * 128:(dk + 1) * 128],
                    B.s["ident_b"][:, :]),
                    waits=tw2 if q == 0 else [], inc=("ltr", 1))
                ntr2 += 1
            evw = [("ltr", ntr2)]
            for q in range(4):
                dk = dkg * 4 + q
                tok = B.op("scalar", lambda e, tt=tt, dk=dk, q=q, dkg=dkg: e.activation(
                    h2[:, dk * 1024 + tt * 128: dk * 1024 + (tt + 1) * 128],
                    B.bankb(4 + dkg % 2)[:, q * 128:(q + 1) * 128],
                    AF.Identity, bias=modsel[:, 48 + dk:49 + dk],
                    scale=colbuf[:, 16 + dk:17 + dk]),
                    waits=evw if q == 0 else [])
            ev2_tok[(tt, dkg)] = tok
    t_h2 = B.tok("oS")

    hff = B.ar(F_HFF, 44032)
    wgu = B.ar(F_WGU, 8192)
    tmpg = B.ar(F_TMP, 1024)
    wfv = B.d["wex_full"].ap()
    gu_ev = {}
    for mc in range(NMI):
        pb = (mc % 2) * 4
        dw = [("agw", 1), t_h2]
        if mc >= 1:
            dw.append(gu_ev[mc - 1])
        B.op("gpsimd", lambda e, mc=mc: e.dma_start(
            out=wgu[:, 0:2048].rearrange("p (a b) -> p a b", a=16),
            in_=wfv[mc:688:43, :, :].transpose([1, 0, 2])),
            waits=dw, inc=("fgw", 16), dma=True)
        B.op("gpsimd", lambda e, mc=mc: e.dma_start(
            out=wgu[:, 2048:4096].rearrange("p (a b) -> p a b", a=16),
            in_=wfv[688 + mc:1376:43, :, :].transpose([1, 0, 2])),
            inc=("fgw", 16), dma=True)
        nw = B.val("fgw")
        for kt in range(16):
            for n in range(2):
                for gup in range(2):
                    mw = [("fgw", nw)] if (kt == 0 and n == 0 and gup == 0) else []
                    if kt == 0 and n == 0 and gup == 0 and mc >= 2:
                        mw.append(gu_ev[mc - 2])
                    B.op("tensor", lambda e, kt=kt, n=n, gup=gup, pb=pb: e.matmul(
                        B.bank((pb + gup * 2 + n) % 8)[:, 0:512],
                        wgu[:, gup * 2048 + kt * 128: gup * 2048 + (kt + 1) * 128],
                        h2[:, kt * 1024 + n * 512: kt * 1024 + (n + 1) * 512],
                        start=(kt == 0), stop=(kt == 15)),
                        waits=mw, inc=("amm", 1))
        for n in range(2):
            B.op("scalar", lambda e, n=n, pb=pb: e.activation(
                tmpg[:, n * 512:(n + 1) * 512], B.bank((pb + n) % 8)[:, 0:512],
                AF.Silu), waits=[amm()] if n == 0 else [])
        t_sg = B.tok("oS")
        for n in range(2):
            tokf = B.op("vector", lambda e, mc=mc, n=n, pb=pb: e.tensor_mul(
                hff[:, mc * 1024 + n * 512: mc * 1024 + (n + 1) * 512],
                tmpg[:, n * 512:(n + 1) * 512],
                B.bank((pb + 2 + n) % 8)[:, 0:512]),
                waits=[t_sg, amm()] if n == 0 else [])
        gu_ev[mc] = tokf
    t_hff = B.tok("oV")

    wd = lambda sl: B.ar(F_WD + sl * 512, 512)
    fms = lambda sl: B.ar(F_FMS + sl * 2048, 2048)
    outs = lambda sl: B.ar(F_OUTS + sl * 1024, 1024, F32)
    ftmp = lambda sl: B.ar(96256 + sl * 512, 512)
    gbB = gaB[:, 2048:4096]
    dn_cast = {}
    amm_after_dn = {}
    fin_tok = {}
    nfr = 0
    for mg in range(4):
        for kc in range(NMI):
            j = mg * NMI + kc
            sl = j % 2
            dw = [t_hff] if j == 0 else []
            if j >= 2:
                dw.append(("amm", amm_after_dn.get(j - 2, 0)))
            B.op("gpsimd", lambda e, mg=mg, kc=kc, sl=sl: e.dma_start(
                out=wd(sl)[:, 0:512].rearrange("p (a b) -> p a b", a=4),
                in_=wfv[1376 + kc * 16 + mg * 4:1376 + kc * 16 + mg * 4 + 4, :, :]
                    .transpose([1, 0, 2])),
                waits=dw, inc=("fdw%d" % sl, 16), dma=True)
            dn_cast[j] = ("fdw%d" % sl, 16 * (j // 2 + 1))
            for mi in range(4):
                for n in range(2):
                    mw = []
                    if mi == 0 and n == 0:
                        mw = [dn_cast[j]]
                        if kc == 0:
                            mw.append(t_hff)
                            if mg >= 1:
                                mw.append(fin_tok[mg - 1])
                    B.op("tensor", lambda e, mi=mi, n=n, sl=sl, kc=kc: e.matmul(
                        B.bank(mi * 2 + n)[:, 0:512],
                        wd(sl)[:, mi * 128:(mi + 1) * 128],
                        hff[:, kc * 1024 + n * 512: kc * 1024 + (n + 1) * 512],
                        start=(kc == 0), stop=(kc == NMI - 1)),
                        waits=mw, inc=("amm", 1))
            amm_after_dn[j] = B.val("amm")
        for mi in range(4):
            for n in range(2):
                B.op("scalar", lambda e, mg=mg, mi=mi, n=n: e.copy(
                    fms(n)[:, mi * 512:(mi + 1) * 512],
                    B.bank(mi * 2 + n)[:, 0:512]),
                    waits=[amm()] if (mi == 0 and n == 0) else [])
        t_fms = B.tok("oS")
        for n in range(2):
            for q in range(4):
                tt = n * 4 + q
                bkp = 4 + (q % 2) * 2
                fsl = q % 2
                for mi in range(4):
                    tw = [t_fms]
                    key = ("bk", bkp)
                    if key in fin_tok:
                        tw.append(fin_tok[key])
                    B.op("tensor", lambda e, n=n, q=q, mi=mi, bkp=bkp: e.transpose(
                        B.bankb(bkp)[:, mi * 128:(mi + 1) * 128],
                        fms(n)[:, mi * 512 + q * 128: mi * 512 + (q + 1) * 128],
                        B.s["ident_b"][:, :]),
                        waits=tw if mi == 0 else [], inc=("ftr", 1))
                    nfr += 1
                cw = [("ftr", nfr)]
                if ("f", fsl) in fin_tok:
                    cw.append(fin_tok[("f", fsl)])
                t_cp = B.op("scalar", lambda e, bkp=bkp, fsl=fsl: e.copy(
                    ftmp(fsl)[:, 0:512], B.bankb(bkp)[:, 0:512]), waits=cw)
                fin_tok[("bk", bkp)] = t_cp
                t_mul = B.op("vector", lambda e, mg=mg, fsl=fsl: e.tensor_tensor(
                    B.ar(65664, 512)[:, 0:512],
                    ftmp(fsl)[:, 0:512],
                    gbB[:, mg * 512:(mg + 1) * 512], OP.mult),
                    waits=[t_cp])
                fin_tok[("f", fsl)] = t_mul
                osl = tt % 2
                ost_w = []
                if B.val("outd%d" % osl) >= 16:
                    ost_w.append(("outd%d" % osl, B.val("outd%d" % osl)))
                t_o = B.op("vector", lambda e, mg=mg, tt=tt: e.scalar_tensor_tensor(
                    outs(tt % 2)[:, 0:512],
                    x2[:, tt * 2048 + mg * 512: tt * 2048 + (mg + 1) * 512], 1.0,
                    B.ar(65664, 512)[:, 0:512], OP.mult, OP.add),
                    waits=ost_w)
                ow = [t_o]
                B.op("gpsimd", lambda e, mg=mg, tt=tt: e.dma_start(
                    out=B.d["out_tok"].ap()[tt * 128:(tt + 1) * 128,
                                            mg * 512:(mg + 1) * 512],
                    in_=outs(tt % 2)[:, 0:512]),
                    waits=ow, inc=("outd%d" % (tt % 2), 16), dma=True)
        fin_tok[mg] = B.tok("oV")
    B.t_out = ("outd0", B.val("outd0"))
    B.t_out2 = ("outd1", B.val("outd1"))


def prep_inputs(inp):
    u = np.asarray(inp["L0_attn_u"], np.float32)
    c = np.asarray(inp["L0_c"], np.float32)
    wmod = np.asarray(inp["L0_wmod"], np.float32)
    ipw = np.asarray(inp["L0_attn_in_proj_w"], np.float32)
    wip = np.zeros((22, 2048, 384), np.float32)
    for g in range(22):
        c0 = g * 384
        c1 = min(c0 + 384, DIP)
        if c1 > c0:
            wip[g, :, 0:c1 - c0] = ipw[:, c0:c1]
    cw = np.asarray(inp["L0_attn_conv_weight"], np.float32)
    cb = np.asarray(inp["L0_attn_conv_bias"], np.float32)
    wcv = np.zeros((128, 128), np.float32)
    for xc in range(32):
        for k in range(4):
            wcv[:, 4 * xc + k] = cw[k, 128 * xc:128 * (xc + 1)]
    wcb = np.ascontiguousarray(cb.reshape(32, 128).T)
    hvec = np.zeros((64, 4), np.float32)
    hvec[:, 0] = np.asarray(inp["L0_attn_A_log"], np.float32)
    hvec[:, 1] = np.asarray(inp["L0_attn_dt_bias"], np.float32)
    hvec[:, 2] = np.asarray(inp["L0_attn_D_param"], np.float32)
    nscale = np.asarray(inp["L0_attn_norm_scale"], np.float32)
    nsc = np.ascontiguousarray(nscale.reshape(32, 128).T)
    wop = np.ascontiguousarray(np.asarray(inp["L0_attn_out_proj_w"], np.float32)
                               .reshape(32, 128, 2048))
    lutb = np.ascontiguousarray(
        np.broadcast_to(np.asarray(inp["L0_ffn_lut"], np.float32)[None, :], (128, 16)))
    gidx = np.asarray(inp["L0_ffn_gate_idx"], np.int32)
    uidx = np.asarray(inp["L0_ffn_up_idx"], np.int32)
    didx = np.asarray(inp["L0_ffn_down_idx"], np.int32)
    tiles = []
    for t in range(N_TILES):
        if t < 688:
            kc, mc = t // 43, t % 43
            blk = gidx[mc * 128:(mc + 1) * 128, kc * 128:(kc + 1) * 128]
        elif t < 1376:
            tt_ = t - 688
            kc, mc = tt_ // 43, tt_ % 43
            blk = uidx[mc * 128:(mc + 1) * 128, kc * 128:(kc + 1) * 128]
        else:
            tt_ = t - 1376
            kc, mc = tt_ // 16, tt_ % 16
            blk = didx[mc * 128:(mc + 1) * 128, kc * 128:(kc + 1) * 128]
        tiles.append(blk)
    tiles = np.ascontiguousarray(np.stack(tiles))
    maps = []
    for core in range(8):
        b, half = core // 2, core % 2
        t0 = half * 1024
        ut = np.zeros((PTOK, DM), np.float32)
        if t0 >= 3:
            ut[CTX0:REAL0] = u[b, t0 - 3:t0]
        ut[REAL0:] = u[b, t0:t0 + 1024]
        ctb = np.ascontiguousarray(
            c.T[256 * core:256 * (core + 1)].reshape(2, 128, 4).transpose(1, 0, 2).reshape(128, 8))
        bselB = np.zeros((128, 4), np.float32)
        bselB[:, b] = 1.0
        maps.append({
            "ctb": ctb,
            "wmod_sh": np.ascontiguousarray(wmod[256 * core:256 * (core + 1)]),
            "bselB": bselB,
            "u_tok": ut,
            "wip": wip,
            "hvec": hvec,
            "initmask": np.full((128, 1), float(half), np.float32),
            "wcv": wcv,
            "wcb": wcb,
            "nsc": nsc,
            "wop": wop,
            "lutb": lutb,
            "gsrc": np.ascontiguousarray(
                tiles[TPC * core:TPC * (core + 1)].transpose(0, 2, 1)
            ).astype(ml_dtypes.bfloat16),
        })
    return maps


def build(debug=False, phases=9, modstop=99):
    B = Builder(debug)
    B.modstop = modstop
    B.din("u_tok", [PTOK, DM], F32)
    B.BIG = B.sb("BIG", [128, ACOLS], BF16)
    B.PS = B.nc.alloc_psum_tensor("PS", [128, 4096], F32)
    phase_const(B)
    if phases >= 1:
        phase_mod(B)
    if phases >= 2:
        phase_ln1(B)
    if phases >= 3:
        phase_inproj(B)
    if phases >= 4:
        phase_ssd(B)
    if phases >= 5:
        phase_tail(B)
    if phases < 5:
        for s in list(B.cnt):
            tok = (s, B.val(s))
            if tok[1] > 0 and s not in ("dbg",):
                B.raw("gpsimd", lambda e, tok=tok: e.wait_ge(B.sem(tok[0]), tok[1]))
        B.dout_dummy = B.dout("out_tok", [TOK, DM], F32) if "out_tok" not in B.d else None
        return B
    if debug:
        B.dout("dbg_ysc", [128, 32768], BF16)
        B.op("gpsimd", lambda e: e.dma_start(out=B.d["dbg_ysc"][:, :], in_=B.ysc),
             waits=[B.t_ysc], inc=("dbg", 16), dma=True)
        B.dout("dbg_x2", [128, 16384], BF16)
        B.op("gpsimd", lambda e: e.dma_start(out=B.d["dbg_x2"][:, :],
                                             in_=B.ar(Y_X2, 16384)),
             waits=[B.tok("oV")], inc=("dbg", 16), dma=True)
        B.dout("dbg_ysl", [128, 32768], BF16)
        B.op("gpsimd", lambda e: e.dma_start(out=B.d["dbg_ysl"][:, :], in_=B.ysc),
             waits=[B.tok("oV")], inc=("dbg", 16), dma=True)
        tok = B.t_out
        tok2 = B.t_out2
        B.raw("gpsimd", lambda e, tok=tok: e.wait_ge(B.sem(tok[0]), tok[1]))
        B.raw("gpsimd", lambda e, tok2=tok2: e.wait_ge(B.sem(tok2[0]), tok2[1]))
        B.raw("gpsimd", lambda e: e.wait_ge(B.sem("dbg"), B.val("dbg")))
    else:
        tok = B.t_out
        tok2 = B.t_out2
        B.raw("gpsimd", lambda e, tok=tok: e.wait_ge(B.sem(tok[0]), tok[1]))
        B.raw("gpsimd", lambda e, tok2=tok2: e.wait_ge(B.sem(tok2[0]), tok2[1]))
    return B


def kernel(**inputs):
    maps = prep_inputs(inputs)
    B = build(False)
    nc = B.finish()
    res = run_bass_kernel_spmd(nc, maps, core_ids=list(range(8)))
    out = np.zeros((4, 2048, 2048), np.float32)
    for core in range(8):
        b, half = core // 2, core % 2
        out[b, half * 1024:(half + 1) * 1024, :] = res.results[core]["out_tok"]
    return out


# version 07
# speedup vs baseline: 1.0335x; 1.0335x over previous
import contextlib
import ml_dtypes
import numpy as np

import concourse.bass as bass
import concourse.mybir as mybir
from concourse.bass_utils import run_bass_kernel_spmd

F32 = mybir.dt.float32
BF16 = mybir.dt.bfloat16
I32 = mybir.dt.int32
AF = mybir.ActivationFunctionType
OP = mybir.AluOpType
AXX = mybir.AxisListType.X

DM = 2048
DI = 4096
NH = 64
HD = 64
NST = 64
CH = 64
NCH = 16
DIP = 8384
DIPP = 8448
INTER = 5504
TOK = 1024
PTOK = 1152
CTX0 = 125
REAL0 = 128
NKD = 16
NKI = 32
NMIP = 66
NMI = 43
LN_EPS = 1e-5
RMS_EPS = 1e-6
N_TILES = 16 * NMI * 2 + NMI * 16
TPC = N_TILES // 8
ACOLS = 98304

ORD = {"vector": "oV", "scalar": "oS", "gpsimd": "oG"}


class Builder:
    def __init__(self, debug=False):
        self.debug = debug
        self.nc = bass.Bass("TRN2", target_bir_lowering=False)
        self.streams = {k: [] for k in ("tensor", "vector", "scalar", "gpsimd", "sync")}
        self.sems = {}
        self.cnt = {}
        self._stack = contextlib.ExitStack()
        self.d = {}
        self.s = {}

    def sem(self, name):
        if name not in self.sems:
            self.sems[name] = self._stack.enter_context(self.nc.semaphore(name))
            self.cnt[name] = 0
        return self.sems[name]

    def val(self, name):
        return self.cnt.get(name, 0)

    def tok(self, name):
        return (name, self.val(name))

    def din(self, name, shape, dtype):
        self.d[name] = self.nc.dram_tensor(name, shape, dtype, kind="ExternalInput")
        return self.d[name]

    def dout(self, name, shape, dtype):
        self.d[name] = self.nc.dram_tensor(name, shape, dtype, kind="ExternalOutput")
        return self.d[name]

    def dtmp(self, name, shape, dtype, **kw):
        self.d[name] = self.nc.dram_tensor(name, shape, dtype, **kw)
        return self.d[name]

    def ddbg(self, name, shape, dtype):
        return self.dout(name, shape, dtype) if self.debug else self.dtmp(name, shape, dtype)

    def sb(self, name, shape, dtype):
        self.s[name] = self.nc.alloc_sbuf_tensor(name, shape, dtype)
        return self.s[name]

    def ar(self, off, cols, dtype=BF16, p0=0, p1=128):
        v = self.BIG[p0:p1, off:off + cols]
        if dtype == BF16:
            return v
        return v.bitcast(dtype)

    def bank(self, i, j=1):
        return self.PS[:, 512 * i:512 * (i + j)]

    def bankb(self, i, j=1):
        return self.PS[:, 512 * i:512 * (i + j)].bitcast(BF16)

    def op(self, engine, fn, waits=(), inc=None, dma=False):
        waits = [(s, v) for s, v in waits if v > 0]
        oname = ORD.get(engine)
        upd = None
        if oname is not None:
            self.sem(oname)
            prev = self.val(oname)
            if prev > 0:
                waits.append((oname, prev))
            if not dma:
                assert inc is None
                self.cnt[oname] = prev + 1
                upd = (oname, 1)
        if upd is None and inc is not None:
            s, n = inc
            self.sem(s)
            self.cnt[s] += n
            upd = (s, n)
        wl = [(self.sem(s), v) for s, v in waits]
        il = None if upd is None else (self.sems[upd[0]], upd[1])

        def run(eng, fn=fn, wl=wl, il=il):
            for s, v in wl:
                eng.wait_ge(s, v)
            ins = fn(eng)
            if il is not None:
                ins.then_inc(il[0], il[1])
        self.streams[engine].append(run)
        if upd is not None:
            return (upd[0], self.val(upd[0]))
        return None

    def raw(self, engine, fn):
        self.streams[engine].append(fn)

    def finish(self):
        nc = self.nc
        with nc.Block() as block:
            for name in ("sync", "tensor", "vector", "scalar", "gpsimd"):
                def section(eng, name=name):
                    for f in self.streams[name]:
                        f(eng)
                getattr(block, name)(section)
        self._stack.close()
        return nc


A_WSTAGE = 0
A_MPIECE = 32768
A_MSTAGE = 36864
A_GFLAT = 40960
A_USTAGE = 49152
A_SQDUMP = 57344
A_XN = 59392
A_HFM = 67584
A_XCONV = 0
A_W3 = 33024
A_WBF = 36608
A_ZST = 48896
A_BC = 87040
A_DT = 91136
A_CS = 93184
A_CSCAL = 95232

S2 = 128
NC2 = 8
S_STATES = 0
S_YCH = 18432
S_EXPCS = 22528
S_CST = 24576
S_EKT = 25600
S_EKTB = 26624
S_KCT = 27136
S_FLAT = 27648
S_VCT = 32896
S_VCB = 65664
S_ATTE = 65664
S_ARG = 73856
S_KSC = 77952
S_SVCT = 82048
T_DEX = 91136
T_TOT = 91200
T_ATTM = 91264
T_TOTR = 91392
T_DEXP = 92544
S_SFIN = 65664
S_SINITB = 69760
S_SCO = 71808
S_QEXP = 73856
S_YST2 = 77952
S_YSC = 32896

def phase_const(B):
    ident_b = B.sb("ident_b", [128, 128], BF16)
    ident_f = B.sb("ident_f", [128, 128], F32)
    ones1 = B.sb("ones1", [1, 128], BF16)
    ones1f = B.sb("ones1f", [1, 128], F32)
    cmask = B.sb("cmask", [64, 64], F32)
    cvals = B.sb("cvals", [128, 4], F32)

    for t in (ident_b, ident_f):
        B.op("gpsimd", lambda e, t=t: e.memset(t[:, :], 0.0))
        B.op("gpsimd", lambda e, t=t: e.affine_select(
            out=t[:, :], in_=t[:, :], compare_op=OP.not_equal,
            fill=1.0, base=0, pattern=[[-1, 128]], channel_multiplier=1))
    onescol = B.sb("onescol", [128, 1], BF16)
    B.op("gpsimd", lambda e: e.memset(onescol[:, :], 1.0))
    B.op("gpsimd", lambda e: e.memset(ones1[:, :], 1.0))
    B.op("gpsimd", lambda e: e.memset(ones1f[:, :], 1.0))
    B.op("gpsimd", lambda e: e.memset(cvals[:, 0:1], float(LN_EPS)))
    B.op("gpsimd", lambda e: e.memset(cvals[:, 1:2], 1.0))
    B.op("gpsimd", lambda e: e.memset(cvals[:, 2:3], float(RMS_EPS)))
    B.op("gpsimd", lambda e: e.memset(cmask[:, :], 0.0))
    B.c_const = B.op("gpsimd", lambda e: e.affine_select(
        out=cmask[:, :], in_=cmask[:, :], compare_op=OP.is_ge,
        fill=-1e30, base=0, pattern=[[1, 64]], channel_multiplier=-1))
    csc = B.ar(A_CSCAL, 2048, F32, 0, 64)
    B.op("gpsimd", lambda e, csc=csc: e.memset(csc[:, :], 1.0))
    B.c_cscal = B.op("gpsimd", lambda e, csc=csc: e.affine_select(
        out=csc[:, 0:1024].rearrange("p (c s) -> p c s", s=128),
        in_=csc[:, 0:1024].rearrange("p (c s) -> p c s", s=128),
        compare_op=OP.not_equal, fill=0.0, base=0,
        pattern=[[0, 8], [1, 128]], channel_multiplier=0))
    B.din("hvec", [64, 4], F32)
    hcols = B.sb("hcols", [64, 4], F32)
    t_ld = B.op("sync", lambda e: e.dma_start(out=hcols[:, :], in_=B.d["hvec"][:, :]),
                inc=("hvld", 16), dma=True)
    B.t_expA = B.op("scalar", lambda e: e.activation(hcols[:, 0:1], hcols[:, 0:1], AF.Exp),
                    waits=[t_ld])


def phase_mod(B):
    B.din("ctb", [128, 8], F32)
    B.din("wmod_sh", [256, 12288], F32)
    B.din("bselB", [128, 4], F32)
    B.dtmp("mod_part", [4, 12288], F32)
    B.dtmp("mod_full", [4, 12288], F32)

    silc = B.sb("silc", [128, 8], F32)
    bsel_sb = B.sb("bsel_sb", [128, 4], F32)
    modcols = B.sb("modcols", [128, 384], F32)
    modsel = B.sb("modsel", [128, 96], F32)
    colbuf = B.sb("colbuf", [128, 64], F32)
    tcol = B.sb("tcol", [128, 32], F32)
    grow = B.sb("grow", [32, 128], F32)
    gaB = B.sb("gaB", [128, 4096], BF16)

    wstage = B.ar(A_WSTAGE, 16384, BF16)
    mpiece = B.ar(A_MPIECE, 4096, F32)
    mstage = B.ar(A_MSTAGE, 4096, F32, 0, 4)
    gflat = B.ar(A_GFLAT, 8192, F32, 0, 1)
    mpT = B.bank(2)

    def mp4s(ns):
        return B.bank(ns % 2)[0:4, :]

    wv = B.d["wmod_sh"].ap().rearrange("(a p) n -> a p n", p=128)

    B.op("sync", lambda e: e.dma_start(out=silc[:, :], in_=B.d["ctb"][:, :]),
         inc=("p0ld", 16), dma=True)
    B.op("sync", lambda e: e.dma_start(out=bsel_sb[:, :], in_=B.d["bselB"][:, :]),
         inc=("p0ld", 16), dma=True)
    t_silu = B.op("scalar", lambda e: e.activation(silc[:, :], silc[:, :], AF.Silu),
                  waits=[("p0ld", 32)])
    if getattr(B, "modstop", 99) <= 1:
        return

    mm_toks = {}
    ev_toks = {}
    for cc in range(6):
        for ns in range(4):
            li = 4 * cc + ns
            for kt in range(2):
                g = 2 * cc + kt
                slot = g % 4
                waits = [("p0w%d" % slot, (g // 4 + 1) * 16)]
                if cc == 0 and ns == 0 and kt == 0:
                    waits.append(t_silu)
                if kt == 0 and li >= 2:
                    waits.append(ev_toks[li - 2])
                mm_toks[(g, ns)] = B.op(
                    "tensor",
                    lambda e, ns=ns, kt=kt, slot=slot: e.matmul(
                        mp4s(ns), silc[:, kt * 4:(kt + 1) * 4],
                        wstage[:, slot * 4096 + ns * 1024: slot * 4096 + (ns + 1) * 1024].bitcast(F32),
                        start=(kt == 0), stop=(kt == 1)),
                    waits=waits, inc=("p0mm", 1))
            evw = [mm_toks[(2 * cc + 1, ns)]]
            if cc >= 1:
                evw.append(("p0st", 16 * cc))
            ev_toks[li] = B.op(
                "scalar",
                lambda e, ns=ns: e.copy(mstage.bitcast(F32)[:, ns * 512:(ns + 1) * 512],
                                        mp4s(ns)),
                waits=evw)
        B.op("gpsimd",
             lambda e, cc=cc: e.dma_start(out=B.d["mod_part"][:, cc * 2048:(cc + 1) * 2048],
                                          in_=mstage.bitcast(F32)[:, :]),
             waits=[ev_toks[4 * cc + 3]], inc=("p0st", 16), dma=True)

    for cc in range(6):
        for kt in range(2):
            g = 2 * cc + kt
            slot = g % 4
            waits = []
            if g >= 4:
                waits.append(mm_toks[(g - 4, 3)])
            B.op("sync",
                 lambda e, kt=kt, cc=cc, slot=slot: e.dma_start(
                     out=wstage[:, slot * 4096:(slot + 1) * 4096].bitcast(F32),
                     in_=wv[kt, :, cc * 2048:(cc + 1) * 2048]),
                 waits=waits, inc=("p0w%d" % slot, 16), dma=True)

    if getattr(B, "modstop", 99) <= 2:
        return
    t_ar = B.op("gpsimd",
                lambda e: e.collective_compute(
                    "AllReduce", OP.add, replica_groups=[list(range(8))],
                    ins=[B.d["mod_part"].ap().opt()], outs=[B.d["mod_full"].ap().opt()]),
                waits=[("p0st", 96)], inc=("p0ar", 1), dma=True)
    if getattr(B, "modstop", 99) <= 3:
        return

    mpf = mpiece
    B.op("gpsimd", lambda e: e.memset(mpf[:, :], 0.0))
    tr_toks = {}
    cp_toks = {}
    ntr = 0
    for pc in range(6):
        waits = [t_ar]
        if pc:
            waits.append(tr_toks[ntr - 1])
        B.op("gpsimd",
             lambda e, pc=pc: e.dma_start(out=mpf[0:4, :],
                                          in_=B.d["mod_full"][:, pc * 2048:(pc + 1) * 2048]),
             waits=waits, inc=("p0mp", 16), dma=True)
        for j in range(16):
            twaits = [("p0mp", (pc + 1) * 16), B.c_const]
            if ntr >= 4:
                twaits.append(cp_toks[ntr - 4])
            tr_toks[ntr] = B.op(
                "tensor",
                lambda e, j=j, ntr=ntr: e.matmul(
                    B.bank(ntr % 4)[:, 0:128],
                    mpf[:, j * 128:(j + 1) * 128],
                    B.s["ident_f"][:, :],
                    start=True, stop=True),
                waits=twaits, inc=("p0tr", 1))
            cp_toks[ntr] = B.op(
                "scalar",
                lambda e, pc=pc, j=j, ntr=ntr: e.copy(
                    modcols[:, (pc * 16 + j) * 4:(pc * 16 + j) * 4 + 4],
                    B.bank(ntr % 4)[:, 0:4]),
                waits=[tr_toks[ntr]])
            ntr += 1

    if getattr(B, "modstop", 99) <= 4:
        return
    for b in range(4):
        src = modcols[:, :].rearrange("p (n c) -> p n c", c=4)[:, :, b]
        if b == 0:
            B.op("vector",
                 lambda e, src=src: e.tensor_scalar(modsel[:, :], src, bsel_sb[:, 0:1],
                                                    None, OP.mult),
                 waits=[cp_toks[95], ("p0ld", 32)])
        else:
            B.op("vector",
                 lambda e, src=src, b=b: e.scalar_tensor_tensor(
                     modsel[:, :], src, bsel_sb[:, b:b + 1], modsel[:, :], OP.mult, OP.add))

    B.op("vector", lambda e: e.tensor_scalar(colbuf[:, 0:16], modsel[:, 16:32], 1.0, None, OP.add))
    B.op("vector", lambda e: e.tensor_scalar(colbuf[:, 16:32], modsel[:, 64:80], 1.0, None, OP.add))
    for k, src0 in ((0, 32), (1, 80)):
        t = tcol[:, k * 16:(k + 1) * 16]
        dst0 = 32 + k * 16
        B.op("vector", lambda e, t=t, src0=src0: e.tensor_scalar(
            t, modsel[:, src0:src0 + 16], -1.0, None, OP.mult))
        B.op("vector", lambda e, t=t, src0=src0: e.tensor_tensor(
            t, modsel[:, src0:src0 + 16], t, OP.max))
        B.op("vector", lambda e, t=t: e.tensor_scalar(t, t, 0.25, None, OP.max))
        B.op("vector", lambda e, dst0=dst0, src0=src0: e.tensor_scalar(
            colbuf[:, dst0:dst0 + 16], modsel[:, src0:src0 + 16], 0.0, None, OP.is_ge))
    B.op("vector", lambda e: e.tensor_scalar(colbuf[:, 32:64], colbuf[:, 32:64],
                                             2.0, -1.0, OP.mult, OP.add))
    B.t_col = B.op("vector", lambda e: e.tensor_mul(colbuf[:, 32:64], colbuf[:, 32:64],
                                                    tcol[:, 0:32]))

    if getattr(B, "modstop", 99) <= 5:
        return
    t_gt = B.op("tensor", lambda e: e.matmul(mpT[0:32, 0:128], colbuf[:, 32:64],
                                             B.s["ident_f"][:, :],
                                             start=True, stop=True),
                waits=[B.t_col, cp_toks[95]], inc=("p0gt", 1))
    t_gr = B.op("scalar", lambda e: e.copy(grow[:, :], mpT[0:32, 0:128]),
                waits=[t_gt])
    B.op("gpsimd", lambda e: e.dma_start(out=gflat[:, 0:2048], in_=grow[0:16, 0:128]),
         waits=[t_gr], inc=("p0gf", 16), dma=True)
    B.op("gpsimd", lambda e: e.dma_start(out=gflat[:, 2048:4096], in_=grow[16:32, 0:128]),
         waits=[t_gr], inc=("p0gf", 16), dma=True)
    bc_toks = {}
    n = 0
    for h in range(2):
        for ns in range(4):
            waits = [("p0gf", 32)]
            if n >= 1:
                waits.append(bc_toks[n - 1])
            t_b = B.op("tensor",
                       lambda e, h=h, ns=ns: e.matmul(
                           mpT[:, 0:512], B.s["ones1f"][:, :],
                           gflat[:, h * 2048 + ns * 512: h * 2048 + (ns + 1) * 512],
                           start=True, stop=True),
                       waits=waits, inc=("p0bc", 1))
            bc_toks[n] = B.op("vector",
                              lambda e, h=h, ns=ns: e.tensor_copy(
                                  gaB[:, h * 2048 + ns * 512: h * 2048 + (ns + 1) * 512],
                                  mpT[:, 0:512]),
                              waits=[t_b])
            n += 1
    B.t_gaB = bc_toks[7]


def phase_ln1(B):
    ustage = B.ar(A_USTAGE, 8192, F32)
    sqdump = B.ar(A_SQDUMP, 2048, BF16)
    xn = B.ar(A_XN, 8192, BF16)
    h_fm = B.ar(A_HFM, NKD * PTOK, BF16)
    B.h_fm = h_fm
    stat = B.sb("stat", [128, 16], F32)
    psTs = (B.bankb(3, 2), B.bankb(5, 2))

    uv = B.d["u_tok"].ap().rearrange("(n p) d -> n p d", p=128)
    modsel = B.s["modsel"]
    colbuf = B.s["colbuf"]

    NT = PTOK // 128
    xn_tok = {}
    ev_tok = {}
    grp_tr_tok = {}
    tr_cnt = 0
    tgroups = [list(range(g, min(g + 4, NT))) for g in range(0, NT, 4)]
    for gi, tg in enumerate(tgroups):
        for i in tg:
            slot = i % 2
            waits = []
            if i >= 2:
                waits.append(xn_tok[i - 2])
            B.op("sync", lambda e, i=i, slot=slot: e.dma_start(
                out=ustage[:, slot * DM:(slot + 1) * DM], in_=uv[i, :, :]),
                waits=waits, inc=("uld%d" % slot, 16), dma=True)
            ldw = [("uld%d" % slot, (i // 2 + 1) * 16)]
            us = ustage[:, slot * DM:(slot + 1) * DM]
            s0 = 8 * (i % 2)
            t_sum = B.op("vector", lambda e, us=us, s0=s0: e.reduce_sum(
                stat[:, s0:s0 + 1], us, AXX), waits=ldw)
            t_qq = B.op("scalar", lambda e, us=us, s0=s0: e.activation(
                sqdump[:, 0:2048], us, AF.Square, scale=float(1.0 / np.sqrt(DM)),
                accum_out=stat[:, s0 + 1:s0 + 2]), waits=ldw)
            B.op("vector", lambda e, s0=s0: e.tensor_mul(
                stat[:, s0 + 2:s0 + 3], stat[:, s0:s0 + 1], stat[:, s0:s0 + 1]))
            t_var = B.op("vector", lambda e, s0=s0: e.scalar_tensor_tensor(
                stat[:, s0 + 3:s0 + 4], stat[:, s0 + 2:s0 + 3], float(-1.0 / (DM * DM)),
                stat[:, s0 + 1:s0 + 2], OP.mult, OP.add), waits=[t_qq])
            t_m = B.op("vector", lambda e, s0=s0: e.tensor_scalar(
                stat[:, s0 + 4:s0 + 5], stat[:, s0:s0 + 1], float(1.0 / DM), None, OP.mult))
            B.op("scalar", lambda e, s0=s0: e.activation(
                stat[:, s0 + 3:s0 + 4], stat[:, s0 + 3:s0 + 4], AF.Ln,
                bias=B.s["cvals"][:, 0:1]), waits=[t_var])
            t_rstd = B.op("scalar", lambda e, s0=s0: e.activation(
                stat[:, s0 + 5:s0 + 6], stat[:, s0 + 3:s0 + 4], AF.Exp, scale=-0.5))
            xw = [t_rstd, t_m]
            if gi >= 1:
                xw.append(grp_tr_tok[gi - 1])
            xn_tok[i] = B.op("vector", lambda e, us=us, s0=s0, i=i: e.tensor_scalar(
                xn[:, (i % 4) * DM:(i % 4 + 1) * DM], us, stat[:, s0 + 4:s0 + 5],
                stat[:, s0 + 5:s0 + 6], OP.subtract, OP.mult), waits=xw)
        for dkg in range(4):
            pst = psTs[dkg % 2]
            gw = [xn_tok[i] for i in tg]
            key = (gi, dkg - 2) if dkg >= 2 else (gi - 1, 2 + dkg)
            if key in ev_tok:
                gw.append(ev_tok[key])
            for dj in range(4):
                dk = dkg * 4 + dj
                for ti, i in enumerate(tg):
                    B.op("tensor", lambda e, dk=dk, dj=dj, ti=ti, i=i, pst=pst:
                        e.transpose(pst[:, dj * 512 + ti * 128: dj * 512 + (ti + 1) * 128],
                                    xn[:, (i % 4) * DM + dk * 128:(i % 4) * DM + (dk + 1) * 128],
                                    B.s["ident_b"][:, :]),
                        waits=gw if (dj == 0 and ti == 0) else [], inc=("tr1", 1))
                    tr_cnt += 1
            lastw = [("tr1", tr_cnt)]
            nt = len(tg)
            for dj in range(4):
                dk = dkg * 4 + dj
                tok = B.op("scalar", lambda e, dk=dk, dj=dj, gi=gi, nt=nt, pst=pst:
                    e.activation(h_fm[:, dk * PTOK + gi * 512: dk * PTOK + gi * 512 + nt * 128],
                                 pst[:, dj * 512: dj * 512 + nt * 128],
                                 AF.Identity, bias=modsel[:, dk:dk + 1],
                                 scale=colbuf[:, dk:dk + 1]),
                    waits=lastw if dj == 0 else [])
            ev_tok[(gi, dkg)] = tok
        grp_tr_tok[gi] = ("tr1", tr_cnt)
    B.t_hfm = B.tok("oS")


def phase_inproj(B):
    B.din("wip", [22, 2048, 384], F32)
    B.din("initmask", [128, 1], F32)
    zs = B.ddbg("zs", [NKI, 128, 1024], BF16)

    imask = B.sb("imask", [128, 1], F32)
    B.op("sync", lambda e: e.dma_start(out=imask[:, :], in_=B.d["initmask"][:, :]),
         inc=("imld", 16), dma=True)

    B.din("gsrc", [TPC, 128, 128], BF16)
    B.din("lutb", [128, 16], F32)
    B.dtmp("wex_sh", [TPC, 128, 128], BF16)
    B.dtmp("wex_full", [N_TILES, 128, 128], BF16, addr_space="Shared")
    lutb = B.sb("lutb_s", [128, 16], F32)
    B.op("sync", lambda e: e.dma_start(out=lutb[:, :], in_=B.d["lutb"][:, :]),
         inc=("lutld", 16), dma=True)
    xgbf = lambda sl: B.ar(50944 + sl * 1024, 1024)
    xacc = lambda sl: B.ar(55040 + sl * 1024, 1024)
    xtj = B.ar(59136, 1024)
    gv = B.d["gsrc"].ap()
    wsh = B.d["wex_sh"].ap()
    ex_ev = {}
    B.ex_next = 0
    B.n_exu = (TPC + 5) // 6
    def emit_expansion_unit():
        c = B.ex_next
        t0 = c * 6
        nt = min(6, TPC - t0)
        if nt <= 0:
            return
        B.ex_next = c + 1
        sl = c % 2
        dw = [B.t_hfm] if c < 2 else [ex_ev[c - 2]]
        B.op("gpsimd", lambda e, t0=t0, nt=nt, sl=sl: e.dma_start(
            out=xgbf(sl)[:, 0:nt * 128].rearrange("p (a b) -> p a b", a=nt),
            in_=gv[t0:t0 + nt, :, :].transpose([1, 0, 2])),
            waits=dw, inc=("exd%d" % sl, 16), dma=True)
        mw = [("exd%d" % sl, 16 * (c // 2 + 1)), ("lutld", 16)]
        if c >= 2:
            mw.append(("exs%d" % sl, 16 * (c // 2)))
        B.op("vector", lambda e, nt=nt, sl=sl: e.tensor_scalar(
            xacc(sl)[:, 0:nt * 128], xgbf(sl)[:, 0:nt * 128], 0.0, lutb[:, 0:1],
            OP.is_equal, OP.mult), waits=mw)
        for j in range(1, 16):
            B.op("vector", lambda e, nt=nt, sl=sl, j=j: e.tensor_scalar(
                xtj[:, 0:nt * 128], xgbf(sl)[:, 0:nt * 128], float(j),
                lutb[:, j:j + 1], OP.is_equal, OP.mult))
            B.op("vector", lambda e, nt=nt, sl=sl: e.tensor_add(
                xacc(sl)[:, 0:nt * 128], xacc(sl)[:, 0:nt * 128],
                xtj[:, 0:nt * 128]))
        ex_ev[c] = B.tok("oV")
        B.op("gpsimd", lambda e, t0=t0, nt=nt, sl=sl: e.dma_start(
            out=wsh[t0:t0 + nt, :, :].transpose([1, 0, 2]),
            in_=xacc(sl)[:, 0:nt * 128].rearrange("p (a b) -> p a b", a=nt)),
            waits=[ex_ev[c]], inc=("exs%d" % sl, 16), dma=True)
    B.emit_expansion_unit = emit_expansion_unit

    xconv = B.ar(A_XCONV, 32864, BF16)
    B.xconv = xconv
    wbf = B.ar(A_WBF, 12288, BF16)
    zst = B.ar(A_ZST, 2048, BF16)
    BC = B.ar(A_BC, 4096, F32)
    dt_b = B.ar(A_DT, 2048, F32, 0, 64)
    cs_b = B.ar(A_CS, 2048, F32, 0, 64)
    B.BCv = BC
    B.dtv = dt_b
    B.csv = cs_b
    h_fm = B.h_fm

    wv = B.d["wip"].ap()
    t_p0done = ("p0mm", B.val("p0mm"))

    cast_tok = {}
    ev_tok = {}
    ctxev_tok = {}
    mm_after_group = {}
    mm_cnt = B.val("i1mm")
    for g in range(22):
        slot = g % 2
        for kt in range(16):
            j = 16 * g + kt
            sub = j % 4
            waits = []
            if j >= 4:
                waits.append(cast_tok[j - 4])
            B.op("sync", lambda e, g=g, kt=kt, sub=sub: e.dma_start(
                out=B.ar(A_W3 + sub * 768, 768, F32),
                in_=wv[g, kt * 128:(kt + 1) * 128, :]),
                waits=waits, inc=("w3_%d" % sub, 16), dma=True)
            cw = [("w3_%d" % sub, (j // 4 + 1) * 16)]
            if kt == 0 and g >= 2:
                cw.append(("i1mm", mm_after_group[g - 2]))
            cast_tok[j] = B.op(
                "scalar",
                lambda e, kt=kt, slot=slot, sub=sub: e.copy(
                    wbf[:, slot * 6144 + kt * 384: slot * 6144 + (kt + 1) * 384],
                    B.ar(A_W3 + sub * 768, 768, F32)),
                waits=cw)
        B.emit_expansion_unit()
        for mi in range(3):
            mc = 3 * g + mi
            for kt in range(16):
                for n in range(2):
                    bk = B.bank(mi * 2 + n)
                    waits = []
                    if kt == 0 and n == 0:
                        waits.append(cast_tok[16 * g + 15])
                        if g == 0 and mi == 0:
                            waits.append(B.t_hfm)
                            waits.append(t_p0done)
                            waits.append(B.t_gaB)
                            waits.append(("p0bc", 8))
                        if mc >= 3 and (mc - 3, 1) in ev_tok:
                            waits.append(ev_tok[(mc - 3, 0)])
                            waits.append(ev_tok[(mc - 3, 1)])
                            if (mc - 3) in ctxev_tok:
                                waits.append(ctxev_tok[mc - 3])
                    B.op("tensor", lambda e, mi=mi, kt=kt, n=n, slot=slot, bk=bk: e.matmul(
                        bk,
                        wbf[:, slot * 6144 + kt * 384 + mi * 128: slot * 6144 + kt * 384 + (mi + 1) * 128],
                        h_fm[:, kt * PTOK + REAL0 + n * 512: kt * PTOK + REAL0 + (n + 1) * 512],
                        start=(kt == 0), stop=(kt == 15)),
                        waits=waits, inc=("i1mm", 1))
                    mm_cnt += 1
                if 32 <= mc < 64:
                    cxw = []
                    if kt == 0 and (mc - 2) in ctxev_tok:
                        cxw.append(ctxev_tok[mc - 2])
                    B.op("tensor", lambda e, mi=mi, kt=kt, slot=slot, mc=mc: e.matmul(
                        B.bank(6 + mc % 2)[:, mi * 4: mi * 4 + 3],
                        wbf[:, slot * 6144 + kt * 384 + mi * 128: slot * 6144 + kt * 384 + (mi + 1) * 128],
                        h_fm[:, kt * PTOK + CTX0: kt * PTOK + CTX0 + 3],
                        start=(kt == 0), stop=(kt == 15)),
                        waits=cxw, inc=("i1mm", 1))
                    mm_cnt += 1
            mmtok = ("i1mm", mm_cnt)
            mm_after_group[g] = mm_cnt
            if mc < 32:
                for n in range(2):
                    zw = [mmtok]
                    if mc >= 2 and n == 0:
                        zw.append(("zdma%d" % (mc % 2), 16 * (mc // 2)))
                    ev_tok[(mc, n)] = B.op("scalar", lambda e, mc=mc, n=n, mi=mi: e.activation(
                        zst[:, (mc % 2) * 1024 + n * 512:(mc % 2) * 1024 + (n + 1) * 512],
                        B.bank(mi * 2 + n), AF.Silu),
                        waits=zw)
                B.op("gpsimd", lambda e, mc=mc: e.dma_start(
                    out=zs.ap()[mc, :, :], in_=zst[:, (mc % 2) * 1024:(mc % 2 + 1) * 1024]),
                    waits=[ev_tok[(mc, 1)]], inc=("zdma%d" % (mc % 2), 16), dma=True)
            elif mc < 64:
                xc = mc - 32
                for n in range(2):
                    ev_tok[(mc, n)] = B.op("scalar", lambda e, xc=xc, n=n, mi=mi: e.copy(
                        xconv[:, xc * 1027 + 3 + n * 512: xc * 1027 + 3 + (n + 1) * 512],
                        B.bank(mi * 2 + n)),
                        waits=[mmtok])
                ctxev_tok[mc] = B.op("scalar", lambda e, xc=xc, mi=mi, mc=mc: e.activation(
                    xconv[:, xc * 1027: xc * 1027 + 3],
                    B.bank(6 + mc % 2)[:, mi * 4: mi * 4 + 3],
                    AF.Identity, scale=imask[:, 0:1]),
                    waits=[mmtok, ("imld", 16)])
            elif mc == 64:
                for n in range(2):
                    bk = B.bank(mi * 2 + n)
                    t_min = B.op("vector", lambda e, bk=bk, n=n: e.tensor_scalar(
                        BC[:, n * 512:(n + 1) * 512], bk, 0.0, None, OP.min),
                        waits=[mmtok])
                    t_exp = B.op("scalar", lambda e, n=n: e.activation(
                        BC[:, n * 512:(n + 1) * 512], BC[:, n * 512:(n + 1) * 512], AF.Exp),
                        waits=[t_min])
                    t_rel = B.op("vector", lambda e, bk=bk, n=n: e.tensor_scalar(
                        BC[:, 1024 + n * 512:1024 + (n + 1) * 512], bk, 0.0, None, OP.max),
                        waits=[t_exp])
                    ev_tok[(mc, n)] = B.op("vector", lambda e, n=n: e.tensor_add(
                        BC[:, n * 512:(n + 1) * 512], BC[:, n * 512:(n + 1) * 512],
                        BC[:, 1024 + n * 512:1024 + (n + 1) * 512]))
                B.t_BC = B.tok("oV")
            else:
                for n in range(2):
                    bk = B.bank(mi * 2 + n)[0:64, :]
                    B.op("scalar", lambda e, bk=bk, n=n: e.activation(
                        dt_b[:, n * 512:(n + 1) * 512], bk, AF.Exp,
                        bias=B.s["hcols"][:, 1:2]),
                        waits=[mmtok, B.t_expA])
                    ev_tok[(mc, n)] = B.op("scalar", lambda e, n=n: e.activation(
                        dt_b[:, n * 512:(n + 1) * 512], dt_b[:, n * 512:(n + 1) * 512],
                        AF.Ln, bias=B.s["cvals"][0:64, 1:2]))
                t_la = B.op("vector", lambda e: e.tensor_scalar(
                    dt_b[:, 0:1024], dt_b[:, 0:1024], B.s["hcols"][:, 0:1], -1.0,
                    OP.mult, OP.mult), waits=[ev_tok[(mc, 1)]])
                B.t_cs = B.op("vector", lambda e: e.tensor_tensor_scan(
                    cs_b[:, 0:1024], B.ar(A_CSCAL, 2048, F32, 0, 64)[:, 0:1024],
                    dt_b[:, 0:1024], 0.0, OP.mult, OP.add),
                    waits=[t_la, B.c_cscal])
        B.emit_expansion_unit()
    while B.ex_next < B.n_exu:
        B.emit_expansion_unit()
    B.op("gpsimd",
         lambda e: e.collective_compute(
             "AllGather", OP.bypass, replica_groups=[list(range(8))],
             ins=[B.d["wex_sh"].ap().opt()], outs=[B.d["wex_full"].ap().opt()]),
         waits=[("exs0", 16 * ((B.n_exu + 1) // 2)), ("exs1", 16 * (B.n_exu // 2))],
         inc=("agw", 1), dma=True)
    B.t_xconv_all = B.tok("oS")
    B.t_ctx_all = B.tok("oS")



def phase_ssd(B):
    ys = B.ddbg("ys", [NC2, 128, 4096], BF16)
    B.dtmp("sfin_d", [128, 2048], F32)
    B.dtmp("sfin_ag", [256, 2048], F32)
    B.din("wcv", [128, 128], F32)
    B.din("wcb", [128, 32], F32)
    wc = B.sb("wc", [128, 160], F32)
    pairsel = B.sb("pairsel", [2, 128], F32)
    dbc = B.sb("dbc", [128, 64], F32)
    cmask01 = B.sb("cmask01", [128, 128], BF16)

    states = B.ar(S_STATES, 18432)
    states4 = states.rearrange("p (a b c) -> p a b c", a=32, c=9)
    ych = B.ar(S_YCH, 4096)
    def vct(c):
        return B.ar(S_VCT + c * 4096, 4096)
    attE = B.ar(S_ATTE, 8192)
    argb = B.ar(S_ARG, 4096, F32)
    ksc = B.ar(S_KSC, 4096)
    flat = B.ar(S_FLAT, 4096, F32, 0, 1)
    expcs = B.ar(S_EXPCS, 2048, F32, 0, 64)
    cs_t = B.ar(S_CST, 1024, F32)
    ekt = B.ar(S_EKT, 1024, F32)
    ektb = B.ar(S_EKTB, 512)
    kct = B.ar(S_KCT, 512)
    def vcb(sl):
        return B.ar(S_VCB + sl * 512, 512)
    svct = B.ar(S_SVCT, 4096)
    dex = B.ar(T_DEX, 64, F32, 0, 64)[:, 0:9]
    total = B.ar(T_TOT, 64, F32, 0, 64)[:, 0:8]
    attm = B.ar(T_ATTM, 128)
    totrow = B.ar(T_TOTR, 1152, F32, 0, 1)
    dexp = B.ar(T_DEXP, 64, F32, 0, 2)[:, 0:9]
    sfin = B.ar(S_SFIN, 4096, F32)
    sinitb = B.ar(S_SINITB, 2048)
    def sco(sl):
        return B.ar(S_SCO + sl * 576, 576, BF16)
    def qexp(sl):
        return B.ar(S_QEXP + sl * 1024, 1024, BF16)
    def yst(sl):
        return B.ar(S_YST2 + sl * 1024, 1024)
    p7flat = B.ar(S_FLAT, 4096, F32, 0, 1)
    cs_b = B.csv
    BC = B.BCv
    ysc = B.ar(S_YSC, 32768)
    B.ysc = ysc

    amm = lambda: ("amm", B.val("amm"))

    B.op("gpsimd", lambda e: e.memset(cmask01[:, :], 1.0))
    B.op("gpsimd", lambda e: e.affine_select(
        out=cmask01[:, :], in_=cmask01[:, :], compare_op=OP.is_ge,
        fill=0.0, base=0, pattern=[[1, S2]], channel_multiplier=-1))
    B.op("gpsimd", lambda e: e.memset(pairsel[:, :], 0.0))
    t_psel = B.op("gpsimd", lambda e: e.affine_select(
        out=pairsel[:, :].rearrange("p (a b) -> p a b", b=64),
        in_=pairsel[:, :].rearrange("p (a b) -> p a b", b=64),
        compare_op=OP.not_equal, fill=1.0, base=0,
        pattern=[[1, 2], [0, 64]], channel_multiplier=-1))
    B.op("sync", lambda e: e.dma_start(out=wc[:, 0:128], in_=B.d["wcv"][:, :]),
         inc=("cwld", 16), dma=True)
    B.op("sync", lambda e: e.dma_start(out=wc[:, 128:160], in_=B.d["wcb"][:, :]),
         inc=("cwld", 16), dma=True)

    tr2 = 0
    cvtr_tok = {}
    for hf in range(2):
        for xc in range(NKI):
            j = hf * NKI + xc
            sa = vcb((j % 2) * 2)
            sb_ = vcb((j % 2) * 2 + 1)
            xcv = B.xconv
            base = xc * 1027 + hf * 512
            w = [("cwld", 32), B.t_xconv_all, B.t_ctx_all]
            if j >= 2:
                w.append(cvtr_tok[j - 2])
            B.op("vector", lambda e, sa=sa, base=base, xc=xc: e.tensor_scalar(
                sa[:, :], xcv[:, base:base + 512], wc[:, 4 * xc:4 * xc + 1],
                wc[:, 128 + xc:129 + xc], OP.mult, OP.add), waits=w)
            cur, oth = sa, sb_
            for k in range(1, 4):
                B.op("vector", lambda e, cur=cur, oth=oth, base=base, xc=xc, k=k:
                     e.scalar_tensor_tensor(
                         oth[:, :], xcv[:, base + k:base + k + 512],
                         wc[:, 4 * xc + k:4 * xc + k + 1], cur[:, :], OP.mult, OP.add))
                cur, oth = oth, cur
            t_sil = B.op("scalar", lambda e, cur=cur: e.activation(cur[:, :], cur[:, :], AF.Silu),
                         waits=[B.tok("oV")])
            cb = 5 - (j % 2)
            for sub in range(4):
                tw = [t_sil]
                if j >= 2:
                    tw.append(cvtr_tok[j - 2])
                B.op("tensor", lambda e, cur=cur, sub=sub, cb=cb: e.transpose(
                    B.bankb(cb)[:, sub * 128: (sub + 1) * 128],
                    cur[:, sub * 128:(sub + 1) * 128], B.s["ident_b"][:, :]),
                    waits=tw if sub == 0 else [], inc=("tr2", 1))
                tr2 += 1
            for sub in range(4):
                cc = hf * 4 + sub
                cpw = [("tr2", tr2)] if sub == 0 else []
                if hf == 0 and xc == 0 and sub == 0:
                    cpw += [("zdma0", 16 * 16), ("zdma1", 16 * 16),
                            ("exs0", 16 * ((B.n_exu + 1) // 2)),
                            ("exs1", 16 * (B.n_exu // 2))]
                B.op("scalar", lambda e, cc=cc, xc=xc, sub=sub, cb=cb: e.copy(
                    vct(cc)[:, xc * 128:(xc + 1) * 128],
                    B.bankb(cb)[:, sub * 128: (sub + 1) * 128]),
                    waits=cpw)
            cvtr_tok[j] = B.tok("oS")
    t_conv = B.tok("oS")
    t_szero = B.op("gpsimd", lambda e: e.memset(states4[:, :, :, 0], 0.0),
                   waits=[t_conv])

    qc2 = B.BCv[0:64, 1024:2048]
    B.op("gpsimd", lambda e: e.dma_start(out=qc2, in_=B.BCv[64:128, 0:1024]),
         waits=[B.t_BC], inc=("qcd", 16), dma=True)
    t_expcs = B.op("scalar", lambda e: e.activation(expcs[:, 0:1024], cs_b[:, 0:1024], AF.Exp),
                   waits=[B.t_cs, t_conv])
    for tt in range(8):
        B.op("tensor", lambda e, tt=tt: e.transpose(
            B.bank(0)[:, tt * 64:(tt + 1) * 64],
            cs_b[:, tt * 128:(tt + 1) * 128], B.s["ident_f"][0:64, 0:64]),
            waits=[B.t_cs, ("i1mm", B.val("i1mm")), B.t_xconv_all, B.t_ctx_all]
                  if tt == 0 else [], inc=("amm", 1))
    t_cst = B.op("vector", lambda e: e.tensor_copy(cs_t[:, 0:512], B.bank(0)[:, 0:512]),
                 waits=[amm(), t_conv])
    t_tot = B.op("vector", lambda e: e.tensor_copy(
        total, cs_b[:, 0:1024].rearrange("p (c s) -> p c s", s=128)[:, :, 127]),
        waits=[B.t_cs])
    B.op("gpsimd", lambda e: e.memset(dex[:, 0:1], 0.0), waits=[B.t_cs])
    t_dex = B.op("scalar", lambda e: e.activation(dex[:, 1:9], total, AF.Exp),
                 waits=[t_tot])
    t_totr = B.op("gpsimd", lambda e: e.dma_start(out=totrow[:, 0:512],
                                                  in_=cs_t[127:128, 0:512]),
                  waits=[t_cst], inc=("totd", 16), dma=True)
    for tt in range(8):
        B.op("tensor", lambda e, tt=tt: e.matmul(
            B.bank(1)[:, tt * 64:(tt + 1) * 64], B.s["ones1f"][:, 0:128],
            totrow[:, tt * 64:(tt + 1) * 64], start=True, stop=True),
            waits=[("totd", 16), amm()] if tt == 0 else [], inc=("amm", 1))
    t_ek1 = B.op("vector", lambda e: e.tensor_tensor(
        ekt[:, 0:512], B.bank(1)[:, 0:512], cs_t[:, 0:512], OP.subtract),
        waits=[amm(), t_cst, t_conv])
    t_ek2 = B.op("scalar", lambda e: e.activation(ekt[:, 0:512], ekt[:, 0:512], AF.Exp),
                 waits=[t_ek1])
    t_ekb = B.op("vector", lambda e: e.tensor_copy(ektb[:, 0:512], ekt[:, 0:512]),
                 waits=[t_ek2])
    for tt in range(8):
        B.op("tensor", lambda e, tt=tt: e.transpose(
            B.bank(2)[:, tt * 64:(tt + 1) * 64],
            BC[0:64, tt * 128:(tt + 1) * 128], B.s["ident_f"][0:64, 0:64]),
            waits=[B.t_BC, amm(), t_ek1] if tt == 0 else [], inc=("amm", 1))
    t_kct = B.op("scalar", lambda e: e.copy(kct[:, 0:512], B.bank(2)[:, 0:512]),
                 waits=[amm(), t_conv])
    B.op("tensor", lambda e: e.transpose(B.bank(3)[0:4, 0:64],
                                         B.s["hcols"][:, 0:4], B.s["ident_f"][0:64, 0:64]),
         waits=[B.t_expA, amm()], inc=("amm", 1))
    hrows = B.sb("hrows", [4, 64], F32)
    t_dr0 = B.op("vector", lambda e: e.tensor_copy(hrows[:, :], B.bank(3)[0:4, 0:64]),
                 waits=[amm()])
    B.op("gpsimd", lambda e: e.dma_start(out=totrow[:, 512:576], in_=hrows[2:3, :]),
         waits=[t_dr0], inc=("drd", 16), dma=True)
    B.op("tensor", lambda e: e.matmul(B.bank(3)[:, 128:192], B.s["ones1f"][:, 0:128],
                                      totrow[:, 512:576], start=True, stop=True),
         waits=[("drd", 16)], inc=("amm", 1))
    t_dbc = B.op("vector", lambda e: e.tensor_copy(dbc[:, :], B.bank(3)[:, 128:192]),
                 waits=[amm()])

    ev_st = {}
    ev_y = {}
    for cc in range(NC2):
        t_svc = B.op("vector", lambda e, cc=cc: e.tensor_tensor(
            svct[:, :].rearrange("p (h q) -> p h q", q=64),
            vct(cc)[:, :].rearrange("p (h q) -> p h q", q=64),
            dbc[:, :].unsqueeze(2).broadcast_to([128, 64, 64]),
            OP.mult), waits=[t_conv, t_dbc] + ([amm()] if cc > 0 else []))
        B.op("tensor", lambda e, cc=cc: e.matmul(
            B.bank(0)[:, 0:128], BC[0:64, cc * 128:(cc + 1) * 128],
            qc2[:, cc * 128:(cc + 1) * 128], start=True, stop=True),
            waits=[B.t_BC, ("qcd", 16), amm()], inc=("amm", 1))
        t_attm = B.op("vector", lambda e: e.tensor_tensor(
            attm[:, :], B.bank(0)[:, 0:128], cmask01[:, :], OP.mult),
            waits=[amm(), t_psel])
        t_ksc = B.op("vector", lambda e, cc=cc: e.tensor_tensor(
            ksc[:, :].rearrange("p (h n) -> p h n", n=64),
            kct[:, cc * 64:(cc + 1) * 64].unsqueeze(1).broadcast_to([128, 64, 64]),
            ektb[:, cc * 64:(cc + 1) * 64].unsqueeze(2).broadcast_to([128, 64, 64]),
            OP.mult), waits=[t_ekb, t_kct] + ([amm()] if cc > 0 else []))
        arg_tok = {}
        t_eh_prev = None
        for qq in range(4):
            fw = [amm()]
            if cc == 0 and qq == 0:
                fw.append(("tr2", B.val("tr2")))
            B.op("gpsimd", lambda e, cc=cc, qq=qq: e.dma_start(
                out=flat[:, 0:2048],
                in_=cs_b[16 * qq:16 * (qq + 1), cc * 128:(cc + 1) * 128]),
                waits=fw, inc=("fld", 16), dma=True)
            nf = B.val("fld")
            for p in range(4):
                bk = B.bank(6 + (p % 2))
                bw = []
                if p == 0:
                    bw.append(("fld", nf))
                if (qq, p - 2) in arg_tok:
                    bw.append(arg_tok[(qq, p - 2)])
                elif qq > 0 and (qq - 1, p + 2) in arg_tok:
                    bw.append(arg_tok[(qq - 1, p + 2)])
                B.op("tensor", lambda e, p=p, bk=bk: e.matmul(
                    bk[:, 0:512], B.s["ones1f"][:, 0:128],
                    flat[:, p * 512:(p + 1) * 512], start=True, stop=True),
                    waits=bw, inc=("amm", 1))
                aw = [amm(), t_cst]
                if p == 0 and t_eh_prev is not None:
                    aw.append(t_eh_prev)
                arg_tok[(qq, p)] = B.op("vector", lambda e, cc=cc, qq=qq, p=p: e.tensor_tensor(
                    argb[:, p * 512:(p + 1) * 512].rearrange("p (h s) -> p h s", s=128),
                    B.bank(6 + (p % 2))[:, 0:512].rearrange("p (h s) -> p h s", s=128),
                    cs_t[:, cc * 64 + qq * 16 + p * 4: cc * 64 + qq * 16 + (p + 1) * 4]
                        .unsqueeze(2).broadcast_to([128, 4, 128]),
                    OP.subtract),
                    waits=aw)
            t_cl = B.op("vector", lambda e: e.tensor_scalar(
                argb[:, 0:2048], argb[:, 0:2048], 0.0, None, OP.min),
                waits=[arg_tok[(qq, 3)]])
            ew = [t_cl]
            if cc == 0 and qq == 0:
                ew.append(("tr2", B.val("tr2")))
            if cc > 0 or qq > 0:
                ew.append(amm())
            t_eh_prev = B.op("scalar", lambda e, qq=qq: e.activation(
                attE[:, qq * 2048:(qq + 1) * 2048], argb[:, 0:2048], AF.Exp),
                waits=ew)
            B.op("vector", lambda e, qq=qq: e.tensor_tensor(
                attE[:, qq * 2048:(qq + 1) * 2048].rearrange("p (h s) -> p h s", s=128),
                attE[:, qq * 2048:(qq + 1) * 2048].rearrange("p (h s) -> p h s", s=128),
                attm[:, :].unsqueeze(1).broadcast_to([128, 16, 128]),
                OP.mult), waits=[t_eh_prev, t_attm])
        t_attE = B.tok("oV")
        for pg in range(4):
            ybk = 1 + 2 * (pg % 2)
            sw = [t_ksc, t_attE, t_svc]
            key = (cc, pg - 2) if pg >= 2 else (cc - 1, pg + 2)
            if key in ev_y:
                sw.append(ev_y[key])
            stkey = (cc, pg - 1) if pg >= 1 else (cc - 1, 3)
            if stkey in ev_st:
                sw.append(ev_st[stkey])
            for pi in range(8):
                pair = pg * 8 + pi
                for hh in range(2):
                    h = pair * 2 + hh
                    tp = None if hh == 0 else (0, 64)
                    B.op("tensor", lambda e, h=h, pi=pi, hh=hh, cc=cc, tp=tp: e.matmul(
                        B.bank(5)[(hh * 64):(hh * 64 + 64), pi * 64:(pi + 1) * 64],
                        ksc[:, h * 64:(h + 1) * 64], vct(cc)[:, h * 64:(h + 1) * 64],
                        start=True, stop=True, tile_position=tp),
                        waits=sw if (pi == 0 and hh == 0) else [], inc=("amm", 1))
                    B.op("tensor", lambda e, h=h, pi=pi, hh=hh, cc=cc, tp=tp, ybk=ybk: e.matmul(
                        B.bank(ybk + pi // 4)[(hh * 64):(hh * 64 + 64),
                                              (pi % 4) * 128:(pi % 4 + 1) * 128],
                        vct(cc)[:, h * 64:(h + 1) * 64],
                        attE[:, h * 128:(h + 1) * 128],
                        start=True, stop=False, tile_position=tp),
                        inc=("amm", 1))
                    B.op("tensor", lambda e, h=h, pi=pi, hh=hh, tp=tp, ybk=ybk: e.matmul(
                        B.bank(ybk + pi // 4)[(hh * 64):(hh * 64 + 64),
                                              (pi % 4) * 128:(pi % 4 + 1) * 128],
                        svct[:, h * 64:(h + 1) * 64],
                        B.s["ident_b"][:, :],
                        start=False, stop=True, tile_position=tp),
                        inc=("amm", 1))
            ev_st[(cc, pg)] = B.op("scalar", lambda e, cc=cc, pg=pg: e.copy(
                states4[:, pg * 8:(pg + 1) * 8, :, cc + 1],
                B.bank(5)[:, 0:512].rearrange("p (a b) -> p a b", a=8)),
                waits=[amm(), t_szero])
            yw = [amm()]
            if cc > 0:
                yw.append(("ysd", 16 * cc))
            ev_y[(cc, pg)] = B.op("scalar", lambda e, pg=pg, ybk=ybk: e.copy(
                ych[:, pg * 1024:(pg + 1) * 1024], B.bank(ybk, 2)[:, 0:1024]),
                waits=yw)
        B.op("gpsimd", lambda e, cc=cc: e.dma_start(
            out=ys.ap()[cc, :, :], in_=ych[:, :]),
            waits=[ev_y[(cc, 3)]], inc=("ysd", 16), dma=True)
    t_states_done = B.tok("oS")

    scan_prev = [None]
    def emit_scan(pair, sl, extra_waits):
        B.op("gpsimd", lambda e, pair=pair: e.dma_start(
            out=dexp, in_=dex.tensor[0:0, 0:0] if False else
            B.ar(T_DEX, 64, F32, 2 * pair, 2 * pair + 2)[:, 0:9]),
            waits=[("amm", B.val("amm"))], inc=("dxd", 16), dma=True)
        nd = B.val("dxd")
        for p in range(2):
            B.op("tensor", lambda e, p=p: e.matmul(
                B.bank(6 + p)[:, 0:288], pairsel[:, :],
                dexp.unsqueeze(1).broadcast_to([2, 32, 9]),
                start=True, stop=True),
                waits=[("dxd", nd), t_psel, t_dex] +
                      ((extra_waits + ([scan_prev[0]] if scan_prev[0] else []))
                       if p == 0 else []),
                inc=("amm", 1))
        toks = []
        for p in range(2):
            toks.append(B.op("vector", lambda e, pair=pair, sl=sl, p=p: e.tensor_tensor_scan(
                sco(sl)[:, p * 288:(p + 1) * 288],
                B.bank(6 + p)[:, 0:288],
                states[:, pair * 576 + p * 288: pair * 576 + (p + 1) * 288],
                0.0, OP.mult, OP.add),
                waits=[("amm", B.val("amm"))]))
        scan_prev[0] = toks[-1]
        return toks[-1]

    for pair in range(32):
        t_sc = emit_scan(pair, pair % 2, [t_states_done] if pair == 0 else [])
        B.op("vector", lambda e, pair=pair: e.tensor_copy(
            sfin[:, pair * 64:(pair + 1) * 64],
            sco(pair % 2).rearrange("p (a b) -> p a b", b=9)[:, :, 8]),
            waits=[t_sc])
    t_sfin = B.tok("oV")
    B.op("gpsimd", lambda e: e.dma_start(out=B.d["sfin_d"][:, :], in_=sfin[:, 0:2048]),
         waits=[t_sfin], inc=("sfd", 16), dma=True)
    t_ag = B.op("gpsimd", lambda e: e.collective_compute(
        "AllGather", OP.bypass,
        replica_groups=[[0, 1], [2, 3], [4, 5], [6, 7]],
        ins=[B.d["sfin_d"].ap().opt()], outs=[B.d["sfin_ag"].ap().opt()]),
        waits=[("sfd", 16)], inc=("agp", 1), dma=True)
    B.op("gpsimd", lambda e: e.dma_start(out=sfin[:, 0:2048], in_=B.d["sfin_ag"][0:128, :]),
         waits=[t_ag], inc=("sgl", 16), dma=True)
    t_sin = B.op("vector", lambda e: e.tensor_scalar(
        sinitb[:, 0:2048], sfin[:, 0:2048], B.s["imask"][:, 0:1], None, OP.mult),
        waits=[("sgl", 16), ("imld", 16)])
    t_sl0 = B.op("vector", lambda e: e.tensor_copy(
        states4[:, :, :, 0],
        sinitb[:, 0:2048].rearrange("p (a b) -> p a b", b=64)),
        waits=[t_sin])

    asm_tok = {}
    prev_qtt = None
    for pair in range(32):
        t_scB = emit_scan(pair, pair % 2, [t_sl0] if pair == 0 else [])
        flat2 = B.ar(S_FLAT, 2048, F32, 0, 2)
        B.op("gpsimd", lambda e, pair=pair, flat2=flat2: e.dma_start(
            out=flat2, in_=expcs[2 * pair:2 * pair + 2, 0:1024]),
            waits=[("amm", B.val("amm")), t_expcs], inc=("p7f", 16), dma=True)
        qx = qexp(pair % 2)
        nf = B.val("p7f")
        t_qtt = None
        for p in range(2):
            bw = [("p7f", nf)] if p == 0 else []
            if t_qtt is not None:
                bw.append(t_qtt)
            elif pair > 0:
                bw.append(prev_qtt)
            B.op("tensor", lambda e, p=p, flat2=flat2: e.matmul(
                B.bank(0 if p == 0 else 5)[:, 0:512], pairsel[:, :],
                flat2[:, p * 512:(p + 1) * 512], start=True, stop=True),
                waits=bw, inc=("amm", 1))
            for hh in range(2):
                t_qtt = B.op("vector", lambda e, p=p, hh=hh, qx=qx: e.tensor_tensor(
                    qx[hh * 64:hh * 64 + 64, p * 512:(p + 1) * 512],
                    (qc2 if hh == 0 else BC[64:128, 0:1024])[:, p * 512:(p + 1) * 512]
                    if hh == 0 else BC[64:128, p * 512:(p + 1) * 512],
                    B.bank(0 if p == 0 else 5)[hh * 64:hh * 64 + 64, 0:512], OP.mult),
                    waits=[amm()] if hh == 0 else [])
        prev_qtt = t_qtt
        t_qx = B.tok("oV")
        scosl = sco(pair % 2).rearrange("p (a b) -> p a b", b=9)
        for hh in range(2):
            for c in range(NC2):
                tp = None if hh == 0 else (64, 64)
                ybk = 1 + 2 * (pair % 2)
                B.op("tensor", lambda e, hh=hh, c=c, tp=tp, ybk=ybk, scosl=scosl, qx=qx,
                     pair=pair: e.matmul(
                    B.bank(ybk + c // 4)[(hh * 64):(hh * 64 + 64),
                                         (c % 4) * 128:(c % 4 + 1) * 128],
                    scosl[hh * 64:hh * 64 + 64, :, c],
                    qx[hh * 64:hh * 64 + 64, c * 128:(c + 1) * 128],
                    start=True, stop=True, tile_position=tp),
                    waits=([t_scB, t_qx] + ([asm_tok[pair - 2]] if pair >= 2 else []))
                          if (hh == 0 and c == 0) else [],
                    inc=("amm", 1))
        gw = [("ysd", 16 * NC2)]
        if pair >= 2:
            gw.append(asm_tok[pair - 2])
        B.op("gpsimd", lambda e, pair=pair: e.dma_start(
            out=yst(pair % 2)[:, :],
            in_=B.d["ys"].ap().transpose([1, 0, 2])[:, :, pair * 128:(pair + 1) * 128]),
            waits=gw, inc=("ygd%d" % (pair % 2), 16), dma=True)
        asm_tok[pair] = B.op("vector", lambda e, pair=pair: e.tensor_add(
            ysc[:, pair * 1024:(pair + 1) * 1024],
            yst(pair % 2)[:, :],
            B.bank(1 + 2 * (pair % 2), 2)[:, 0:1024]),
            waits=[("ygd%d" % (pair % 2), 16 * (pair // 2 + 1)), ("amm", B.val("amm"))])
    B.t_ysc = B.tok("oV")


E_GST = 0
E_GBF = 8192
E_TJ = 10240
E_ACC = 12288
E_WEX = 14336
Y_ZSB = 22528
Y_WOP = 24576
Y_WOPB = 26624
Y_SQ = 27648
Y_OFM = 65664
Y_X2 = 16384
Y_U2 = 87040
L_XN2 = 87040
L_SQD = 93440
F_H2 = 0
F_HFF = 32896
F_WGU = 76928
F_WD = 85120
F_FMS = 91136
F_TMP = 95232
F_OUTS = 86144


def phase_tail(B):
    B.din("nsc", [128, 32], F32)
    B.din("wop", [32, 128, 2048], F32)
    B.dout("out_tok", [TOK, DM], F32)

    nsc = B.sb("nsc_s", [128, 32], F32)
    rowstat = B.ar(T_TOTR, 2048, F32, 0, 1)
    B.op("sync", lambda e: e.dma_start(out=nsc[:, :], in_=B.d["nsc"][:, :]),
         inc=("tld", 16), dma=True)

    ysc = B.ysc
    amm = lambda: ("amm", B.val("amm"))
    uv = B.d["u_tok"].ap().rearrange("(n p) d -> n p d", p=128)
    modsel = B.s["modsel"]
    colbuf = B.s["colbuf"]
    gaB = B.s["gaB"]

    ysq = lambda sl: B.ar(Y_SQ + sl * 1024, 1024)
    sq_tok = {}
    for pair in range(32):
        sw = [B.t_ysc]
        if pair >= 2:
            sw.append(("sqm", pair - 1))
            sw.append(("sqm0", pair - 1))
        sq_tok[pair] = B.op("scalar", lambda e, pair=pair: e.activation(
            ysq(pair % 2)[:, :], ysc[:, pair * 1024:(pair + 1) * 1024], AF.Square,
            scale=float(1.0)), waits=sw)
        for n in range(2):
            B.op("tensor", lambda e, pair=pair, n=n: e.matmul(
                B.bank(4 + n)[0:1, 0:512], B.s["onescol"][:, 0:1],
                ysq(pair % 2)[:, n * 512:(n + 1) * 512],
                start=(pair == 0), stop=(pair == 31)),
                waits=[sq_tok[pair]] if n == 0 else [],
                inc=("sqm", 1) if n == 1 else ("sqm0", 1))
    t_rms1 = B.op("scalar", lambda e: e.activation(
        rowstat[0:1, 0:512], B.bank(4)[0:1, 0:512], AF.Ln,
        scale=float(1.0 / DI), bias=B.s["cvals"][0:1, 2:3]),
        waits=[("sqm", 32), ("sqm0", 32)])
    B.op("scalar", lambda e: e.activation(
        rowstat[0:1, 512:1024], B.bank(5)[0:1, 0:512], AF.Ln,
        scale=float(1.0 / DI), bias=B.s["cvals"][0:1, 2:3]))
    t_rms2 = B.op("scalar", lambda e: e.activation(
        rowstat[0:1, 0:1024], rowstat[0:1, 0:1024], AF.Exp, scale=-0.5))
    if B.debug:
        B.dout("dbg_rstd", [1, 1024], F32)
        B.op("gpsimd", lambda e: e.dma_start(out=B.d["dbg_rstd"][:, :],
                                             in_=rowstat[0:1, 0:1024]),
             waits=[t_rms2], inc=("dbg", 16), dma=True)
    for n in range(2):
        B.op("tensor", lambda e, n=n: e.matmul(
            B.bank(6 + n)[:, 0:512], B.s["ones1f"][:, 0:128],
            rowstat[0:1, n * 512:(n + 1) * 512], start=True, stop=True),
            waits=[t_rms2, ("sqm", 32)], inc=("rstb", 1))
    zsb = lambda sl: B.ar(Y_ZSB + sl * 1024, 1024)
    ysc_tok = {}
    for pair in range(32):
        zw = []
        if pair == 0:
            zw += [B.t_ysc, ("zdma0", 256), ("zdma1", 256)]
        if pair >= 2:
            zw.append(ysc_tok[pair - 2])
        B.op("gpsimd", lambda e, pair=pair: e.dma_start(
            out=zsb(pair % 2)[:, :], in_=B.d["zs"].ap()[pair, :, :]),
            waits=zw, inc=("zrl%d" % (pair % 2), 16), dma=True)
        t_prez = B.op("vector", lambda e, pair=pair: e.scalar_tensor_tensor(
            ysc[:, pair * 1024:(pair + 1) * 1024],
            ysc[:, pair * 1024:(pair + 1) * 1024],
            nsc[:, pair:pair + 1],
            B.bank(6, 2)[:, 0:1024],
            OP.mult, OP.mult),
            waits=[("rstb", 2), ("tld", 16)])
        prez_w = []
        if B.debug and pair == 0:
            B.dout("dbg_zsb", [128, 1024], BF16)
            B.op("gpsimd", lambda e: e.dma_start(out=B.d["dbg_zsb"][:, :],
                                                 in_=zsb(0)[:, :]),
                 waits=[("zrl0", 16)], inc=("dbg", 16), dma=True)
        if B.debug and pair == 0:
            B.dout("dbg_prez", [128, 1024], BF16)
            B.op("gpsimd", lambda e: e.dma_start(out=B.d["dbg_prez"][:, :],
                                                 in_=ysc[:, 0:1024]),
                 waits=[t_prez], inc=("dbg", 16), dma=True)
            prez_w = [("dbg", B.val("dbg"))]
        ysc_tok[pair] = B.op("vector", lambda e, pair=pair: e.tensor_mul(
            ysc[:, pair * 1024:(pair + 1) * 1024],
            ysc[:, pair * 1024:(pair + 1) * 1024],
            zsb(pair % 2)[:, :]),
            waits=[("zrl%d" % (pair % 2), 16 * (pair // 2 + 1))] + prez_w)
    t_yscl = B.tok("oV")

    wopst = lambda sl: B.ar(Y_WOP + sl * 1024, 1024, F32)
    wopb = lambda sl: B.ar(Y_WOPB + sl * 512, 512)
    ofm = B.ar(Y_OFM, 16384)
    wv2 = B.d["wop"].ap()
    op_ev = {}
    cast_tok = {}
    amm_after_op = {}
    for mg in range(4):
        for kc in range(32):
            j = mg * 32 + kc
            sl = j % 2
            dw = []
            if j < 2:
                dw.append(B.t_ysc)
            if j >= 2:
                dw.append(cast_tok[j - 2])
            B.op("sync", lambda e, mg=mg, kc=kc, sl=sl: e.dma_start(
                out=wopst(sl)[:, 0:512],
                in_=wv2[kc, :, mg * 512:(mg + 1) * 512]),
                waits=dw, inc=("opw%d" % sl, 16), dma=True)
            cw2 = [("opw%d" % sl, 16 * (j // 2 + 1))]
            if j == 0:
                cw2.append(t_yscl)
            if j >= 2:
                cw2.append(("amm", amm_after_op.get(j - 2, 0)))
            cast_tok[j] = B.op("vector", lambda e, sl=sl: e.tensor_copy(
                wopb(sl)[:, 0:512], wopst(sl)[:, 0:512]),
                waits=cw2)
            for mi in range(4):
                for n in range(2):
                    mw = []
                    if mi == 0 and n == 0:
                        mw = [cast_tok[j], ("rstb", 2)]
                        if kc == 0:
                            mw.append(t_yscl)
                            if mg >= 1:
                                mw += [op_ev[(mg - 1, mi2, n2)] for mi2 in range(4)
                                       for n2 in range(2)]
                    B.op("tensor", lambda e, mi=mi, n=n, sl=sl, kc=kc: e.matmul(
                        B.bank(mi * 2 + n)[:, 0:512],
                        wopb(sl)[:, mi * 128:(mi + 1) * 128],
                        ysc[:, kc * 1024 + n * 512: kc * 1024 + (n + 1) * 512],
                        start=(kc == 0), stop=(kc == 31)),
                        waits=mw, inc=("amm", 1))
            amm_after_op[j] = B.val("amm")
        for mi in range(4):
            for n in range(2):
                ew2 = []
                if mi == 0 and n == 0:
                    ew2 = [amm()]
                op_ev[(mg, mi, n)] = B.op("scalar", lambda e, mg=mg, mi=mi, n=n: e.copy(
                    ofm[:, (mg * 4 + mi) * 1024 + n * 512:(mg * 4 + mi) * 1024 + (n + 1) * 512],
                    B.bank(mi * 2 + n)[:, 0:512]),
                    waits=ew2)
    if B.debug:
        B.dout("dbg_ofm", [128, 16384], BF16)
        B.op("gpsimd", lambda e: e.dma_start(out=B.d["dbg_ofm"][:, :], in_=ofm),
             waits=[B.tok("oS")], inc=("dbg", 16), dma=True)
        B.cnt["rtrHOLD"] = 0
    t_ofm = B.tok("oS")
    if B.debug:
        t_ofm_dbg = ("dbg", B.val("dbg"))
    else:
        t_ofm_dbg = None

    x2 = B.ar(Y_X2, 16384)
    u2 = lambda sl: B.ar(Y_U2 + sl * 4096, 4096, F32)
    restmp = lambda sl: B.ar(sl * 2048, 2048)
    rscr = B.ar(4096, 512)
    res_tok = {}
    rcp_tok = {}
    ntr = 0
    for tt in range(8):
        sl = tt % 2
        dw = [t_ofm] + ([t_ofm_dbg] if t_ofm_dbg else [])
        if tt >= 2:
            dw.append(res_tok[tt - 2])
        B.op("sync", lambda e, tt=tt, sl=sl: e.dma_start(
            out=u2(sl)[:, 0:2048], in_=uv[tt + 1, :, :]),
            waits=dw, inc=("url%d" % sl, 16), dma=True)
        for dkg in range(4):
            for q in range(4):
                dk = dkg * 4 + q
                tw = [t_ofm]
                if ntr >= 32:
                    tw.append(rcp_tok[tt - 2])
                B.op("tensor", lambda e, tt=tt, dk=dk, q=q, dkg=dkg: e.transpose(
                    B.bankb(dkg + 4 * (tt % 2))[:, q * 128:(q + 1) * 128],
                    ofm[:, dk * 1024 + tt * 128: dk * 1024 + (tt + 1) * 128],
                    B.s["ident_b"][:, :]),
                    waits=tw if (dkg == 0 and q == 0) else [], inc=("rtr", 1))
                ntr += 1
        for dkg in range(4):
            cw = []
            if dkg == 0:
                cw.append(("rtr", ntr))
                if tt >= 2:
                    cw.append(res_tok[tt - 2])
            B.op("scalar", lambda e, dkg=dkg, sl=sl: e.copy(
                restmp(sl)[:, dkg * 512:(dkg + 1) * 512],
                B.bankb(dkg + 4 * sl)[:, 0:512]), waits=cw)
        rcp_tok[tt] = B.tok("oS")
        for dkg in range(4):
            B.op("vector", lambda e, dkg=dkg, sl=sl: e.tensor_tensor(
                rscr[:, 0:512],
                restmp(sl)[:, dkg * 512:(dkg + 1) * 512],
                gaB[:, dkg * 512:(dkg + 1) * 512], OP.mult),
                waits=[rcp_tok[tt], ("url%d" % sl, 16 * (tt // 2 + 1))]
                      if dkg == 0 else [])
            rtk = B.op("vector", lambda e, tt=tt, dkg=dkg, sl=sl: e.scalar_tensor_tensor(
                x2[:, tt * 2048 + dkg * 512: tt * 2048 + (dkg + 1) * 512],
                u2(sl)[:, dkg * 512:(dkg + 1) * 512], 1.0,
                rscr[:, 0:512], OP.mult, OP.add))
        res_tok[tt] = rtk
    t_x2 = B.tok("oV")

    h2 = B.ar(F_H2, 16384)
    xn2 = lambda sl: B.ar(L_XN2 + sl * 2048, 2048)
    sqd = B.ar(L_SQD, 2048)
    stat = B.s["stat"]
    ln_tok = {}
    ev2_tok = {}
    ntr2 = 0
    for tt in range(8):
        s0 = 8 * (tt % 2)
        lw = [t_x2]
        t_sum = B.op("vector", lambda e, tt=tt, s0=s0: e.reduce_sum(
            stat[:, s0:s0 + 1], x2[:, tt * 2048:(tt + 1) * 2048], AXX), waits=lw)
        t_qq = B.op("scalar", lambda e, tt=tt, s0=s0: e.activation(
            sqd[:, :], x2[:, tt * 2048:(tt + 1) * 2048], AF.Square,
            scale=float(1.0 / np.sqrt(DM)), accum_out=stat[:, s0 + 1:s0 + 2]),
            waits=lw + ([("rtr", ntr)] if tt == 0 else []))
        B.op("vector", lambda e, s0=s0: e.tensor_mul(
            stat[:, s0 + 2:s0 + 3], stat[:, s0:s0 + 1], stat[:, s0:s0 + 1]))
        t_var = B.op("vector", lambda e, s0=s0: e.scalar_tensor_tensor(
            stat[:, s0 + 3:s0 + 4], stat[:, s0 + 2:s0 + 3], float(-1.0 / (DM * DM)),
            stat[:, s0 + 1:s0 + 2], OP.mult, OP.add), waits=[t_qq])
        t_m = B.op("vector", lambda e, s0=s0: e.tensor_scalar(
            stat[:, s0 + 4:s0 + 5], stat[:, s0:s0 + 1], float(1.0 / DM), None, OP.mult))
        B.op("scalar", lambda e, s0=s0: e.activation(
            stat[:, s0 + 3:s0 + 4], stat[:, s0 + 3:s0 + 4], AF.Ln,
            bias=B.s["cvals"][:, 0:1]), waits=[t_var])
        t_rstd = B.op("scalar", lambda e, s0=s0: e.activation(
            stat[:, s0 + 5:s0 + 6], stat[:, s0 + 3:s0 + 4], AF.Exp, scale=-0.5))
        xw = [t_rstd, t_m]
        if tt >= 2:
            xw.append(("ltr", ntr2))
        ln_tok[tt] = B.op("vector", lambda e, tt=tt, s0=s0: e.tensor_scalar(
            xn2((tt % 2))[:, :], x2[:, tt * 2048:(tt + 1) * 2048],
            stat[:, s0 + 4:s0 + 5], stat[:, s0 + 5:s0 + 6],
            OP.subtract, OP.mult), waits=xw)
        for dkg in range(4):
            for q in range(4):
                dk = dkg * 4 + q
                tw2 = [ln_tok[tt]]
                if (tt, dkg) not in ((0, 0), (0, 1)):
                    key = (tt, dkg - 2) if dkg >= 2 else (tt - 1, dkg + 2)
                    tw2.append(ev2_tok[key])
                B.op("tensor", lambda e, tt=tt, dk=dk, q=q, dkg=dkg: e.transpose(
                    B.bankb(4 + dkg % 2)[:, q * 128:(q + 1) * 128],
                    xn2((tt % 2))[:, dk * 128:(dk + 1) * 128],
                    B.s["ident_b"][:, :]),
                    waits=tw2 if q == 0 else [], inc=("ltr", 1))
                ntr2 += 1
            evw = [("ltr", ntr2)]
            for q in range(4):
                dk = dkg * 4 + q
                tok = B.op("scalar", lambda e, tt=tt, dk=dk, q=q, dkg=dkg: e.activation(
                    h2[:, dk * 1024 + tt * 128: dk * 1024 + (tt + 1) * 128],
                    B.bankb(4 + dkg % 2)[:, q * 128:(q + 1) * 128],
                    AF.Identity, bias=modsel[:, 48 + dk:49 + dk],
                    scale=colbuf[:, 16 + dk:17 + dk]),
                    waits=evw if q == 0 else [])
            ev2_tok[(tt, dkg)] = tok
    t_h2 = B.tok("oS")

    hff = B.ar(F_HFF, 44032)
    wgu = B.ar(F_WGU, 8192)
    tmpg = B.ar(F_TMP, 1024)
    wfv = B.d["wex_full"].ap()
    gu_ev = {}
    for mc in range(NMI):
        pb = (mc % 2) * 4
        dw = [("agw", 1), t_h2]
        if mc >= 1:
            dw.append(gu_ev[mc - 1])
        B.op("gpsimd", lambda e, mc=mc: e.dma_start(
            out=wgu[:, 0:2048].rearrange("p (a b) -> p a b", a=16),
            in_=wfv[mc:688:43, :, :].transpose([1, 0, 2])),
            waits=dw, inc=("fgw", 16), dma=True)
        B.op("gpsimd", lambda e, mc=mc: e.dma_start(
            out=wgu[:, 2048:4096].rearrange("p (a b) -> p a b", a=16),
            in_=wfv[688 + mc:1376:43, :, :].transpose([1, 0, 2])),
            inc=("fgw", 16), dma=True)
        nw = B.val("fgw")
        for kt in range(16):
            for n in range(2):
                for gup in range(2):
                    mw = [("fgw", nw)] if (kt == 0 and n == 0 and gup == 0) else []
                    if kt == 0 and n == 0 and gup == 0 and mc >= 2:
                        mw.append(gu_ev[mc - 2])
                    B.op("tensor", lambda e, kt=kt, n=n, gup=gup, pb=pb: e.matmul(
                        B.bank((pb + gup * 2 + n) % 8)[:, 0:512],
                        wgu[:, gup * 2048 + kt * 128: gup * 2048 + (kt + 1) * 128],
                        h2[:, kt * 1024 + n * 512: kt * 1024 + (n + 1) * 512],
                        start=(kt == 0), stop=(kt == 15)),
                        waits=mw, inc=("amm", 1))
        for n in range(2):
            B.op("scalar", lambda e, n=n, pb=pb: e.activation(
                tmpg[:, n * 512:(n + 1) * 512], B.bank((pb + n) % 8)[:, 0:512],
                AF.Silu), waits=[amm()] if n == 0 else [])
        t_sg = B.tok("oS")
        for n in range(2):
            tokf = B.op("vector", lambda e, mc=mc, n=n, pb=pb: e.tensor_mul(
                hff[:, mc * 1024 + n * 512: mc * 1024 + (n + 1) * 512],
                tmpg[:, n * 512:(n + 1) * 512],
                B.bank((pb + 2 + n) % 8)[:, 0:512]),
                waits=[t_sg, amm()] if n == 0 else [])
        gu_ev[mc] = tokf
    t_hff = B.tok("oV")

    wd = lambda sl: B.ar(F_WD + sl * 512, 512)
    fms = lambda sl: B.ar(F_FMS + sl * 2048, 2048)
    outs = lambda sl: B.ar(F_OUTS + sl * 1024, 1024, F32)
    ftmp = lambda sl: B.ar(96256 + sl * 512, 512)
    gbB = gaB[:, 2048:4096]
    dn_cast = {}
    amm_after_dn = {}
    fin_tok = {}
    nfr = 0
    for mg in range(4):
        for kc in range(NMI):
            j = mg * NMI + kc
            sl = j % 2
            dw = [t_hff] if j == 0 else []
            if j >= 2:
                dw.append(("amm", amm_after_dn.get(j - 2, 0)))
            B.op("gpsimd", lambda e, mg=mg, kc=kc, sl=sl: e.dma_start(
                out=wd(sl)[:, 0:512].rearrange("p (a b) -> p a b", a=4),
                in_=wfv[1376 + kc * 16 + mg * 4:1376 + kc * 16 + mg * 4 + 4, :, :]
                    .transpose([1, 0, 2])),
                waits=dw, inc=("fdw%d" % sl, 16), dma=True)
            dn_cast[j] = ("fdw%d" % sl, 16 * (j // 2 + 1))
            for mi in range(4):
                for n in range(2):
                    mw = []
                    if mi == 0 and n == 0:
                        mw = [dn_cast[j]]
                        if kc == 0:
                            mw.append(t_hff)
                            if mg >= 1:
                                mw.append(fin_tok[mg - 1])
                    B.op("tensor", lambda e, mi=mi, n=n, sl=sl, kc=kc: e.matmul(
                        B.bank(mi * 2 + n)[:, 0:512],
                        wd(sl)[:, mi * 128:(mi + 1) * 128],
                        hff[:, kc * 1024 + n * 512: kc * 1024 + (n + 1) * 512],
                        start=(kc == 0), stop=(kc == NMI - 1)),
                        waits=mw, inc=("amm", 1))
            amm_after_dn[j] = B.val("amm")
        for mi in range(4):
            for n in range(2):
                B.op("scalar", lambda e, mg=mg, mi=mi, n=n: e.copy(
                    fms(n)[:, mi * 512:(mi + 1) * 512],
                    B.bank(mi * 2 + n)[:, 0:512]),
                    waits=[amm()] if (mi == 0 and n == 0) else [])
        t_fms = B.tok("oS")
        for n in range(2):
            for q in range(4):
                tt = n * 4 + q
                bkp = 4 + (q % 2) * 2
                fsl = q % 2
                for mi in range(4):
                    tw = [t_fms]
                    key = ("bk", bkp)
                    if key in fin_tok:
                        tw.append(fin_tok[key])
                    B.op("tensor", lambda e, n=n, q=q, mi=mi, bkp=bkp: e.transpose(
                        B.bankb(bkp)[:, mi * 128:(mi + 1) * 128],
                        fms(n)[:, mi * 512 + q * 128: mi * 512 + (q + 1) * 128],
                        B.s["ident_b"][:, :]),
                        waits=tw if mi == 0 else [], inc=("ftr", 1))
                    nfr += 1
                cw = [("ftr", nfr)]
                if ("f", fsl) in fin_tok:
                    cw.append(fin_tok[("f", fsl)])
                t_cp = B.op("scalar", lambda e, bkp=bkp, fsl=fsl: e.copy(
                    ftmp(fsl)[:, 0:512], B.bankb(bkp)[:, 0:512]), waits=cw)
                fin_tok[("bk", bkp)] = t_cp
                t_mul = B.op("vector", lambda e, mg=mg, fsl=fsl: e.tensor_tensor(
                    B.ar(65664, 512)[:, 0:512],
                    ftmp(fsl)[:, 0:512],
                    gbB[:, mg * 512:(mg + 1) * 512], OP.mult),
                    waits=[t_cp])
                fin_tok[("f", fsl)] = t_mul
                osl = tt % 2
                ost_w = []
                if B.val("outd%d" % osl) >= 16:
                    ost_w.append(("outd%d" % osl, B.val("outd%d" % osl)))
                t_o = B.op("vector", lambda e, mg=mg, tt=tt: e.scalar_tensor_tensor(
                    outs(tt % 2)[:, 0:512],
                    x2[:, tt * 2048 + mg * 512: tt * 2048 + (mg + 1) * 512], 1.0,
                    B.ar(65664, 512)[:, 0:512], OP.mult, OP.add),
                    waits=ost_w)
                ow = [t_o]
                B.op("gpsimd", lambda e, mg=mg, tt=tt: e.dma_start(
                    out=B.d["out_tok"].ap()[tt * 128:(tt + 1) * 128,
                                            mg * 512:(mg + 1) * 512],
                    in_=outs(tt % 2)[:, 0:512]),
                    waits=ow, inc=("outd%d" % (tt % 2), 16), dma=True)
        fin_tok[mg] = B.tok("oV")
    B.t_out = ("outd0", B.val("outd0"))
    B.t_out2 = ("outd1", B.val("outd1"))


def prep_inputs(inp):
    u = np.asarray(inp["L0_attn_u"], np.float32)
    c = np.asarray(inp["L0_c"], np.float32)
    wmod = np.asarray(inp["L0_wmod"], np.float32)
    ipw = np.asarray(inp["L0_attn_in_proj_w"], np.float32)
    wip = np.zeros((22, 2048, 384), np.float32)
    for g in range(22):
        c0 = g * 384
        c1 = min(c0 + 384, DIP)
        if c1 > c0:
            wip[g, :, 0:c1 - c0] = ipw[:, c0:c1]
    cw = np.asarray(inp["L0_attn_conv_weight"], np.float32)
    cb = np.asarray(inp["L0_attn_conv_bias"], np.float32)
    wcv = np.zeros((128, 128), np.float32)
    for xc in range(32):
        for k in range(4):
            wcv[:, 4 * xc + k] = cw[k, 128 * xc:128 * (xc + 1)]
    wcb = np.ascontiguousarray(cb.reshape(32, 128).T)
    hvec = np.zeros((64, 4), np.float32)
    hvec[:, 0] = np.asarray(inp["L0_attn_A_log"], np.float32)
    hvec[:, 1] = np.asarray(inp["L0_attn_dt_bias"], np.float32)
    hvec[:, 2] = np.asarray(inp["L0_attn_D_param"], np.float32)
    nscale = np.asarray(inp["L0_attn_norm_scale"], np.float32)
    nsc = np.ascontiguousarray(nscale.reshape(32, 128).T)
    wop = np.ascontiguousarray(np.asarray(inp["L0_attn_out_proj_w"], np.float32)
                               .reshape(32, 128, 2048))
    lutb = np.ascontiguousarray(
        np.broadcast_to(np.asarray(inp["L0_ffn_lut"], np.float32)[None, :], (128, 16)))
    gidx = np.asarray(inp["L0_ffn_gate_idx"], np.int32)
    uidx = np.asarray(inp["L0_ffn_up_idx"], np.int32)
    didx = np.asarray(inp["L0_ffn_down_idx"], np.int32)
    tiles = []
    for t in range(N_TILES):
        if t < 688:
            kc, mc = t // 43, t % 43
            blk = gidx[mc * 128:(mc + 1) * 128, kc * 128:(kc + 1) * 128]
        elif t < 1376:
            tt_ = t - 688
            kc, mc = tt_ // 43, tt_ % 43
            blk = uidx[mc * 128:(mc + 1) * 128, kc * 128:(kc + 1) * 128]
        else:
            tt_ = t - 1376
            kc, mc = tt_ // 16, tt_ % 16
            blk = didx[mc * 128:(mc + 1) * 128, kc * 128:(kc + 1) * 128]
        tiles.append(blk)
    tiles = np.ascontiguousarray(np.stack(tiles))
    maps = []
    for core in range(8):
        b, half = core // 2, core % 2
        t0 = half * 1024
        ut = np.zeros((PTOK, DM), np.float32)
        if t0 >= 3:
            ut[CTX0:REAL0] = u[b, t0 - 3:t0]
        ut[REAL0:] = u[b, t0:t0 + 1024]
        ctb = np.ascontiguousarray(
            c.T[256 * core:256 * (core + 1)].reshape(2, 128, 4).transpose(1, 0, 2).reshape(128, 8))
        bselB = np.zeros((128, 4), np.float32)
        bselB[:, b] = 1.0
        maps.append({
            "ctb": ctb,
            "wmod_sh": np.ascontiguousarray(wmod[256 * core:256 * (core + 1)]),
            "bselB": bselB,
            "u_tok": ut,
            "wip": wip,
            "hvec": hvec,
            "initmask": np.full((128, 1), float(half), np.float32),
            "wcv": wcv,
            "wcb": wcb,
            "nsc": nsc,
            "wop": wop,
            "lutb": lutb,
            "gsrc": np.ascontiguousarray(
                tiles[TPC * core:TPC * (core + 1)].transpose(0, 2, 1)
            ).astype(ml_dtypes.bfloat16),
        })
    return maps


def build(debug=False, phases=9, modstop=99):
    B = Builder(debug)
    B.modstop = modstop
    B.din("u_tok", [PTOK, DM], F32)
    B.BIG = B.sb("BIG", [128, ACOLS], BF16)
    B.PS = B.nc.alloc_psum_tensor("PS", [128, 4096], F32)
    phase_const(B)
    if phases >= 1:
        phase_mod(B)
    if phases >= 2:
        phase_ln1(B)
    if phases >= 3:
        phase_inproj(B)
    if phases >= 4:
        phase_ssd(B)
    if phases >= 5:
        phase_tail(B)
    if phases < 5:
        for s in list(B.cnt):
            tok = (s, B.val(s))
            if tok[1] > 0 and s not in ("dbg",):
                B.raw("gpsimd", lambda e, tok=tok: e.wait_ge(B.sem(tok[0]), tok[1]))
        B.dout_dummy = B.dout("out_tok", [TOK, DM], F32) if "out_tok" not in B.d else None
        return B
    if debug:
        B.dout("dbg_ysc", [128, 32768], BF16)
        B.op("gpsimd", lambda e: e.dma_start(out=B.d["dbg_ysc"][:, :], in_=B.ysc),
             waits=[B.t_ysc], inc=("dbg", 16), dma=True)
        B.dout("dbg_x2", [128, 16384], BF16)
        B.op("gpsimd", lambda e: e.dma_start(out=B.d["dbg_x2"][:, :],
                                             in_=B.ar(Y_X2, 16384)),
             waits=[B.tok("oV")], inc=("dbg", 16), dma=True)
        B.dout("dbg_ysl", [128, 32768], BF16)
        B.op("gpsimd", lambda e: e.dma_start(out=B.d["dbg_ysl"][:, :], in_=B.ysc),
             waits=[B.tok("oV")], inc=("dbg", 16), dma=True)
        tok = B.t_out
        tok2 = B.t_out2
        B.raw("gpsimd", lambda e, tok=tok: e.wait_ge(B.sem(tok[0]), tok[1]))
        B.raw("gpsimd", lambda e, tok2=tok2: e.wait_ge(B.sem(tok2[0]), tok2[1]))
        B.raw("gpsimd", lambda e: e.wait_ge(B.sem("dbg"), B.val("dbg")))
    else:
        tok = B.t_out
        tok2 = B.t_out2
        B.raw("gpsimd", lambda e, tok=tok: e.wait_ge(B.sem(tok[0]), tok[1]))
        B.raw("gpsimd", lambda e, tok2=tok2: e.wait_ge(B.sem(tok2[0]), tok2[1]))
    return B


def kernel(**inputs):
    maps = prep_inputs(inputs)
    B = build(False)
    nc = B.finish()
    res = run_bass_kernel_spmd(nc, maps, core_ids=list(range(8)))
    out = np.zeros((4, 2048, 2048), np.float32)
    for core in range(8):
        b, half = core // 2, core % 2
        out[b, half * 1024:(half + 1) * 1024, :] = res.results[core]["out_tok"]
    return out


# version 08
# speedup vs baseline: 1.2366x; 1.1964x over previous
import contextlib
import ml_dtypes
import numpy as np

import concourse.bass as bass
import concourse.mybir as mybir
from concourse.bass_utils import run_bass_kernel_spmd

F32 = mybir.dt.float32
BF16 = mybir.dt.bfloat16
I32 = mybir.dt.int32
AF = mybir.ActivationFunctionType
OP = mybir.AluOpType
AXX = mybir.AxisListType.X

DM = 2048
DI = 4096
NH = 64
HD = 64
NST = 64
CH = 64
NCH = 16
DIP = 8384
DIPP = 8448
INTER = 5504
TOK = 1024
PTOK = 1152
CTX0 = 125
REAL0 = 128
NKD = 16
NKI = 32
NMIP = 66
NMI = 43
LN_EPS = 1e-5
RMS_EPS = 1e-6
N_TILES = 16 * NMI * 2 + NMI * 16
TPC = N_TILES // 8
ACOLS = 98304

ORD = {"vector": "oV", "scalar": "oS", "gpsimd": "oG"}


class Builder:
    def __init__(self, debug=False):
        self.debug = debug
        self.nc = bass.Bass("TRN2", target_bir_lowering=False)
        self.streams = {k: [] for k in ("tensor", "vector", "scalar", "gpsimd", "sync")}
        self.sems = {}
        self.cnt = {}
        self._stack = contextlib.ExitStack()
        self.d = {}
        self.s = {}

    def sem(self, name):
        if name not in self.sems:
            self.sems[name] = self._stack.enter_context(self.nc.semaphore(name))
            self.cnt[name] = 0
        return self.sems[name]

    def val(self, name):
        return self.cnt.get(name, 0)

    def tok(self, name):
        return (name, self.val(name))

    def din(self, name, shape, dtype):
        self.d[name] = self.nc.dram_tensor(name, shape, dtype, kind="ExternalInput")
        return self.d[name]

    def dout(self, name, shape, dtype):
        self.d[name] = self.nc.dram_tensor(name, shape, dtype, kind="ExternalOutput")
        return self.d[name]

    def dtmp(self, name, shape, dtype, **kw):
        self.d[name] = self.nc.dram_tensor(name, shape, dtype, **kw)
        return self.d[name]

    def ddbg(self, name, shape, dtype):
        return self.dout(name, shape, dtype) if self.debug else self.dtmp(name, shape, dtype)

    def sb(self, name, shape, dtype):
        self.s[name] = self.nc.alloc_sbuf_tensor(name, shape, dtype)
        return self.s[name]

    def ar(self, off, cols, dtype=BF16, p0=0, p1=128):
        v = self.BIG[p0:p1, off:off + cols]
        if dtype == BF16:
            return v
        return v.bitcast(dtype)

    def bank(self, i, j=1):
        return self.PS[:, 512 * i:512 * (i + j)]

    def bankb(self, i, j=1):
        return self.PS[:, 512 * i:512 * (i + j)].bitcast(BF16)

    def op(self, engine, fn, waits=(), inc=None, dma=False):
        waits = [(s, v) for s, v in waits if v > 0]
        oname = ORD.get(engine)
        upd = None
        if oname is not None:
            self.sem(oname)
            prev = self.val(oname)
            if prev > 0:
                waits.append((oname, prev))
            if not dma:
                assert inc is None
                self.cnt[oname] = prev + 1
                upd = (oname, 1)
        if upd is None and inc is not None:
            s, n = inc
            self.sem(s)
            self.cnt[s] += n
            upd = (s, n)
        wl = [(self.sem(s), v) for s, v in waits]
        il = None if upd is None else (self.sems[upd[0]], upd[1])

        def run(eng, fn=fn, wl=wl, il=il):
            for s, v in wl:
                eng.wait_ge(s, v)
            ins = fn(eng)
            if il is not None:
                ins.then_inc(il[0], il[1])
        self.streams[engine].append(run)
        if upd is not None:
            return (upd[0], self.val(upd[0]))
        return None

    def raw(self, engine, fn):
        self.streams[engine].append(fn)

    def finish(self):
        nc = self.nc
        with nc.Block() as block:
            for name in ("sync", "tensor", "vector", "scalar", "gpsimd"):
                def section(eng, name=name):
                    for f in self.streams[name]:
                        f(eng)
                getattr(block, name)(section)
        self._stack.close()
        return nc


A_WSTAGE = 0
A_MPIECE = 32768
A_MSTAGE = 36864
A_GFLAT = 40960
A_USTAGE = 49152
A_SQDUMP = 57344
A_XN = 59392
A_HFM = 67584
A_XCONV = 0
A_W3 = 33024
A_WBF = 36608
A_ZST = 48896
A_BC = 87040
A_DT = 91136
A_CS = 93184
A_CSCAL = 95232

S2 = 128
NC2 = 8
S_STATES = 0
S_YCH = 18432
S_EXPCS = 22528
S_CST = 24576
S_EKT = 25600
S_EKTB = 26624
S_KCT = 27136
S_FLAT = 27648
S_VCT = 32896
S_VCB = 65664
S_ATTE = 65664
S_ARG = 73856
S_KSC = 77952
S_SVCT = 82048
T_DEX = 91136
T_TOT = 91200
T_ATTM = 91264
T_TOTR = 91392
T_DEXP = 92544
S_SFIN = 65664
S_SINITB = 69760
S_SCO = 71808
S_QEXP = 73856
S_YST2 = 77952
S_YSC = 32896

def phase_const(B):
    ident_b = B.sb("ident_b", [128, 128], BF16)
    ident_f = B.sb("ident_f", [128, 128], F32)
    ones1 = B.sb("ones1", [1, 128], BF16)
    ones1f = B.sb("ones1f", [1, 128], F32)
    ones1b = B.sb("ones1b", [1, 128], BF16)
    cmask = B.sb("cmask", [64, 64], F32)
    cvals = B.sb("cvals", [128, 4], F32)

    for t in (ident_b, ident_f):
        B.op("gpsimd", lambda e, t=t: e.memset(t[:, :], 0.0))
        B.op("gpsimd", lambda e, t=t: e.affine_select(
            out=t[:, :], in_=t[:, :], compare_op=OP.not_equal,
            fill=1.0, base=0, pattern=[[-1, 128]], channel_multiplier=1))
    onescol = B.sb("onescol", [128, 1], BF16)
    B.op("gpsimd", lambda e: e.memset(onescol[:, :], 1.0))
    B.op("gpsimd", lambda e: e.memset(ones1[:, :], 1.0))
    B.op("gpsimd", lambda e: e.memset(ones1f[:, :], 1.0))
    B.op("gpsimd", lambda e: e.memset(ones1b[:, :], 1.0))
    B.op("gpsimd", lambda e: e.memset(cvals[:, 0:1], float(LN_EPS)))
    B.op("gpsimd", lambda e: e.memset(cvals[:, 1:2], 1.0))
    B.op("gpsimd", lambda e: e.memset(cvals[:, 2:3], float(RMS_EPS)))
    B.op("gpsimd", lambda e: e.memset(cmask[:, :], 0.0))
    B.c_const = B.op("gpsimd", lambda e: e.affine_select(
        out=cmask[:, :], in_=cmask[:, :], compare_op=OP.is_ge,
        fill=-1e30, base=0, pattern=[[1, 64]], channel_multiplier=-1))
    csc = B.ar(A_CSCAL, 2048, F32, 0, 64)
    B.op("gpsimd", lambda e, csc=csc: e.memset(csc[:, :], 1.0))
    B.c_cscal = B.op("gpsimd", lambda e, csc=csc: e.affine_select(
        out=csc[:, 0:1024].rearrange("p (c s) -> p c s", s=128),
        in_=csc[:, 0:1024].rearrange("p (c s) -> p c s", s=128),
        compare_op=OP.not_equal, fill=0.0, base=0,
        pattern=[[0, 8], [1, 128]], channel_multiplier=0))
    B.din("hvec", [64, 4], F32)
    hcols = B.sb("hcols", [64, 4], F32)
    t_ld = B.op("sync", lambda e: e.dma_start(out=hcols[:, :], in_=B.d["hvec"][:, :]),
                inc=("hvld", 16), dma=True)
    B.t_expA = B.op("scalar", lambda e: e.activation(hcols[:, 0:1], hcols[:, 0:1], AF.Exp),
                    waits=[t_ld])


def phase_mod(B):
    B.din("ctb", [128, 8], F32)
    B.din("wmod_sh", [256, 12288], F32)
    B.din("bselB", [128, 4], F32)
    B.dtmp("mod_part", [4, 12288], F32)
    B.dtmp("mod_full", [4, 12288], F32)

    silc = B.sb("silc", [128, 8], F32)
    bsel_sb = B.sb("bsel_sb", [128, 4], F32)
    modcols = B.sb("modcols", [128, 384], F32)
    modsel = B.sb("modsel", [128, 96], F32)
    colbuf = B.sb("colbuf", [128, 64], F32)
    tcol = B.sb("tcol", [128, 32], F32)
    grow = B.sb("grow", [32, 128], F32)
    gaB = B.sb("gaB", [128, 4096], BF16)

    wstage = B.ar(A_WSTAGE, 16384, BF16)
    mpiece = B.ar(A_MPIECE, 4096, F32)
    mstage = B.ar(A_MSTAGE, 4096, F32, 0, 4)
    gflat = B.ar(A_GFLAT, 8192, F32, 0, 1)
    mpT = B.bank(2)

    def mp4s(ns):
        return B.bank(ns % 2)[0:4, :]

    wv = B.d["wmod_sh"].ap().rearrange("(a p) n -> a p n", p=128)

    B.op("sync", lambda e: e.dma_start(out=silc[:, :], in_=B.d["ctb"][:, :]),
         inc=("p0ld", 16), dma=True)
    B.op("sync", lambda e: e.dma_start(out=bsel_sb[:, :], in_=B.d["bselB"][:, :]),
         inc=("p0ld", 16), dma=True)
    t_silu = B.op("scalar", lambda e: e.activation(silc[:, :], silc[:, :], AF.Silu),
                  waits=[("p0ld", 32)])
    if getattr(B, "modstop", 99) <= 1:
        return

    mm_toks = {}
    ev_toks = {}
    for cc in range(6):
        for ns in range(4):
            li = 4 * cc + ns
            for kt in range(2):
                g = 2 * cc + kt
                slot = g % 4
                waits = [("p0w%d" % slot, (g // 4 + 1) * 16)]
                if cc == 0 and ns == 0 and kt == 0:
                    waits.append(t_silu)
                if kt == 0 and li >= 2:
                    waits.append(ev_toks[li - 2])
                mm_toks[(g, ns)] = B.op(
                    "tensor",
                    lambda e, ns=ns, kt=kt, slot=slot: e.matmul(
                        mp4s(ns), silc[:, kt * 4:(kt + 1) * 4],
                        wstage[:, slot * 4096 + ns * 1024: slot * 4096 + (ns + 1) * 1024].bitcast(F32),
                        start=(kt == 0), stop=(kt == 1)),
                    waits=waits, inc=("p0mm", 1))
            evw = [mm_toks[(2 * cc + 1, ns)]]
            if cc >= 1:
                evw.append(("p0st", 16 * cc))
            ev_toks[li] = B.op(
                "scalar",
                lambda e, ns=ns: e.copy(mstage.bitcast(F32)[:, ns * 512:(ns + 1) * 512],
                                        mp4s(ns)),
                waits=evw)
        B.op("gpsimd",
             lambda e, cc=cc: e.dma_start(out=B.d["mod_part"][:, cc * 2048:(cc + 1) * 2048],
                                          in_=mstage.bitcast(F32)[:, :]),
             waits=[ev_toks[4 * cc + 3]], inc=("p0st", 16), dma=True)

    for cc in range(6):
        for kt in range(2):
            g = 2 * cc + kt
            slot = g % 4
            waits = []
            if g >= 4:
                waits.append(mm_toks[(g - 4, 3)])
            B.op("sync",
                 lambda e, kt=kt, cc=cc, slot=slot: e.dma_start(
                     out=wstage[:, slot * 4096:(slot + 1) * 4096].bitcast(F32),
                     in_=wv[kt, :, cc * 2048:(cc + 1) * 2048]),
                 waits=waits, inc=("p0w%d" % slot, 16), dma=True)

    if getattr(B, "modstop", 99) <= 2:
        return
    t_ar = B.op("gpsimd",
                lambda e: e.collective_compute(
                    "AllReduce", OP.add, replica_groups=[list(range(8))],
                    ins=[B.d["mod_part"].ap().opt()], outs=[B.d["mod_full"].ap().opt()]),
                waits=[("p0st", 96)], inc=("p0ar", 1), dma=True)
    if getattr(B, "modstop", 99) <= 3:
        return

    mpf = mpiece
    B.op("gpsimd", lambda e: e.memset(mpf[:, :], 0.0))
    tr_toks = {}
    cp_toks = {}
    ntr = 0
    for pc in range(6):
        waits = [t_ar]
        if pc:
            waits.append(tr_toks[ntr - 1])
        B.op("gpsimd",
             lambda e, pc=pc: e.dma_start(out=mpf[0:4, :],
                                          in_=B.d["mod_full"][:, pc * 2048:(pc + 1) * 2048]),
             waits=waits, inc=("p0mp", 16), dma=True)
        for j in range(16):
            twaits = [("p0mp", (pc + 1) * 16), B.c_const]
            if ntr >= 4:
                twaits.append(cp_toks[ntr - 4])
            tr_toks[ntr] = B.op(
                "tensor",
                lambda e, j=j, ntr=ntr: e.matmul(
                    B.bank(ntr % 4)[:, 0:128],
                    mpf[:, j * 128:(j + 1) * 128],
                    B.s["ident_f"][:, :],
                    start=True, stop=True),
                waits=twaits, inc=("p0tr", 1))
            cp_toks[ntr] = B.op(
                "scalar",
                lambda e, pc=pc, j=j, ntr=ntr: e.copy(
                    modcols[:, (pc * 16 + j) * 4:(pc * 16 + j) * 4 + 4],
                    B.bank(ntr % 4)[:, 0:4]),
                waits=[tr_toks[ntr]])
            ntr += 1

    if getattr(B, "modstop", 99) <= 4:
        return
    for b in range(4):
        src = modcols[:, :].rearrange("p (n c) -> p n c", c=4)[:, :, b]
        if b == 0:
            B.op("vector",
                 lambda e, src=src: e.tensor_scalar(modsel[:, :], src, bsel_sb[:, 0:1],
                                                    None, OP.mult),
                 waits=[cp_toks[95], ("p0ld", 32)])
        else:
            B.op("vector",
                 lambda e, src=src, b=b: e.scalar_tensor_tensor(
                     modsel[:, :], src, bsel_sb[:, b:b + 1], modsel[:, :], OP.mult, OP.add))

    B.op("vector", lambda e: e.tensor_scalar(colbuf[:, 0:16], modsel[:, 16:32], 1.0, None, OP.add))
    B.op("vector", lambda e: e.tensor_scalar(colbuf[:, 16:32], modsel[:, 64:80], 1.0, None, OP.add))
    for k, src0 in ((0, 32), (1, 80)):
        t = tcol[:, k * 16:(k + 1) * 16]
        dst0 = 32 + k * 16
        B.op("vector", lambda e, t=t, src0=src0: e.tensor_scalar(
            t, modsel[:, src0:src0 + 16], -1.0, None, OP.mult))
        B.op("vector", lambda e, t=t, src0=src0: e.tensor_tensor(
            t, modsel[:, src0:src0 + 16], t, OP.max))
        B.op("vector", lambda e, t=t: e.tensor_scalar(t, t, 0.25, None, OP.max))
        B.op("vector", lambda e, dst0=dst0, src0=src0: e.tensor_scalar(
            colbuf[:, dst0:dst0 + 16], modsel[:, src0:src0 + 16], 0.0, None, OP.is_ge))
    B.op("vector", lambda e: e.tensor_scalar(colbuf[:, 32:64], colbuf[:, 32:64],
                                             2.0, -1.0, OP.mult, OP.add))
    B.t_col = B.op("vector", lambda e: e.tensor_mul(colbuf[:, 32:64], colbuf[:, 32:64],
                                                    tcol[:, 0:32]))

    if getattr(B, "modstop", 99) <= 5:
        return
    t_gt = B.op("tensor", lambda e: e.matmul(mpT[0:32, 0:128], colbuf[:, 32:64],
                                             B.s["ident_f"][:, :],
                                             start=True, stop=True),
                waits=[B.t_col, cp_toks[95]], inc=("p0gt", 1))
    t_gr = B.op("scalar", lambda e: e.copy(grow[:, :], mpT[0:32, 0:128]),
                waits=[t_gt])
    B.op("gpsimd", lambda e: e.dma_start(out=gflat[:, 0:2048], in_=grow[0:16, 0:128]),
         waits=[t_gr], inc=("p0gf", 16), dma=True)
    B.op("gpsimd", lambda e: e.dma_start(out=gflat[:, 2048:4096], in_=grow[16:32, 0:128]),
         waits=[t_gr], inc=("p0gf", 16), dma=True)
    bc_toks = {}
    n = 0
    for h in range(2):
        for ns in range(4):
            waits = [("p0gf", 32)]
            if n >= 1:
                waits.append(bc_toks[n - 1])
            t_b = B.op("tensor",
                       lambda e, h=h, ns=ns: e.matmul(
                           mpT[:, 0:512], B.s["ones1f"][:, :],
                           gflat[:, h * 2048 + ns * 512: h * 2048 + (ns + 1) * 512],
                           start=True, stop=True),
                       waits=waits, inc=("p0bc", 1))
            bc_toks[n] = B.op("vector",
                              lambda e, h=h, ns=ns: e.tensor_copy(
                                  gaB[:, h * 2048 + ns * 512: h * 2048 + (ns + 1) * 512],
                                  mpT[:, 0:512]),
                              waits=[t_b])
            n += 1
    B.t_gaB = bc_toks[7]


def phase_ln1(B):
    ustage = B.ar(A_USTAGE, 8192, F32)
    sqdump = B.ar(A_SQDUMP, 2048, BF16)
    xn = B.ar(A_XN, 8192, BF16)
    h_fm = B.ar(A_HFM, NKD * PTOK, BF16)
    B.h_fm = h_fm
    stat = B.sb("stat", [128, 16], F32)
    psTs = (B.bankb(3, 2), B.bankb(5, 2))

    uv = B.d["u_tok"].ap().rearrange("(n p) d -> n p d", p=128)
    modsel = B.s["modsel"]
    colbuf = B.s["colbuf"]

    NT = PTOK // 128
    xn_tok = {}
    ev_tok = {}
    grp_tr_tok = {}
    tr_cnt = 0
    tgroups = [list(range(g, min(g + 4, NT))) for g in range(0, NT, 4)]
    for gi, tg in enumerate(tgroups):
        for i in tg:
            slot = i % 2
            waits = []
            if i >= 2:
                waits.append(xn_tok[i - 2])
            B.op("sync", lambda e, i=i, slot=slot: e.dma_start(
                out=ustage[:, slot * DM:(slot + 1) * DM], in_=uv[i, :, :]),
                waits=waits, inc=("uld%d" % slot, 16), dma=True)
            ldw = [("uld%d" % slot, (i // 2 + 1) * 16)]
            us = ustage[:, slot * DM:(slot + 1) * DM]
            s0 = 8 * (i % 2)
            t_sum = B.op("vector", lambda e, us=us, s0=s0: e.reduce_sum(
                stat[:, s0:s0 + 1], us, AXX), waits=ldw)
            t_qq = B.op("scalar", lambda e, us=us, s0=s0: e.activation(
                sqdump[:, 0:2048], us, AF.Square, scale=float(1.0 / np.sqrt(DM)),
                accum_out=stat[:, s0 + 1:s0 + 2]), waits=ldw)
            B.op("vector", lambda e, s0=s0: e.tensor_mul(
                stat[:, s0 + 2:s0 + 3], stat[:, s0:s0 + 1], stat[:, s0:s0 + 1]))
            t_var = B.op("vector", lambda e, s0=s0: e.scalar_tensor_tensor(
                stat[:, s0 + 3:s0 + 4], stat[:, s0 + 2:s0 + 3], float(-1.0 / (DM * DM)),
                stat[:, s0 + 1:s0 + 2], OP.mult, OP.add), waits=[t_qq])
            t_m = B.op("vector", lambda e, s0=s0: e.tensor_scalar(
                stat[:, s0 + 4:s0 + 5], stat[:, s0:s0 + 1], float(1.0 / DM), None, OP.mult))
            B.op("scalar", lambda e, s0=s0: e.activation(
                stat[:, s0 + 3:s0 + 4], stat[:, s0 + 3:s0 + 4], AF.Ln,
                bias=B.s["cvals"][:, 0:1]), waits=[t_var])
            t_rstd = B.op("scalar", lambda e, s0=s0: e.activation(
                stat[:, s0 + 5:s0 + 6], stat[:, s0 + 3:s0 + 4], AF.Exp, scale=-0.5))
            xw = [t_rstd, t_m]
            if gi >= 1:
                xw.append(grp_tr_tok[gi - 1])
            xn_tok[i] = B.op("vector", lambda e, us=us, s0=s0, i=i: e.tensor_scalar(
                xn[:, (i % 4) * DM:(i % 4 + 1) * DM], us, stat[:, s0 + 4:s0 + 5],
                stat[:, s0 + 5:s0 + 6], OP.subtract, OP.mult), waits=xw)
        for dkg in range(4):
            pst = psTs[dkg % 2]
            gw = [xn_tok[i] for i in tg]
            key = (gi, dkg - 2) if dkg >= 2 else (gi - 1, 2 + dkg)
            if key in ev_tok:
                gw.append(ev_tok[key])
            for dj in range(4):
                dk = dkg * 4 + dj
                for ti, i in enumerate(tg):
                    B.op("tensor", lambda e, dk=dk, dj=dj, ti=ti, i=i, pst=pst:
                        e.transpose(pst[:, dj * 512 + ti * 128: dj * 512 + (ti + 1) * 128],
                                    xn[:, (i % 4) * DM + dk * 128:(i % 4) * DM + (dk + 1) * 128],
                                    B.s["ident_b"][:, :]),
                        waits=gw if (dj == 0 and ti == 0) else [], inc=("tr1", 1))
                    tr_cnt += 1
            lastw = [("tr1", tr_cnt)]
            nt = len(tg)
            for dj in range(4):
                dk = dkg * 4 + dj
                tok = B.op("scalar", lambda e, dk=dk, dj=dj, gi=gi, nt=nt, pst=pst:
                    e.activation(h_fm[:, dk * PTOK + gi * 512: dk * PTOK + gi * 512 + nt * 128],
                                 pst[:, dj * 512: dj * 512 + nt * 128],
                                 AF.Identity, bias=modsel[:, dk:dk + 1],
                                 scale=colbuf[:, dk:dk + 1]),
                    waits=lastw if dj == 0 else [])
            ev_tok[(gi, dkg)] = tok
        grp_tr_tok[gi] = ("tr1", tr_cnt)
    B.t_hfm = B.tok("oS")


def phase_inproj(B):
    B.din("wip", [22, 2048, 384], F32)
    B.din("initmask", [128, 1], F32)
    zs = B.ddbg("zs", [NKI, 128, 1024], BF16)

    imask = B.sb("imask", [128, 1], F32)
    B.op("sync", lambda e: e.dma_start(out=imask[:, :], in_=B.d["initmask"][:, :]),
         inc=("imld", 16), dma=True)

    B.din("gsrc", [TPC, 128, 128], BF16)
    B.din("lutb", [128, 16], F32)
    B.dtmp("wex_sh", [TPC, 128, 128], BF16)
    B.dtmp("wex_full", [N_TILES, 128, 128], BF16, addr_space="Shared")
    lutb = B.sb("lutb_s", [128, 16], F32)
    B.op("sync", lambda e: e.dma_start(out=lutb[:, :], in_=B.d["lutb"][:, :]),
         inc=("lutld", 16), dma=True)
    xgbf = lambda sl: B.ar(50944 + sl * 1024, 1024)
    xacc = lambda sl: B.ar(55040 + sl * 1024, 1024)
    xtj = B.ar(59136, 1024)
    gv = B.d["gsrc"].ap()
    wsh = B.d["wex_sh"].ap()
    ex_ev = {}
    B.ex_next = 0
    B.n_exu = (TPC + 5) // 6
    def emit_expansion_unit():
        c = B.ex_next
        t0 = c * 6
        nt = min(6, TPC - t0)
        if nt <= 0:
            return
        B.ex_next = c + 1
        sl = c % 2
        dw = [B.t_hfm] if c < 2 else [ex_ev[c - 2]]
        B.op("gpsimd", lambda e, t0=t0, nt=nt, sl=sl: e.dma_start(
            out=xgbf(sl)[:, 0:nt * 128].rearrange("p (a b) -> p a b", a=nt),
            in_=gv[t0:t0 + nt, :, :].transpose([1, 0, 2])),
            waits=dw, inc=("exd%d" % sl, 16), dma=True)
        mw = [("exd%d" % sl, 16 * (c // 2 + 1)), ("lutld", 16)]
        if c >= 2:
            mw.append(("exs%d" % sl, 16 * (c // 2)))
        B.op("vector", lambda e, nt=nt, sl=sl: e.tensor_scalar(
            xacc(sl)[:, 0:nt * 128], xgbf(sl)[:, 0:nt * 128], 0.0, lutb[:, 0:1],
            OP.is_equal, OP.mult), waits=mw)
        for j in range(1, 16):
            B.op("vector", lambda e, nt=nt, sl=sl, j=j: e.tensor_scalar(
                xtj[:, 0:nt * 128], xgbf(sl)[:, 0:nt * 128], float(j),
                lutb[:, j:j + 1], OP.is_equal, OP.mult))
            B.op("vector", lambda e, nt=nt, sl=sl: e.tensor_add(
                xacc(sl)[:, 0:nt * 128], xacc(sl)[:, 0:nt * 128],
                xtj[:, 0:nt * 128]))
        ex_ev[c] = B.tok("oV")
        B.op("gpsimd", lambda e, t0=t0, nt=nt, sl=sl: e.dma_start(
            out=wsh[t0:t0 + nt, :, :].transpose([1, 0, 2]),
            in_=xacc(sl)[:, 0:nt * 128].rearrange("p (a b) -> p a b", a=nt)),
            waits=[ex_ev[c]], inc=("exs%d" % sl, 16), dma=True)
    B.emit_expansion_unit = emit_expansion_unit

    xconv = B.ar(A_XCONV, 32864, BF16)
    B.xconv = xconv
    wbf = B.ar(A_WBF, 12288, BF16)
    zst = B.ar(A_ZST, 2048, BF16)
    BC = B.ar(A_BC, 4096, F32)
    dt_b = B.ar(A_DT, 2048, F32, 0, 64)
    cs_b = B.ar(A_CS, 2048, F32, 0, 64)
    B.BCv = BC
    B.dtv = dt_b
    B.csv = cs_b
    h_fm = B.h_fm

    wv = B.d["wip"].ap()
    t_p0done = ("p0mm", B.val("p0mm"))

    cast_tok = {}
    ev_tok = {}
    ctxev_tok = {}
    mm_after_group = {}
    mm_cnt = B.val("i1mm")
    for g in range(22):
        slot = g % 2
        def emit_casts(gg):
            gslot = gg % 2
            for kt in range(16):
                j = 16 * gg + kt
                sub = j % 4
                waits = []
                if j >= 4:
                    waits.append(cast_tok[j - 4])
                B.op("sync", lambda e, gg=gg, kt=kt, sub=sub: e.dma_start(
                    out=B.ar(A_W3 + sub * 768, 768, F32),
                    in_=wv[gg, kt * 128:(kt + 1) * 128, :]),
                    waits=waits, inc=("w3_%d" % sub, 16), dma=True)
                cw = [("w3_%d" % sub, (j // 4 + 1) * 16)]
                if kt == 0 and gg >= 2:
                    cw.append(("i1mm", mm_after_group[gg - 2]))
                cast_tok[j] = B.op(
                    "scalar",
                    lambda e, kt=kt, gslot=gslot, sub=sub: e.copy(
                        wbf[:, gslot * 6144 + kt * 384: gslot * 6144 + (kt + 1) * 384],
                        B.ar(A_W3 + sub * 768, 768, F32)),
                    waits=cw)
        if g == 0:
            emit_casts(0)
        if g + 1 < 22:
            emit_casts(g + 1)
        B.emit_expansion_unit()
        for mi in range(3):
            mc = 3 * g + mi
            for kt in range(16):
                for n in range(2):
                    bk = B.bank(mi * 2 + n)
                    waits = []
                    if kt == 0 and n == 0:
                        waits.append(cast_tok[16 * g + 15])
                        if g == 0 and mi == 0:
                            waits.append(B.t_hfm)
                            waits.append(t_p0done)
                            waits.append(B.t_gaB)
                            waits.append(("p0bc", 8))
                        if mc >= 3 and (mc - 3, 1) in ev_tok:
                            waits.append(ev_tok[(mc - 3, 0)])
                            waits.append(ev_tok[(mc - 3, 1)])
                            if (mc - 3) in ctxev_tok:
                                waits.append(ctxev_tok[mc - 3])
                    B.op("tensor", lambda e, mi=mi, kt=kt, n=n, slot=slot, bk=bk: e.matmul(
                        bk,
                        wbf[:, slot * 6144 + kt * 384 + mi * 128: slot * 6144 + kt * 384 + (mi + 1) * 128],
                        h_fm[:, kt * PTOK + REAL0 + n * 512: kt * PTOK + REAL0 + (n + 1) * 512],
                        start=(kt == 0), stop=(kt == 15)),
                        waits=waits, inc=("i1mm", 1))
                    mm_cnt += 1
                if 32 <= mc < 64:
                    cxw = []
                    if kt == 0 and (mc - 2) in ctxev_tok:
                        cxw.append(ctxev_tok[mc - 2])
                    B.op("tensor", lambda e, mi=mi, kt=kt, slot=slot, mc=mc: e.matmul(
                        B.bank(6 + mc % 2)[:, mi * 4: mi * 4 + 3],
                        wbf[:, slot * 6144 + kt * 384 + mi * 128: slot * 6144 + kt * 384 + (mi + 1) * 128],
                        h_fm[:, kt * PTOK + CTX0: kt * PTOK + CTX0 + 3],
                        start=(kt == 0), stop=(kt == 15)),
                        waits=cxw, inc=("i1mm", 1))
                    mm_cnt += 1
            mmtok = ("i1mm", mm_cnt)
            mm_after_group[g] = mm_cnt
            if mc < 32:
                for n in range(2):
                    zw = [mmtok]
                    if mc >= 2 and n == 0:
                        zw.append(("zdma%d" % (mc % 2), 16 * (mc // 2)))
                    ev_tok[(mc, n)] = B.op("scalar", lambda e, mc=mc, n=n, mi=mi: e.activation(
                        zst[:, (mc % 2) * 1024 + n * 512:(mc % 2) * 1024 + (n + 1) * 512],
                        B.bank(mi * 2 + n), AF.Silu),
                        waits=zw)
                B.op("gpsimd", lambda e, mc=mc: e.dma_start(
                    out=zs.ap()[mc, :, :], in_=zst[:, (mc % 2) * 1024:(mc % 2 + 1) * 1024]),
                    waits=[ev_tok[(mc, 1)]], inc=("zdma%d" % (mc % 2), 16), dma=True)
            elif mc < 64:
                xc = mc - 32
                for n in range(2):
                    ev_tok[(mc, n)] = B.op("scalar", lambda e, xc=xc, n=n, mi=mi: e.copy(
                        xconv[:, xc * 1027 + 3 + n * 512: xc * 1027 + 3 + (n + 1) * 512],
                        B.bank(mi * 2 + n)),
                        waits=[mmtok])
                ctxev_tok[mc] = B.op("scalar", lambda e, xc=xc, mi=mi, mc=mc: e.activation(
                    xconv[:, xc * 1027: xc * 1027 + 3],
                    B.bank(6 + mc % 2)[:, mi * 4: mi * 4 + 3],
                    AF.Identity, scale=imask[:, 0:1]),
                    waits=[mmtok, ("imld", 16)])
            elif mc == 64:
                for n in range(2):
                    bk = B.bank(mi * 2 + n)
                    t_min = B.op("vector", lambda e, bk=bk, n=n: e.tensor_scalar(
                        BC[:, n * 512:(n + 1) * 512], bk, 0.0, None, OP.min),
                        waits=[mmtok])
                    t_exp = B.op("scalar", lambda e, n=n: e.activation(
                        BC[:, n * 512:(n + 1) * 512], BC[:, n * 512:(n + 1) * 512], AF.Exp),
                        waits=[t_min])
                    t_rel = B.op("vector", lambda e, bk=bk, n=n: e.tensor_scalar(
                        BC[:, 1024 + n * 512:1024 + (n + 1) * 512], bk, 0.0, None, OP.max),
                        waits=[t_exp])
                    ev_tok[(mc, n)] = B.op("vector", lambda e, n=n: e.tensor_add(
                        BC[:, n * 512:(n + 1) * 512], BC[:, n * 512:(n + 1) * 512],
                        BC[:, 1024 + n * 512:1024 + (n + 1) * 512]))
                B.t_BC = B.tok("oV")
            else:
                for n in range(2):
                    bk = B.bank(mi * 2 + n)[0:64, :]
                    B.op("scalar", lambda e, bk=bk, n=n: e.activation(
                        dt_b[:, n * 512:(n + 1) * 512], bk, AF.Exp,
                        bias=B.s["hcols"][:, 1:2]),
                        waits=[mmtok, B.t_expA])
                    ev_tok[(mc, n)] = B.op("scalar", lambda e, n=n: e.activation(
                        dt_b[:, n * 512:(n + 1) * 512], dt_b[:, n * 512:(n + 1) * 512],
                        AF.Ln, bias=B.s["cvals"][0:64, 1:2]))
                t_la = B.op("vector", lambda e: e.tensor_scalar(
                    dt_b[:, 0:1024], dt_b[:, 0:1024], B.s["hcols"][:, 0:1], -1.0,
                    OP.mult, OP.mult), waits=[ev_tok[(mc, 1)]])
                B.t_cs = B.op("vector", lambda e: e.tensor_tensor_scan(
                    cs_b[:, 0:1024], B.ar(A_CSCAL, 2048, F32, 0, 64)[:, 0:1024],
                    dt_b[:, 0:1024], 0.0, OP.mult, OP.add),
                    waits=[t_la, B.c_cscal])
        B.emit_expansion_unit()
    while B.ex_next < B.n_exu:
        B.emit_expansion_unit()
    B.op("gpsimd",
         lambda e: e.collective_compute(
             "AllGather", OP.bypass, replica_groups=[list(range(8))],
             ins=[B.d["wex_sh"].ap().opt()], outs=[B.d["wex_full"].ap().opt()]),
         waits=[("exs0", 16 * ((B.n_exu + 1) // 2)), ("exs1", 16 * (B.n_exu // 2))],
         inc=("agw", 1), dma=True)
    B.t_xconv_all = B.tok("oS")
    B.t_ctx_all = B.tok("oS")



def phase_ssd(B):
    ys = B.ddbg("ys", [NC2, 128, 4096], BF16)
    B.dtmp("sfin_d", [128, 2048], F32)
    B.dtmp("sfin_ag", [256, 2048], F32)
    B.din("wcv", [128, 128], F32)
    B.din("wcb", [128, 32], F32)
    wc = B.sb("wc", [128, 160], F32)
    pairsel = B.sb("pairsel", [2, 128], F32)
    dbc = B.sb("dbc", [128, 64], F32)
    cmask01 = B.sb("cmask01", [128, 128], BF16)

    states = B.ar(S_STATES, 18432)
    states4 = states.rearrange("p (a b c) -> p a b c", a=32, c=9)
    ych = B.ar(S_YCH, 4096)
    def vct(c):
        return B.ar(S_VCT + c * 4096, 4096)
    attE = B.ar(S_ATTE, 8192)
    argb = B.ar(S_ARG, 4096, F32)
    ksc = B.ar(S_KSC, 4096)
    flat = B.ar(S_FLAT, 2048, BF16, 0, 1)
    expcs = B.ar(S_EXPCS, 1024, BF16, 0, 64)
    csb_b = B.ar(97280, 1024, BF16, 0, 64)
    cs_t = B.ar(S_CST, 1024, F32)
    ekt = B.ar(S_EKT, 1024, F32)
    ektb = B.ar(S_EKTB, 512)
    kct = B.ar(S_KCT, 512)
    def vcb(sl):
        return B.ar(S_VCB + sl * 512, 512)
    svct = B.ar(S_SVCT, 4096)
    dex = B.ar(T_DEX, 64, BF16, 0, 64)[:, 0:9]
    total = B.ar(T_TOT, 64, F32, 0, 64)[:, 0:8]
    attm = B.ar(T_ATTM, 128)
    totrow = B.ar(T_TOTR, 1152, F32, 0, 1)
    dexp = B.ar(T_DEXP, 64, BF16, 0, 2)[:, 0:9]
    sfin = B.ar(S_SFIN, 4096, F32)
    sinitb = B.ar(S_SINITB, 2048)
    def sco(sl):
        return B.ar(S_SCO + sl * 576, 576, BF16)
    def qexp(sl):
        return B.ar(S_QEXP + sl * 1024, 1024, BF16)
    def yst(sl):
        return B.ar(S_YST2 + sl * 1024, 1024)
    p7flat = B.ar(S_FLAT, 4096, F32, 0, 1)
    cs_b = B.csv
    BC = B.BCv
    ysc = B.ar(S_YSC, 32768)
    B.ysc = ysc

    amm = lambda: ("amm", B.val("amm"))

    B.op("gpsimd", lambda e: e.memset(cmask01[:, :], 1.0))
    B.op("gpsimd", lambda e: e.affine_select(
        out=cmask01[:, :], in_=cmask01[:, :], compare_op=OP.is_ge,
        fill=0.0, base=0, pattern=[[1, S2]], channel_multiplier=-1))
    B.op("gpsimd", lambda e: e.memset(pairsel[:, :], 0.0))
    t_psel = B.op("gpsimd", lambda e: e.affine_select(
        out=pairsel[:, :].rearrange("p (a b) -> p a b", b=64),
        in_=pairsel[:, :].rearrange("p (a b) -> p a b", b=64),
        compare_op=OP.not_equal, fill=1.0, base=0,
        pattern=[[1, 2], [0, 64]], channel_multiplier=-1))
    pairsel_b = B.sb("pairsel_b", [2, 128], BF16)
    t_pselb = B.op("vector", lambda e: e.tensor_copy(pairsel_b[:, :], pairsel[:, :]),
                   waits=[t_psel])
    B.op("sync", lambda e: e.dma_start(out=wc[:, 0:128], in_=B.d["wcv"][:, :]),
         inc=("cwld", 16), dma=True)
    B.op("sync", lambda e: e.dma_start(out=wc[:, 128:160], in_=B.d["wcb"][:, :]),
         inc=("cwld", 16), dma=True)

    tr2 = 0
    cvtr_tok = {}
    for hf in range(2):
        for xc in range(NKI):
            j = hf * NKI + xc
            sa = vcb((j % 2) * 2)
            sb_ = vcb((j % 2) * 2 + 1)
            xcv = B.xconv
            base = xc * 1027 + hf * 512
            w = [("cwld", 32), B.t_xconv_all, B.t_ctx_all]
            if j >= 2:
                w.append(cvtr_tok[j - 2])
            B.op("vector", lambda e, sa=sa, base=base, xc=xc: e.tensor_scalar(
                sa[:, :], xcv[:, base:base + 512], wc[:, 4 * xc:4 * xc + 1],
                wc[:, 128 + xc:129 + xc], OP.mult, OP.add), waits=w)
            cur, oth = sa, sb_
            for k in range(1, 4):
                B.op("vector", lambda e, cur=cur, oth=oth, base=base, xc=xc, k=k:
                     e.scalar_tensor_tensor(
                         oth[:, :], xcv[:, base + k:base + k + 512],
                         wc[:, 4 * xc + k:4 * xc + k + 1], cur[:, :], OP.mult, OP.add))
                cur, oth = oth, cur
            t_sil = B.op("scalar", lambda e, cur=cur: e.activation(cur[:, :], cur[:, :], AF.Silu),
                         waits=[B.tok("oV")])
            cb = 5 - (j % 2)
            for sub in range(4):
                tw = [t_sil]
                if j >= 2:
                    tw.append(cvtr_tok[j - 2])
                B.op("tensor", lambda e, cur=cur, sub=sub, cb=cb: e.transpose(
                    B.bankb(cb)[:, sub * 128: (sub + 1) * 128],
                    cur[:, sub * 128:(sub + 1) * 128], B.s["ident_b"][:, :]),
                    waits=tw if sub == 0 else [], inc=("tr2", 1))
                tr2 += 1
            for sub in range(4):
                cc = hf * 4 + sub
                cpw = [("tr2", tr2)] if sub == 0 else []
                if hf == 0 and xc == 0 and sub == 0:
                    cpw += [("zdma0", 16 * 16), ("zdma1", 16 * 16),
                            ("exs0", 16 * ((B.n_exu + 1) // 2)),
                            ("exs1", 16 * (B.n_exu // 2))]
                B.op("scalar", lambda e, cc=cc, xc=xc, sub=sub, cb=cb: e.copy(
                    vct(cc)[:, xc * 128:(xc + 1) * 128],
                    B.bankb(cb)[:, sub * 128: (sub + 1) * 128]),
                    waits=cpw)
            cvtr_tok[j] = B.tok("oS")
    t_conv = B.tok("oS")
    t_szero = B.op("gpsimd", lambda e: e.memset(states4[:, :, :, 0], 0.0),
                   waits=[t_conv])

    qc2 = B.BCv[0:64, 1024:2048]
    B.op("gpsimd", lambda e: e.dma_start(out=qc2, in_=B.BCv[64:128, 0:1024]),
         waits=[B.t_BC], inc=("qcd", 16), dma=True)
    t_expcs = B.op("scalar", lambda e: e.activation(expcs[:, 0:1024], cs_b[:, 0:1024], AF.Exp),
                   waits=[B.t_cs, t_conv])
    B.t_csb = B.op("vector", lambda e: e.tensor_copy(csb_b[:, 0:1024], cs_b[:, 0:1024]),
                   waits=[B.t_cs])
    for tt in range(8):
        B.op("tensor", lambda e, tt=tt: e.transpose(
            B.bank(0)[:, tt * 64:(tt + 1) * 64],
            cs_b[:, tt * 128:(tt + 1) * 128], B.s["ident_f"][0:64, 0:64]),
            waits=[B.t_cs, ("i1mm", B.val("i1mm")), B.t_xconv_all, B.t_ctx_all]
                  if tt == 0 else [], inc=("amm", 1))
    t_cst = B.op("vector", lambda e: e.tensor_copy(cs_t[:, 0:512], B.bank(0)[:, 0:512]),
                 waits=[amm(), t_conv])
    t_tot = B.op("vector", lambda e: e.tensor_copy(
        total, cs_b[:, 0:1024].rearrange("p (c s) -> p c s", s=128)[:, :, 127]),
        waits=[B.t_cs])
    B.op("gpsimd", lambda e: e.memset(dex[:, 0:1], 0.0), waits=[B.t_cs])
    t_dex = B.op("scalar", lambda e: e.activation(dex[:, 1:9], total, AF.Exp),
                 waits=[t_tot])
    t_totr = B.op("gpsimd", lambda e: e.dma_start(out=totrow[:, 0:512],
                                                  in_=cs_t[127:128, 0:512]),
                  waits=[t_cst], inc=("totd", 16), dma=True)
    for tt in range(8):
        B.op("tensor", lambda e, tt=tt: e.matmul(
            B.bank(1)[:, tt * 64:(tt + 1) * 64], B.s["ones1f"][:, 0:128],
            totrow[:, tt * 64:(tt + 1) * 64], start=True, stop=True),
            waits=[("totd", 16), amm()] if tt == 0 else [], inc=("amm", 1))
    t_ek1 = B.op("vector", lambda e: e.tensor_tensor(
        ekt[:, 0:512], B.bank(1)[:, 0:512], cs_t[:, 0:512], OP.subtract),
        waits=[amm(), t_cst, t_conv])
    t_ek2 = B.op("scalar", lambda e: e.activation(ekt[:, 0:512], ekt[:, 0:512], AF.Exp),
                 waits=[t_ek1])
    t_ekb = B.op("vector", lambda e: e.tensor_copy(ektb[:, 0:512], ekt[:, 0:512]),
                 waits=[t_ek2])
    for tt in range(8):
        B.op("tensor", lambda e, tt=tt: e.transpose(
            B.bank(2)[:, tt * 64:(tt + 1) * 64],
            BC[0:64, tt * 128:(tt + 1) * 128], B.s["ident_f"][0:64, 0:64]),
            waits=[B.t_BC, amm(), t_ek1] if tt == 0 else [], inc=("amm", 1))
    t_kct = B.op("scalar", lambda e: e.copy(kct[:, 0:512], B.bank(2)[:, 0:512]),
                 waits=[amm(), t_conv])
    B.op("tensor", lambda e: e.transpose(B.bank(3)[0:4, 0:64],
                                         B.s["hcols"][:, 0:4], B.s["ident_f"][0:64, 0:64]),
         waits=[B.t_expA, amm()], inc=("amm", 1))
    hrows = B.sb("hrows", [4, 64], F32)
    t_dr0 = B.op("vector", lambda e: e.tensor_copy(hrows[:, :], B.bank(3)[0:4, 0:64]),
                 waits=[amm()])
    B.op("gpsimd", lambda e: e.dma_start(out=totrow[:, 512:576], in_=hrows[2:3, :]),
         waits=[t_dr0], inc=("drd", 16), dma=True)
    B.op("tensor", lambda e: e.matmul(B.bank(3)[:, 128:192], B.s["ones1f"][:, 0:128],
                                      totrow[:, 512:576], start=True, stop=True),
         waits=[("drd", 16)], inc=("amm", 1))
    t_dbc = B.op("vector", lambda e: e.tensor_copy(dbc[:, :], B.bank(3)[:, 128:192]),
                 waits=[amm()])

    ev_st = {}
    ev_y = {}
    for cc in range(NC2):
        t_svc = B.op("vector", lambda e, cc=cc: e.tensor_tensor(
            svct[:, :].rearrange("p (h q) -> p h q", q=64),
            vct(cc)[:, :].rearrange("p (h q) -> p h q", q=64),
            dbc[:, :].unsqueeze(2).broadcast_to([128, 64, 64]),
            OP.mult), waits=[t_conv, t_dbc] + ([amm()] if cc > 0 else []))
        B.op("tensor", lambda e, cc=cc: e.matmul(
            B.bank(0)[:, 0:128], BC[0:64, cc * 128:(cc + 1) * 128],
            qc2[:, cc * 128:(cc + 1) * 128], start=True, stop=True),
            waits=[B.t_BC, ("qcd", 16), amm()], inc=("amm", 1))
        t_attm = B.op("vector", lambda e: e.tensor_tensor(
            attm[:, :], B.bank(0)[:, 0:128], cmask01[:, :], OP.mult),
            waits=[amm(), t_psel])
        t_ksc = B.op("vector", lambda e, cc=cc: e.tensor_tensor(
            ksc[:, :].rearrange("p (h n) -> p h n", n=64),
            kct[:, cc * 64:(cc + 1) * 64].unsqueeze(1).broadcast_to([128, 64, 64]),
            ektb[:, cc * 64:(cc + 1) * 64].unsqueeze(2).broadcast_to([128, 64, 64]),
            OP.mult), waits=[t_ekb, t_kct] + ([amm()] if cc > 0 else []))
        arg_tok = {}
        t_eh_prev = None
        for qq in range(4):
            fw = [amm()]
            if cc == 0 and qq == 0:
                fw.append(("tr2", B.val("tr2")))
            if cc == 0 and qq == 0:
                fw.append(B.t_csb)
            B.op("gpsimd", lambda e, cc=cc, qq=qq: e.dma_start(
                out=flat[:, 0:2048],
                in_=csb_b[16 * qq:16 * (qq + 1), cc * 128:(cc + 1) * 128]),
                waits=fw, inc=("fld", 16), dma=True)
            nf = B.val("fld")
            for p in range(4):
                bk = B.bank(6 + (p % 2))
                bw = []
                if p == 0:
                    bw.append(("fld", nf))
                if (qq, p - 2) in arg_tok:
                    bw.append(arg_tok[(qq, p - 2)])
                elif qq > 0 and (qq - 1, p + 2) in arg_tok:
                    bw.append(arg_tok[(qq - 1, p + 2)])
                B.op("tensor", lambda e, p=p, bk=bk: e.matmul(
                    bk[:, 0:512], B.s["ones1b"][:, 0:128],
                    flat[:, p * 512:(p + 1) * 512], start=True, stop=True),
                    waits=bw, inc=("amm", 1))
                aw = [amm(), t_cst]
                if p == 0 and t_eh_prev is not None:
                    aw.append(t_eh_prev)
                arg_tok[(qq, p)] = B.op("vector", lambda e, cc=cc, qq=qq, p=p: e.tensor_tensor(
                    argb[:, p * 512:(p + 1) * 512].rearrange("p (h s) -> p h s", s=128),
                    B.bank(6 + (p % 2))[:, 0:512].rearrange("p (h s) -> p h s", s=128),
                    cs_t[:, cc * 64 + qq * 16 + p * 4: cc * 64 + qq * 16 + (p + 1) * 4]
                        .unsqueeze(2).broadcast_to([128, 4, 128]),
                    OP.subtract),
                    waits=aw)
            t_cl = B.op("vector", lambda e: e.tensor_scalar(
                argb[:, 0:2048], argb[:, 0:2048], 0.0, None, OP.min),
                waits=[arg_tok[(qq, 3)]])
            ew = [t_cl]
            if cc == 0 and qq == 0:
                ew.append(("tr2", B.val("tr2")))
            if cc > 0 or qq > 0:
                ew.append(amm())
            t_eh_prev = B.op("scalar", lambda e, qq=qq: e.activation(
                attE[:, qq * 2048:(qq + 1) * 2048], argb[:, 0:2048], AF.Exp),
                waits=ew)
            B.op("vector", lambda e, qq=qq: e.tensor_tensor(
                attE[:, qq * 2048:(qq + 1) * 2048].rearrange("p (h s) -> p h s", s=128),
                attE[:, qq * 2048:(qq + 1) * 2048].rearrange("p (h s) -> p h s", s=128),
                attm[:, :].unsqueeze(1).broadcast_to([128, 16, 128]),
                OP.mult), waits=[t_eh_prev, t_attm])
        t_attE = B.tok("oV")
        for pg in range(4):
            ybk = 1 + 2 * (pg % 2)
            sw = [t_ksc, t_attE, t_svc]
            key = (cc, pg - 2) if pg >= 2 else (cc - 1, pg + 2)
            if key in ev_y:
                sw.append(ev_y[key])
            stkey = (cc, pg - 1) if pg >= 1 else (cc - 1, 3)
            if stkey in ev_st:
                sw.append(ev_st[stkey])
            for pi in range(8):
                pair = pg * 8 + pi
                for hh in range(2):
                    h = pair * 2 + hh
                    tp = None if hh == 0 else (0, 64)
                    B.op("tensor", lambda e, h=h, pi=pi, hh=hh, cc=cc, tp=tp: e.matmul(
                        B.bank(5)[(hh * 64):(hh * 64 + 64), pi * 64:(pi + 1) * 64],
                        ksc[:, h * 64:(h + 1) * 64], vct(cc)[:, h * 64:(h + 1) * 64],
                        start=True, stop=True, tile_position=tp),
                        waits=sw if (pi == 0 and hh == 0) else [], inc=("amm", 1))
                    B.op("tensor", lambda e, h=h, pi=pi, hh=hh, cc=cc, tp=tp, ybk=ybk: e.matmul(
                        B.bank(ybk + pi // 4)[(hh * 64):(hh * 64 + 64),
                                              (pi % 4) * 128:(pi % 4 + 1) * 128],
                        vct(cc)[:, h * 64:(h + 1) * 64],
                        attE[:, h * 128:(h + 1) * 128],
                        start=True, stop=False, tile_position=tp),
                        inc=("amm", 1))
                    B.op("tensor", lambda e, h=h, pi=pi, hh=hh, tp=tp, ybk=ybk: e.matmul(
                        B.bank(ybk + pi // 4)[(hh * 64):(hh * 64 + 64),
                                              (pi % 4) * 128:(pi % 4 + 1) * 128],
                        svct[:, h * 64:(h + 1) * 64],
                        B.s["ident_b"][:, :],
                        start=False, stop=True, tile_position=tp),
                        inc=("amm", 1))
            ev_st[(cc, pg)] = B.op("scalar", lambda e, cc=cc, pg=pg: e.copy(
                states4[:, pg * 8:(pg + 1) * 8, :, cc + 1],
                B.bank(5)[:, 0:512].rearrange("p (a b) -> p a b", a=8)),
                waits=[amm(), t_szero])
            yw = [amm()]
            if cc > 0:
                yw.append(("ysd", 16 * cc))
            ev_y[(cc, pg)] = B.op("scalar", lambda e, pg=pg, ybk=ybk: e.copy(
                ych[:, pg * 1024:(pg + 1) * 1024], B.bank(ybk, 2)[:, 0:1024]),
                waits=yw)
        B.op("gpsimd", lambda e, cc=cc: e.dma_start(
            out=ys.ap()[cc, :, :], in_=ych[:, :]),
            waits=[ev_y[(cc, 3)]], inc=("ysd", 16), dma=True)
    t_states_done = B.tok("oS")

    scan_prev = [None]
    def emit_scan(pair, sl, extra_waits):
        B.op("gpsimd", lambda e, pair=pair: e.dma_start(
            out=dexp, in_=dex.tensor[0:0, 0:0] if False else
            B.ar(T_DEX, 64, BF16, 2 * pair, 2 * pair + 2)[:, 0:9]),
            waits=[("amm", B.val("amm"))], inc=("dxd", 16), dma=True)
        nd = B.val("dxd")
        for p in range(2):
            B.op("tensor", lambda e, p=p: e.matmul(
                B.bank(6 + p)[:, 0:288], pairsel_b[:, :],
                dexp.unsqueeze(1).broadcast_to([2, 32, 9]),
                start=True, stop=True),
                waits=[("dxd", nd), t_pselb, t_dex] +
                      ((extra_waits + ([scan_prev[0]] if scan_prev[0] else []))
                       if p == 0 else []),
                inc=("amm", 1))
        toks = []
        for p in range(2):
            toks.append(B.op("vector", lambda e, pair=pair, sl=sl, p=p: e.tensor_tensor_scan(
                sco(sl)[:, p * 288:(p + 1) * 288],
                B.bank(6 + p)[:, 0:288],
                states[:, pair * 576 + p * 288: pair * 576 + (p + 1) * 288],
                0.0, OP.mult, OP.add),
                waits=[("amm", B.val("amm"))]))
        scan_prev[0] = toks[-1]
        return toks[-1]

    for pair in range(32):
        t_sc = emit_scan(pair, pair % 2, [t_states_done] if pair == 0 else [])
        B.op("vector", lambda e, pair=pair: e.tensor_copy(
            sfin[:, pair * 64:(pair + 1) * 64],
            sco(pair % 2).rearrange("p (a b) -> p a b", b=9)[:, :, 8]),
            waits=[t_sc])
    t_sfin = B.tok("oV")
    B.op("gpsimd", lambda e: e.dma_start(out=B.d["sfin_d"][:, :], in_=sfin[:, 0:2048]),
         waits=[t_sfin], inc=("sfd", 16), dma=True)
    t_ag = B.op("gpsimd", lambda e: e.collective_compute(
        "AllGather", OP.bypass,
        replica_groups=[[0, 1], [2, 3], [4, 5], [6, 7]],
        ins=[B.d["sfin_d"].ap().opt()], outs=[B.d["sfin_ag"].ap().opt()]),
        waits=[("sfd", 16)], inc=("agp", 1), dma=True)
    B.op("gpsimd", lambda e: e.dma_start(out=sfin[:, 0:2048], in_=B.d["sfin_ag"][0:128, :]),
         waits=[t_ag], inc=("sgl", 16), dma=True)
    t_sin = B.op("vector", lambda e: e.tensor_scalar(
        sinitb[:, 0:2048], sfin[:, 0:2048], B.s["imask"][:, 0:1], None, OP.mult),
        waits=[("sgl", 16), ("imld", 16)])
    t_sl0 = B.op("vector", lambda e: e.tensor_copy(
        states4[:, :, :, 0],
        sinitb[:, 0:2048].rearrange("p (a b) -> p a b", b=64)),
        waits=[t_sin])

    asm_tok = {}
    prev_qtt = None
    for pair in range(32):
        t_scB = emit_scan(pair, pair % 2, [t_sl0] if pair == 0 else [])
        flat2 = B.ar(S_FLAT, 1024, BF16, 0, 2)
        B.op("gpsimd", lambda e, pair=pair, flat2=flat2: e.dma_start(
            out=flat2, in_=expcs[2 * pair:2 * pair + 2, 0:1024]),
            waits=[("amm", B.val("amm")), t_expcs], inc=("p7f", 16), dma=True)
        qx = qexp(pair % 2)
        nf = B.val("p7f")
        t_qtt = None
        for p in range(2):
            bw = [("p7f", nf)] if p == 0 else []
            if t_qtt is not None:
                bw.append(t_qtt)
            elif pair > 0:
                bw.append(prev_qtt)
            B.op("tensor", lambda e, p=p, flat2=flat2: e.matmul(
                B.bank(0 if p == 0 else 5)[:, 0:512], pairsel_b[:, :],
                flat2[:, p * 512:(p + 1) * 512], start=True, stop=True),
                waits=bw, inc=("amm", 1))
            for hh in range(2):
                t_qtt = B.op("vector", lambda e, p=p, hh=hh, qx=qx: e.tensor_tensor(
                    qx[hh * 64:hh * 64 + 64, p * 512:(p + 1) * 512],
                    (qc2 if hh == 0 else BC[64:128, 0:1024])[:, p * 512:(p + 1) * 512]
                    if hh == 0 else BC[64:128, p * 512:(p + 1) * 512],
                    B.bank(0 if p == 0 else 5)[hh * 64:hh * 64 + 64, 0:512], OP.mult),
                    waits=[amm()] if hh == 0 else [])
        prev_qtt = t_qtt
        t_qx = B.tok("oV")
        scosl = sco(pair % 2).rearrange("p (a b) -> p a b", b=9)
        for hh in range(2):
            for c in range(NC2):
                tp = None if hh == 0 else (64, 64)
                ybk = 1 + 2 * (pair % 2)
                B.op("tensor", lambda e, hh=hh, c=c, tp=tp, ybk=ybk, scosl=scosl, qx=qx,
                     pair=pair: e.matmul(
                    B.bank(ybk + c // 4)[(hh * 64):(hh * 64 + 64),
                                         (c % 4) * 128:(c % 4 + 1) * 128],
                    scosl[hh * 64:hh * 64 + 64, :, c],
                    qx[hh * 64:hh * 64 + 64, c * 128:(c + 1) * 128],
                    start=True, stop=True, tile_position=tp),
                    waits=([t_scB, t_qx] + ([asm_tok[pair - 2]] if pair >= 2 else []))
                          if (hh == 0 and c == 0) else [],
                    inc=("amm", 1))
        gw = [("ysd", 16 * NC2)]
        if pair >= 2:
            gw.append(asm_tok[pair - 2])
        B.op("gpsimd", lambda e, pair=pair: e.dma_start(
            out=yst(pair % 2)[:, :],
            in_=B.d["ys"].ap().transpose([1, 0, 2])[:, :, pair * 128:(pair + 1) * 128]),
            waits=gw, inc=("ygd%d" % (pair % 2), 16), dma=True)
        asm_tok[pair] = B.op("vector", lambda e, pair=pair: e.tensor_add(
            ysc[:, pair * 1024:(pair + 1) * 1024],
            yst(pair % 2)[:, :],
            B.bank(1 + 2 * (pair % 2), 2)[:, 0:1024]),
            waits=[("ygd%d" % (pair % 2), 16 * (pair // 2 + 1)), ("amm", B.val("amm"))])
    B.t_ysc = B.tok("oV")


E_GST = 0
E_GBF = 8192
E_TJ = 10240
E_ACC = 12288
E_WEX = 14336
Y_ZSB = 22528
Y_WOP = 24576
Y_WOPB = 26624
Y_SQ = 27648
Y_OFM = 65664
Y_X2 = 16384
Y_U2 = 87040
L_XN2 = 87040
L_SQD = 93440
F_H2 = 0
F_HFF = 32896
F_WGU = 76928
F_WD = 85120
F_FMS = 91136
F_TMP = 95232
F_OUTS = 86144


def phase_tail(B):
    B.din("nsc", [128, 32], F32)
    B.din("wop", [32, 128, 2048], F32)
    B.dout("out_tok", [TOK, DM], F32)

    nsc = B.sb("nsc_s", [128, 32], F32)
    rowstat = B.ar(T_TOTR, 2048, F32, 0, 1)
    B.op("sync", lambda e: e.dma_start(out=nsc[:, :], in_=B.d["nsc"][:, :]),
         inc=("tld", 16), dma=True)

    ysc = B.ysc
    amm = lambda: ("amm", B.val("amm"))
    uv = B.d["u_tok"].ap().rearrange("(n p) d -> n p d", p=128)
    modsel = B.s["modsel"]
    colbuf = B.s["colbuf"]
    gaB = B.s["gaB"]

    ysq = lambda sl: B.ar(Y_SQ + sl * 1024, 1024)
    sq_tok = {}
    for pair in range(32):
        sw = [B.t_ysc]
        if pair >= 2:
            sw.append(("sqm", pair - 1))
            sw.append(("sqm0", pair - 1))
        sq_tok[pair] = B.op("scalar", lambda e, pair=pair: e.activation(
            ysq(pair % 2)[:, :], ysc[:, pair * 1024:(pair + 1) * 1024], AF.Square,
            scale=float(1.0)), waits=sw)
        for n in range(2):
            B.op("tensor", lambda e, pair=pair, n=n: e.matmul(
                B.bank(4 + n)[0:1, 0:512], B.s["onescol"][:, 0:1],
                ysq(pair % 2)[:, n * 512:(n + 1) * 512],
                start=(pair == 0), stop=(pair == 31)),
                waits=[sq_tok[pair]] if n == 0 else [],
                inc=("sqm", 1) if n == 1 else ("sqm0", 1))
    t_rms1 = B.op("scalar", lambda e: e.activation(
        rowstat[0:1, 0:512], B.bank(4)[0:1, 0:512], AF.Ln,
        scale=float(1.0 / DI), bias=B.s["cvals"][0:1, 2:3]),
        waits=[("sqm", 32), ("sqm0", 32)])
    B.op("scalar", lambda e: e.activation(
        rowstat[0:1, 512:1024], B.bank(5)[0:1, 0:512], AF.Ln,
        scale=float(1.0 / DI), bias=B.s["cvals"][0:1, 2:3]))
    t_rms2 = B.op("scalar", lambda e: e.activation(
        rowstat[0:1, 0:1024], rowstat[0:1, 0:1024], AF.Exp, scale=-0.5))
    if B.debug:
        B.dout("dbg_rstd", [1, 1024], F32)
        B.op("gpsimd", lambda e: e.dma_start(out=B.d["dbg_rstd"][:, :],
                                             in_=rowstat[0:1, 0:1024]),
             waits=[t_rms2], inc=("dbg", 16), dma=True)
    for n in range(2):
        B.op("tensor", lambda e, n=n: e.matmul(
            B.bank(6 + n)[:, 0:512], B.s["ones1f"][:, 0:128],
            rowstat[0:1, n * 512:(n + 1) * 512], start=True, stop=True),
            waits=[t_rms2, ("sqm", 32)], inc=("rstb", 1))
    zsb = lambda sl: B.ar(Y_ZSB + sl * 1024, 1024)
    ysc_tok = {}
    for pair in range(32):
        zw = []
        if pair == 0:
            zw += [B.t_ysc, ("zdma0", 256), ("zdma1", 256)]
        if pair >= 2:
            zw.append(ysc_tok[pair - 2])
        B.op("gpsimd", lambda e, pair=pair: e.dma_start(
            out=zsb(pair % 2)[:, :], in_=B.d["zs"].ap()[pair, :, :]),
            waits=zw, inc=("zrl%d" % (pair % 2), 16), dma=True)
        t_prez = B.op("vector", lambda e, pair=pair: e.scalar_tensor_tensor(
            ysc[:, pair * 1024:(pair + 1) * 1024],
            ysc[:, pair * 1024:(pair + 1) * 1024],
            nsc[:, pair:pair + 1],
            B.bank(6, 2)[:, 0:1024],
            OP.mult, OP.mult),
            waits=[("rstb", 2), ("tld", 16)])
        prez_w = []
        if B.debug and pair == 0:
            B.dout("dbg_zsb", [128, 1024], BF16)
            B.op("gpsimd", lambda e: e.dma_start(out=B.d["dbg_zsb"][:, :],
                                                 in_=zsb(0)[:, :]),
                 waits=[("zrl0", 16)], inc=("dbg", 16), dma=True)
        if B.debug and pair == 0:
            B.dout("dbg_prez", [128, 1024], BF16)
            B.op("gpsimd", lambda e: e.dma_start(out=B.d["dbg_prez"][:, :],
                                                 in_=ysc[:, 0:1024]),
                 waits=[t_prez], inc=("dbg", 16), dma=True)
            prez_w = [("dbg", B.val("dbg"))]
        ysc_tok[pair] = B.op("vector", lambda e, pair=pair: e.tensor_mul(
            ysc[:, pair * 1024:(pair + 1) * 1024],
            ysc[:, pair * 1024:(pair + 1) * 1024],
            zsb(pair % 2)[:, :]),
            waits=[("zrl%d" % (pair % 2), 16 * (pair // 2 + 1))] + prez_w)
    t_yscl = B.tok("oV")

    wopst = lambda sl: B.ar(Y_WOP + sl * 1024, 1024, F32)
    wopb = lambda sl: B.ar(Y_WOPB + sl * 512, 512)
    ofm = B.ar(Y_OFM, 16384)
    wv2 = B.d["wop"].ap()
    op_ev = {}
    cast_tok = {}
    amm_after_op = {}
    for mg in range(4):
        for kc in range(32):
            j = mg * 32 + kc
            sl = j % 2
            dw = []
            if j < 2:
                dw.append(B.t_ysc)
            if j >= 2:
                dw.append(cast_tok[j - 2])
            B.op("sync", lambda e, mg=mg, kc=kc, sl=sl: e.dma_start(
                out=wopst(sl)[:, 0:512],
                in_=wv2[kc, :, mg * 512:(mg + 1) * 512]),
                waits=dw, inc=("opw%d" % sl, 16), dma=True)
            cw2 = [("opw%d" % sl, 16 * (j // 2 + 1))]
            if j == 0:
                cw2.append(t_yscl)
            if j >= 2:
                cw2.append(("amm", amm_after_op.get(j - 2, 0)))
            cast_tok[j] = B.op("vector", lambda e, sl=sl: e.tensor_copy(
                wopb(sl)[:, 0:512], wopst(sl)[:, 0:512]),
                waits=cw2)
            for mi in range(4):
                for n in range(2):
                    mw = []
                    if mi == 0 and n == 0:
                        mw = [cast_tok[j], ("rstb", 2)]
                        if kc == 0:
                            mw.append(t_yscl)
                            if mg >= 1:
                                mw += [op_ev[(mg - 1, mi2, n2)] for mi2 in range(4)
                                       for n2 in range(2)]
                    B.op("tensor", lambda e, mi=mi, n=n, sl=sl, kc=kc: e.matmul(
                        B.bank(mi * 2 + n)[:, 0:512],
                        wopb(sl)[:, mi * 128:(mi + 1) * 128],
                        ysc[:, kc * 1024 + n * 512: kc * 1024 + (n + 1) * 512],
                        start=(kc == 0), stop=(kc == 31)),
                        waits=mw, inc=("amm", 1))
            amm_after_op[j] = B.val("amm")
        for mi in range(4):
            for n in range(2):
                ew2 = []
                if mi == 0 and n == 0:
                    ew2 = [amm()]
                op_ev[(mg, mi, n)] = B.op("scalar", lambda e, mg=mg, mi=mi, n=n: e.copy(
                    ofm[:, (mg * 4 + mi) * 1024 + n * 512:(mg * 4 + mi) * 1024 + (n + 1) * 512],
                    B.bank(mi * 2 + n)[:, 0:512]),
                    waits=ew2)
    if B.debug:
        B.dout("dbg_ofm", [128, 16384], BF16)
        B.op("gpsimd", lambda e: e.dma_start(out=B.d["dbg_ofm"][:, :], in_=ofm),
             waits=[B.tok("oS")], inc=("dbg", 16), dma=True)
        B.cnt["rtrHOLD"] = 0
    t_ofm = B.tok("oS")
    if B.debug:
        t_ofm_dbg = ("dbg", B.val("dbg"))
    else:
        t_ofm_dbg = None

    x2 = B.ar(Y_X2, 16384)
    u2 = lambda sl: B.ar(Y_U2 + sl * 4096, 4096, F32)
    restmp = lambda sl: B.ar(sl * 2048, 2048)
    rscr = B.ar(4096, 512)
    res_tok = {}
    rcp_tok = {}
    ntr = 0
    for tt in range(8):
        sl = tt % 2
        dw = [t_ofm] + ([t_ofm_dbg] if t_ofm_dbg else [])
        if tt >= 2:
            dw.append(res_tok[tt - 2])
        B.op("sync", lambda e, tt=tt, sl=sl: e.dma_start(
            out=u2(sl)[:, 0:2048], in_=uv[tt + 1, :, :]),
            waits=dw, inc=("url%d" % sl, 16), dma=True)
        for dkg in range(4):
            for q in range(4):
                dk = dkg * 4 + q
                tw = [t_ofm]
                if ntr >= 32:
                    tw.append(rcp_tok[tt - 2])
                B.op("tensor", lambda e, tt=tt, dk=dk, q=q, dkg=dkg: e.transpose(
                    B.bankb(dkg + 4 * (tt % 2))[:, q * 128:(q + 1) * 128],
                    ofm[:, dk * 1024 + tt * 128: dk * 1024 + (tt + 1) * 128],
                    B.s["ident_b"][:, :]),
                    waits=tw if (dkg == 0 and q == 0) else [], inc=("rtr", 1))
                ntr += 1
        for dkg in range(4):
            cw = []
            if dkg == 0:
                cw.append(("rtr", ntr))
                if tt >= 2:
                    cw.append(res_tok[tt - 2])
            B.op("scalar", lambda e, dkg=dkg, sl=sl: e.copy(
                restmp(sl)[:, dkg * 512:(dkg + 1) * 512],
                B.bankb(dkg + 4 * sl)[:, 0:512]), waits=cw)
        rcp_tok[tt] = B.tok("oS")
        for dkg in range(4):
            B.op("vector", lambda e, dkg=dkg, sl=sl: e.tensor_tensor(
                rscr[:, 0:512],
                restmp(sl)[:, dkg * 512:(dkg + 1) * 512],
                gaB[:, dkg * 512:(dkg + 1) * 512], OP.mult),
                waits=[rcp_tok[tt], ("url%d" % sl, 16 * (tt // 2 + 1))]
                      if dkg == 0 else [])
            rtk = B.op("vector", lambda e, tt=tt, dkg=dkg, sl=sl: e.scalar_tensor_tensor(
                x2[:, tt * 2048 + dkg * 512: tt * 2048 + (dkg + 1) * 512],
                u2(sl)[:, dkg * 512:(dkg + 1) * 512], 1.0,
                rscr[:, 0:512], OP.mult, OP.add))
        res_tok[tt] = rtk
    t_x2 = B.tok("oV")

    h2 = B.ar(F_H2, 16384)
    xn2 = lambda sl: B.ar(L_XN2 + sl * 2048, 2048)
    sqd = B.ar(L_SQD, 2048)
    stat = B.s["stat"]
    ln_tok = {}
    ev2_tok = {}
    ntr2 = 0
    for tt in range(8):
        s0 = 8 * (tt % 2)
        lw = [t_x2]
        t_sum = B.op("vector", lambda e, tt=tt, s0=s0: e.reduce_sum(
            stat[:, s0:s0 + 1], x2[:, tt * 2048:(tt + 1) * 2048], AXX), waits=lw)
        t_qq = B.op("scalar", lambda e, tt=tt, s0=s0: e.activation(
            sqd[:, :], x2[:, tt * 2048:(tt + 1) * 2048], AF.Square,
            scale=float(1.0 / np.sqrt(DM)), accum_out=stat[:, s0 + 1:s0 + 2]),
            waits=lw + ([("rtr", ntr)] if tt == 0 else []))
        B.op("vector", lambda e, s0=s0: e.tensor_mul(
            stat[:, s0 + 2:s0 + 3], stat[:, s0:s0 + 1], stat[:, s0:s0 + 1]))
        t_var = B.op("vector", lambda e, s0=s0: e.scalar_tensor_tensor(
            stat[:, s0 + 3:s0 + 4], stat[:, s0 + 2:s0 + 3], float(-1.0 / (DM * DM)),
            stat[:, s0 + 1:s0 + 2], OP.mult, OP.add), waits=[t_qq])
        t_m = B.op("vector", lambda e, s0=s0: e.tensor_scalar(
            stat[:, s0 + 4:s0 + 5], stat[:, s0:s0 + 1], float(1.0 / DM), None, OP.mult))
        B.op("scalar", lambda e, s0=s0: e.activation(
            stat[:, s0 + 3:s0 + 4], stat[:, s0 + 3:s0 + 4], AF.Ln,
            bias=B.s["cvals"][:, 0:1]), waits=[t_var])
        t_rstd = B.op("scalar", lambda e, s0=s0: e.activation(
            stat[:, s0 + 5:s0 + 6], stat[:, s0 + 3:s0 + 4], AF.Exp, scale=-0.5))
        xw = [t_rstd, t_m]
        if tt >= 2:
            xw.append(("ltr", ntr2))
        ln_tok[tt] = B.op("vector", lambda e, tt=tt, s0=s0: e.tensor_scalar(
            xn2((tt % 2))[:, :], x2[:, tt * 2048:(tt + 1) * 2048],
            stat[:, s0 + 4:s0 + 5], stat[:, s0 + 5:s0 + 6],
            OP.subtract, OP.mult), waits=xw)
        for dkg in range(4):
            for q in range(4):
                dk = dkg * 4 + q
                tw2 = [ln_tok[tt]]
                if (tt, dkg) not in ((0, 0), (0, 1)):
                    key = (tt, dkg - 2) if dkg >= 2 else (tt - 1, dkg + 2)
                    tw2.append(ev2_tok[key])
                B.op("tensor", lambda e, tt=tt, dk=dk, q=q, dkg=dkg: e.transpose(
                    B.bankb(4 + dkg % 2)[:, q * 128:(q + 1) * 128],
                    xn2((tt % 2))[:, dk * 128:(dk + 1) * 128],
                    B.s["ident_b"][:, :]),
                    waits=tw2 if q == 0 else [], inc=("ltr", 1))
                ntr2 += 1
            evw = [("ltr", ntr2)]
            for q in range(4):
                dk = dkg * 4 + q
                tok = B.op("scalar", lambda e, tt=tt, dk=dk, q=q, dkg=dkg: e.activation(
                    h2[:, dk * 1024 + tt * 128: dk * 1024 + (tt + 1) * 128],
                    B.bankb(4 + dkg % 2)[:, q * 128:(q + 1) * 128],
                    AF.Identity, bias=modsel[:, 48 + dk:49 + dk],
                    scale=colbuf[:, 16 + dk:17 + dk]),
                    waits=evw if q == 0 else [])
            ev2_tok[(tt, dkg)] = tok
    t_h2 = B.tok("oS")

    hff = B.ar(F_HFF, 44032)
    wgu = lambda sl: B.ar(F_WGU + sl * 4096, 4096)
    tmpg = B.ar(F_TMP, 1024)
    wfv = B.d["wex_full"].ap()
    gu_ev = {}
    for mc in range(NMI):
        pb = (mc % 2) * 4
        sl = mc % 2
        dw = [("agw", 1), t_h2]
        if mc >= 2:
            dw.append(gu_ev[mc - 2])
        B.op("gpsimd", lambda e, mc=mc, sl=sl: e.dma_start(
            out=wgu(sl)[:, 0:2048].rearrange("p (a b) -> p a b", a=16),
            in_=wfv[mc:688:43, :, :].transpose([1, 0, 2])),
            waits=dw, inc=("fgw%d" % sl, 16), dma=True)
        B.op("gpsimd", lambda e, mc=mc, sl=sl: e.dma_start(
            out=wgu(sl)[:, 2048:4096].rearrange("p (a b) -> p a b", a=16),
            in_=wfv[688 + mc:1376:43, :, :].transpose([1, 0, 2])),
            inc=("fgw%d" % sl, 16), dma=True)
        nw = B.val("fgw%d" % sl)
        for kt in range(16):
            for n in range(2):
                for gup in range(2):
                    mw = [("fgw%d" % sl, nw)] if (kt == 0 and n == 0 and gup == 0) else []
                    if kt == 0 and n == 0 and gup == 0 and mc >= 2:
                        mw.append(gu_ev[mc - 2])
                    B.op("tensor", lambda e, kt=kt, n=n, gup=gup, pb=pb, sl=sl: e.matmul(
                        B.bank((pb + gup * 2 + n) % 8)[:, 0:512],
                        wgu(sl)[:, gup * 2048 + kt * 128: gup * 2048 + (kt + 1) * 128],
                        h2[:, kt * 1024 + n * 512: kt * 1024 + (n + 1) * 512],
                        start=(kt == 0), stop=(kt == 15)),
                        waits=mw, inc=("amm", 1))
        for n in range(2):
            sw2 = []
            if n == 0:
                sw2 = [amm()]
                if mc >= 1:
                    sw2.append(gu_ev[mc - 1])
            B.op("scalar", lambda e, n=n, pb=pb: e.activation(
                tmpg[:, n * 512:(n + 1) * 512], B.bank((pb + n) % 8)[:, 0:512],
                AF.Silu), waits=sw2)
        t_sg = B.tok("oS")
        for n in range(2):
            tokf = B.op("vector", lambda e, mc=mc, n=n, pb=pb: e.tensor_mul(
                hff[:, mc * 1024 + n * 512: mc * 1024 + (n + 1) * 512],
                tmpg[:, n * 512:(n + 1) * 512],
                B.bank((pb + 2 + n) % 8)[:, 0:512]),
                waits=[t_sg, amm()] if n == 0 else [])
        gu_ev[mc] = tokf
    t_hff = B.tok("oV")

    wd = lambda sl: B.ar(F_WD + sl * 512, 512)
    fms = lambda sl: B.ar(F_FMS + sl * 2048, 2048)
    outs = lambda sl: B.ar(F_OUTS + sl * 1024, 1024, F32)
    ftmp = lambda sl: B.ar(96256 + sl * 512, 512)
    gbB = gaB[:, 2048:4096]
    dn_cast = {}
    amm_after_dn = {}
    fin_tok = {}
    nfr = 0
    for mg in range(4):
        for kc in range(NMI):
            j = mg * NMI + kc
            sl = j % 2
            dw = [t_hff] if j == 0 else []
            if j >= 2:
                dw.append(("amm", amm_after_dn.get(j - 2, 0)))
            B.op("gpsimd", lambda e, mg=mg, kc=kc, sl=sl: e.dma_start(
                out=wd(sl)[:, 0:512].rearrange("p (a b) -> p a b", a=4),
                in_=wfv[1376 + kc * 16 + mg * 4:1376 + kc * 16 + mg * 4 + 4, :, :]
                    .transpose([1, 0, 2])),
                waits=dw, inc=("fdw%d" % sl, 16), dma=True)
            dn_cast[j] = ("fdw%d" % sl, 16 * (j // 2 + 1))
            for mi in range(4):
                for n in range(2):
                    mw = []
                    if mi == 0 and n == 0:
                        mw = [dn_cast[j]]
                        if kc == 0:
                            mw.append(t_hff)
                            if mg >= 1:
                                mw.append(fin_tok[mg - 1])
                    B.op("tensor", lambda e, mi=mi, n=n, sl=sl, kc=kc: e.matmul(
                        B.bank(mi * 2 + n)[:, 0:512],
                        wd(sl)[:, mi * 128:(mi + 1) * 128],
                        hff[:, kc * 1024 + n * 512: kc * 1024 + (n + 1) * 512],
                        start=(kc == 0), stop=(kc == NMI - 1)),
                        waits=mw, inc=("amm", 1))
            amm_after_dn[j] = B.val("amm")
        for mi in range(4):
            for n in range(2):
                B.op("scalar", lambda e, mg=mg, mi=mi, n=n: e.copy(
                    fms(n)[:, mi * 512:(mi + 1) * 512],
                    B.bank(mi * 2 + n)[:, 0:512]),
                    waits=[amm()] if (mi == 0 and n == 0) else [])
        t_fms = B.tok("oS")
        for n in range(2):
            for q in range(4):
                tt = n * 4 + q
                bkp = 4 + (q % 2) * 2
                fsl = q % 2
                for mi in range(4):
                    tw = [t_fms]
                    key = ("bk", bkp)
                    if key in fin_tok:
                        tw.append(fin_tok[key])
                    B.op("tensor", lambda e, n=n, q=q, mi=mi, bkp=bkp: e.transpose(
                        B.bankb(bkp)[:, mi * 128:(mi + 1) * 128],
                        fms(n)[:, mi * 512 + q * 128: mi * 512 + (q + 1) * 128],
                        B.s["ident_b"][:, :]),
                        waits=tw if mi == 0 else [], inc=("ftr", 1))
                    nfr += 1
                cw = [("ftr", nfr)]
                if ("f", fsl) in fin_tok:
                    cw.append(fin_tok[("f", fsl)])
                t_cp = B.op("scalar", lambda e, bkp=bkp, fsl=fsl: e.copy(
                    ftmp(fsl)[:, 0:512], B.bankb(bkp)[:, 0:512]), waits=cw)
                fin_tok[("bk", bkp)] = t_cp
                t_mul = B.op("vector", lambda e, mg=mg, fsl=fsl: e.tensor_tensor(
                    B.ar(65664, 512)[:, 0:512],
                    ftmp(fsl)[:, 0:512],
                    gbB[:, mg * 512:(mg + 1) * 512], OP.mult),
                    waits=[t_cp])
                fin_tok[("f", fsl)] = t_mul
                osl = tt % 2
                ost_w = []
                if B.val("outd%d" % osl) >= 16:
                    ost_w.append(("outd%d" % osl, B.val("outd%d" % osl)))
                t_o = B.op("vector", lambda e, mg=mg, tt=tt: e.scalar_tensor_tensor(
                    outs(tt % 2)[:, 0:512],
                    x2[:, tt * 2048 + mg * 512: tt * 2048 + (mg + 1) * 512], 1.0,
                    B.ar(65664, 512)[:, 0:512], OP.mult, OP.add),
                    waits=ost_w)
                ow = [t_o]
                B.op("gpsimd", lambda e, mg=mg, tt=tt: e.dma_start(
                    out=B.d["out_tok"].ap()[tt * 128:(tt + 1) * 128,
                                            mg * 512:(mg + 1) * 512],
                    in_=outs(tt % 2)[:, 0:512]),
                    waits=ow, inc=("outd%d" % (tt % 2), 16), dma=True)
        fin_tok[mg] = B.tok("oV")
    B.t_out = ("outd0", B.val("outd0"))
    B.t_out2 = ("outd1", B.val("outd1"))


def prep_inputs(inp):
    u = np.asarray(inp["L0_attn_u"], np.float32)
    c = np.asarray(inp["L0_c"], np.float32)
    wmod = np.asarray(inp["L0_wmod"], np.float32)
    ipw = np.asarray(inp["L0_attn_in_proj_w"], np.float32)
    wip = np.zeros((22, 2048, 384), np.float32)
    for g in range(22):
        c0 = g * 384
        c1 = min(c0 + 384, DIP)
        if c1 > c0:
            wip[g, :, 0:c1 - c0] = ipw[:, c0:c1]
    cw = np.asarray(inp["L0_attn_conv_weight"], np.float32)
    cb = np.asarray(inp["L0_attn_conv_bias"], np.float32)
    wcv = np.zeros((128, 128), np.float32)
    for xc in range(32):
        for k in range(4):
            wcv[:, 4 * xc + k] = cw[k, 128 * xc:128 * (xc + 1)]
    wcb = np.ascontiguousarray(cb.reshape(32, 128).T)
    hvec = np.zeros((64, 4), np.float32)
    hvec[:, 0] = np.asarray(inp["L0_attn_A_log"], np.float32)
    hvec[:, 1] = np.asarray(inp["L0_attn_dt_bias"], np.float32)
    hvec[:, 2] = np.asarray(inp["L0_attn_D_param"], np.float32)
    nscale = np.asarray(inp["L0_attn_norm_scale"], np.float32)
    nsc = np.ascontiguousarray(nscale.reshape(32, 128).T)
    wop = np.ascontiguousarray(np.asarray(inp["L0_attn_out_proj_w"], np.float32)
                               .reshape(32, 128, 2048))
    lutb = np.ascontiguousarray(
        np.broadcast_to(np.asarray(inp["L0_ffn_lut"], np.float32)[None, :], (128, 16)))
    gidx = np.asarray(inp["L0_ffn_gate_idx"], np.int32)
    uidx = np.asarray(inp["L0_ffn_up_idx"], np.int32)
    didx = np.asarray(inp["L0_ffn_down_idx"], np.int32)
    tiles = []
    for t in range(N_TILES):
        if t < 688:
            kc, mc = t // 43, t % 43
            blk = gidx[mc * 128:(mc + 1) * 128, kc * 128:(kc + 1) * 128]
        elif t < 1376:
            tt_ = t - 688
            kc, mc = tt_ // 43, tt_ % 43
            blk = uidx[mc * 128:(mc + 1) * 128, kc * 128:(kc + 1) * 128]
        else:
            tt_ = t - 1376
            kc, mc = tt_ // 16, tt_ % 16
            blk = didx[mc * 128:(mc + 1) * 128, kc * 128:(kc + 1) * 128]
        tiles.append(blk)
    tiles = np.ascontiguousarray(np.stack(tiles))
    maps = []
    for core in range(8):
        b, half = core // 2, core % 2
        t0 = half * 1024
        ut = np.zeros((PTOK, DM), np.float32)
        if t0 >= 3:
            ut[CTX0:REAL0] = u[b, t0 - 3:t0]
        ut[REAL0:] = u[b, t0:t0 + 1024]
        ctb = np.ascontiguousarray(
            c.T[256 * core:256 * (core + 1)].reshape(2, 128, 4).transpose(1, 0, 2).reshape(128, 8))
        bselB = np.zeros((128, 4), np.float32)
        bselB[:, b] = 1.0
        maps.append({
            "ctb": ctb,
            "wmod_sh": np.ascontiguousarray(wmod[256 * core:256 * (core + 1)]),
            "bselB": bselB,
            "u_tok": ut,
            "wip": wip,
            "hvec": hvec,
            "initmask": np.full((128, 1), float(half), np.float32),
            "wcv": wcv,
            "wcb": wcb,
            "nsc": nsc,
            "wop": wop,
            "lutb": lutb,
            "gsrc": np.ascontiguousarray(
                tiles[TPC * core:TPC * (core + 1)].transpose(0, 2, 1)
            ).astype(ml_dtypes.bfloat16),
        })
    return maps


def build(debug=False, phases=9, modstop=99):
    B = Builder(debug)
    B.modstop = modstop
    B.din("u_tok", [PTOK, DM], F32)
    B.BIG = B.sb("BIG", [128, ACOLS], BF16)
    B.PS = B.nc.alloc_psum_tensor("PS", [128, 4096], F32)
    phase_const(B)
    if phases >= 1:
        phase_mod(B)
    if phases >= 2:
        phase_ln1(B)
    if phases >= 3:
        phase_inproj(B)
    if phases >= 4:
        phase_ssd(B)
    if phases >= 5:
        phase_tail(B)
    if phases < 5:
        for s in list(B.cnt):
            tok = (s, B.val(s))
            if tok[1] > 0 and s not in ("dbg",):
                B.raw("gpsimd", lambda e, tok=tok: e.wait_ge(B.sem(tok[0]), tok[1]))
        B.dout_dummy = B.dout("out_tok", [TOK, DM], F32) if "out_tok" not in B.d else None
        return B
    if debug:
        B.dout("dbg_ysc", [128, 32768], BF16)
        B.op("gpsimd", lambda e: e.dma_start(out=B.d["dbg_ysc"][:, :], in_=B.ysc),
             waits=[B.t_ysc], inc=("dbg", 16), dma=True)
        B.dout("dbg_x2", [128, 16384], BF16)
        B.op("gpsimd", lambda e: e.dma_start(out=B.d["dbg_x2"][:, :],
                                             in_=B.ar(Y_X2, 16384)),
             waits=[B.tok("oV")], inc=("dbg", 16), dma=True)
        B.dout("dbg_ysl", [128, 32768], BF16)
        B.op("gpsimd", lambda e: e.dma_start(out=B.d["dbg_ysl"][:, :], in_=B.ysc),
             waits=[B.tok("oV")], inc=("dbg", 16), dma=True)
        tok = B.t_out
        tok2 = B.t_out2
        B.raw("gpsimd", lambda e, tok=tok: e.wait_ge(B.sem(tok[0]), tok[1]))
        B.raw("gpsimd", lambda e, tok2=tok2: e.wait_ge(B.sem(tok2[0]), tok2[1]))
        B.raw("gpsimd", lambda e: e.wait_ge(B.sem("dbg"), B.val("dbg")))
    else:
        tok = B.t_out
        tok2 = B.t_out2
        B.raw("gpsimd", lambda e, tok=tok: e.wait_ge(B.sem(tok[0]), tok[1]))
        B.raw("gpsimd", lambda e, tok2=tok2: e.wait_ge(B.sem(tok2[0]), tok2[1]))
    return B


def kernel(**inputs):
    maps = prep_inputs(inputs)
    B = build(False)
    nc = B.finish()
    res = run_bass_kernel_spmd(nc, maps, core_ids=list(range(8)))
    out = np.zeros((4, 2048, 2048), np.float32)
    for core in range(8):
        b, half = core // 2, core % 2
        out[b, half * 1024:(half + 1) * 1024, :] = res.results[core]["out_tok"]
    return out
